# version 06
# baseline (speedup 1.0000x reference)
import contextlib
import os

import numpy as np
import ml_dtypes

import concourse.bass as bass
import concourse.mybir as mybir
from concourse.bass_utils import run_bass_kernel_spmd

BFD = ml_dtypes.bfloat16
F32 = mybir.dt.float32
BF16 = mybir.dt.bfloat16
F8 = mybir.dt.float8e4
AF = mybir.ActivationFunctionType
ALU = mybir.AluOpType
AX = mybir.AxisListType

B, T, D = 4, 2048, 2048
H, HD = 16, 128
HID = 8192
TOK = 1024
NT = T // 128
NTQ = TOK // 128
ND = D // 128
NH = HID // 128
NHB = 16
EPS32 = float(np.finfo(np.float32).eps)
C3 = float(np.float32(BFD(0.044715)))
CS = float(np.float32(BFD(np.sqrt(2.0 / np.pi))))
S4 = float(128.0 ** -0.25)
SQRT128 = float(np.sqrt(128.0))

DEBUG = False
PH = int(os.environ.get("KPH", "99"))


_INC_INFO = {}


class Ctr:
    def __init__(self, sem):
        self.sem = sem
        self.n = 0

    def inc(self, instr, k=1):
        instr.then_inc(self.sem, k)
        self.n += k
        _INC_INFO[id(instr)] = (self.sem, self.n, instr)
        return self.n

    def dinc(self, instr):
        return self.inc(instr, 16)


_EP_OPS = frozenset((
    "tensor_scalar", "tensor_tensor", "tensor_add", "tensor_sub", "tensor_mul",
    "tensor_max", "tensor_copy", "tensor_reduce", "reduce_sum", "reduce_max",
    "reciprocal", "scalar_tensor_tensor", "memset", "activation", "square",
    "sqrt", "mul", "add", "copy", "sign",
))


class EP:

    def __init__(self, eng, guard):
        self._e = eng
        self._g = guard
        self._pend = {}
        self._mark = {}

    def wait_ge(self, sem, n):
        self._mark[id(sem)] = max(self._mark.get(id(sem), -1), n)
        return self._e.wait_ge(sem, n)

    def _require(self, instr):
        info = _INC_INFO.get(id(instr))
        if info is None or info[2] is not instr:
            self._g.inc(instr)
            info = _INC_INFO[id(instr)]
        sem, n, _ = info
        if self._mark.get(id(sem), -1) < n:
            self._mark[id(sem)] = n
            self._e.wait_ge(sem, n)

    def __getattr__(self, name):
        f = getattr(self._e, name)
        if name not in _EP_OPS:
            return f

        def g(*args, **kw):
            aps = [a for a in args if isinstance(a, bass.AP)]
            aps += [a for a in kw.values() if isinstance(a, bass.AP)]
            if not aps:
                return f(*args, **kw)
            outn = aps[0].tensor.name
            for a in aps[1:]:
                w = self._pend.get(a.tensor.name)
                if w is not None:
                    self._require(w)
            w = self._pend.get(outn)
            if w is not None:
                self._require(w)
            instr = f(*args, **kw)
            self._pend[outn] = instr
            ac = kw.get("accum_out")
            if isinstance(ac, bass.AP):
                self._pend[ac.tensor.name] = instr
            return instr

        return g


def build_nc():
    nc = bass.Bass("TRN2", target_bir_lowering=False)

    x_h = nc.dram_tensor("x", [T, D], F32, kind="ExternalInput")
    gq_h = nc.dram_tensor("g_q", [D], F32, kind="ExternalInput")
    wq_h = nc.dram_tensor("Wq", [D, D], F32, kind="ExternalInput")
    bq_h = nc.dram_tensor("bq", [D], F32, kind="ExternalInput")
    gkv_h = nc.dram_tensor("g_kv", [D], F32, kind="ExternalInput")
    wkv_h = nc.dram_tensor("Wkv", [D, 2 * D], F32, kind="ExternalInput")
    bkv_h = nc.dram_tensor("bkv", [2 * D], F32, kind="ExternalInput")
    gh_h = nc.dram_tensor("g_head", [HD], F32, kind="ExternalInput")
    wo_h = nc.dram_tensor("Wo", [D, D], F32, kind="ExternalInput")
    bo_h = nc.dram_tensor("bo", [D], F32, kind="ExternalInput")
    wg_h = nc.dram_tensor("w_gating", [2, D, HID], F32, kind="ExternalInput")
    wl_h = nc.dram_tensor("w_linear", [HID, D], F32, kind="ExternalInput")
    y_h = nc.dram_tensor("y", [TOK, D], F32, kind="ExternalOutput")
    v_hbm = nc.dram_tensor("v_scr", [H, T, HD], BF16, kind="Internal")
    kn_hbm = nc.dram_tensor("kn_scr", [H, HD, T], BF16, kind="Internal")
    qn_hbm = nc.dram_tensor("qn_scr", [H, HD, TOK], BF16, kind="Internal")
    ffn_hbm = nc.dram_tensor("ffn_scr", [TOK, D], BF16, kind="Internal")
    if DEBUG:
        dbg_xnqT = nc.dram_tensor("dbg_xnqT", [128, ND * TOK], mybir.dt.uint8, kind="ExternalOutput")
        dbg_stats = nc.dram_tensor("dbg_stats", [128, 6 * NT], F32, kind="ExternalOutput")
        dbg_ffn = nc.dram_tensor("dbg_ffn", [TOK, D], BF16, kind="ExternalOutput")
        dbg_qnT = nc.dram_tensor("dbg_qnT", [128, H * TOK], BF16, kind="ExternalOutput")
        dbg_knT = nc.dram_tensor("dbg_knT", [128, H * T], BF16, kind="ExternalOutput")
        dbg_wvT = nc.dram_tensor("dbg_wvT", [128, H * TOK], BF16, kind="ExternalOutput")

    es = contextlib.ExitStack()
    with es:
        def sb(name, shape, dt, st=None):
            return (st or es).enter_context(nc.sbuf_tensor(name, shape, dt))

        def sem(name):
            return Ctr(es.enter_context(nc.semaphore(name)))

        c_lx = [sem("lx0"), sem("lx1")]
        c_lw = [sem("lw0"), sem("lw1"), sem("lw2"), sem("lw3")]
        c_ldx = sem("ldx")
        c_ldw = sem("ldw")
        c_lds = sem("lds")
        c_ldg = sem("ldg")
        c_lg = [sem("lg0"), sem("lg1")]
        c_stg = sem("stg")
        c_xfree = sem("xfree")
        c_tin = sem("tin")
        c_tpd = sem("tpd")
        c_tev = sem("tev")
        c_wcv = sem("wcv")
        c_mmd = sem("mmd")
        c_zw = sem("zw")
        c_pev = sem("pev")
        c_sg = sem("sg")
        c_vg = sem("vg")
        c_zr = sem("zr")
        c_bc = sem("bc")
        c_wvn = sem("wvn")
        c_vs = sem("vs")
        c_ss = sem("ss")
        c_ps = sem("ps")

        def vgd(eng, ins):
            c_vs.inc(ins)
            eng.wait_ge(c_vs.sem, c_vs.n)
            return ins

        ident = sb("ident", [128, 128], BF16)
        ones1f = sb("ones1f", [1, 128], F32)
        onescol_b = sb("onescol_b", [128, 1], BF16)
        ones1b = sb("ones1b", [1, 128], BF16)
        gq_c = sb("gq_c", [128, ND], F32)
        gkv_c = sb("gkv_c", [128, ND], F32)
        gh_c = sb("gh_c", [128, 1], F32)
        ghs_c = sb("ghs_c", [128, 1], F32)
        gh_r = sb("gh_r", [1, 128], F32)
        bq_r = sb("bq_r", [1, D], F32)
        bkv_r = sb("bkv_r", [1, 2 * D], F32)
        bo_r = sb("bo_r", [1, D], F32)
        negM_c = sb("negM_c", [128, 1], F32)
        scal1 = sb("scal1", [1, 128], F32)
        sum_c = sb("sum_c", [128, NT], F32)
        ssq_c = sb("ssq_c", [128, NT], F32)
        m_c = sb("m_c", [128, NT], F32)
        var_c = sb("var_c", [128, NT], F32)
        rstd_c = sb("rstd_c", [128, NT], F32)
        nmrs_c = sb("nmrs_c", [128, NT], F32)
        A_c = sb("A_c", [128, NT], F32)
        nmA_c = sb("nmA_c", [128, NT], F32)
        st1 = sb("st1", [128, NT], F32)
        st2 = sb("st2", [128, NT], F32)
        st3 = sb("st3", [128, NT], F32)
        wstg = sb("wstg", [128, 4 * 512], F32)

        with nc.Block() as blk:
            @blk.sync
            def _(sync):
                for dst, src in (
                    (gq_c[:, :], gq_h.ap().rearrange("(c p) -> p c", p=128)),
                    (gkv_c[:, :], gkv_h.ap().rearrange("(c p) -> p c", p=128)),
                    (gh_c[:, :], gh_h.ap().rearrange("(p one) -> p one", one=1)),
                    (gh_r[:, :], gh_h.ap().rearrange("(one p) -> one p", one=1)),
                    (bq_r[:, :], bq_h.ap().rearrange("(one d) -> one d", one=1)),
                    (bkv_r[:, :], bkv_h.ap().rearrange("(one d) -> one d", one=1)),
                    (bo_r[:, :], bo_h.ap().rearrange("(one d) -> one d", one=1)),
                ):
                    c_lds.dinc(sync.dma_start(dst, src, allow_slow_non_contiguous=True))

            @blk.gpsimd
            def _(g):
                g.memset(ident[:, :], 0.0).then_inc(c_ps.sem, 1)
                c_ps.n += 1
                g.memset(ones1f[:, :], 1.0)
                g.memset(onescol_b[:, :], 1.0)
                g.memset(ones1b[:, :], 1.0)
                g.wait_ge(c_ps.sem, c_ps.n)
                c_ps.inc(g.affine_select(out=ident[:, :], in_=ident[:, :],
                                         compare_op=ALU.not_equal, fill=1.0, base=0,
                                         pattern=[[-1, 128]], channel_multiplier=1))

            @blk.vector
            def _(v):
                v = EP(v, c_vs)
                v.wait_ge(c_lds.sem, c_lds.n)
                v.tensor_scalar(ghs_c[:, :], gh_c[:, :], S4, None, ALU.mult)
                vgd(v, v.tensor_mul(scal1[:1, :], gh_r[:1, :], gh_r[:1, :]))
                vgd(v, v.reduce_max(scal1[:1, :1], scal1[:1, :], AX.X))
                c_vg.inc(v.tensor_scalar(scal1[:1, :1], scal1[:1, :1], -SQRT128, None, ALU.mult))

        with nc.Block() as blk:
            with nc.psum_tensor("p0b", [128, 512], F32) as p0b:
                @blk.tensor
                def _(t):
                    t.wait_ge(c_vg.sem, c_vg.n)
                    c_mmd.inc(t.matmul(p0b[:, 0:1], ones1f[:1, :], scal1[:1, :1],
                                       start=True, stop=True))

                @blk.vector
                def _(v):
                    v = EP(v, c_vs)
                    v.wait_ge(c_mmd.sem, c_mmd.n)
                    c_vg.inc(v.tensor_scalar(negM_c[:, :], p0b[:, 0:1], 1.0, None, ALU.mult))

        with contextlib.ExitStack() as st_p1:
            if PH < 1:
                return nc
            xbuf = sb("xbuf1", [128, 2 * D], F32, st_p1)
            sqscr = sb("sqscr1", [128, 2 * D], F32, st_p1)
            with nc.Block() as blk:
                lx0 = [c.n for c in c_lx]
                xf0, sg0, vg0 = c_xfree.n, c_sg.n, c_vg.n

                @blk.sync
                def _(sync):
                    for i in range(NT):
                        if i >= 2:
                            sync.wait_ge(c_xfree.sem, xf0 + (i - 1))
                        c_lx[i % 2].dinc(sync.dma_start(xbuf[:, (i % 2) * D:(i % 2 + 1) * D],
                                                        x_h[i * 128:(i + 1) * 128, :]))

                @blk.scalar
                def _(s):
                    s = EP(s, c_ss)
                    for i in range(NT):
                        s.wait_ge(c_lx[i % 2].sem, lx0[i % 2] + 16 * (i // 2 + 1))
                        if i >= 2:
                            s.wait_ge(c_xfree.sem, xf0 + (i - 1))
                        c_sg.inc(s.square(sqscr[:, (i % 2) * D:(i % 2 + 1) * D],
                                          xbuf[:, (i % 2) * D:(i % 2 + 1) * D]))

                @blk.vector
                def _(v):
                    v = EP(v, c_vs)
                    for i in range(NT):
                        v.wait_ge(c_sg.sem, sg0 + (i + 1))
                        v.reduce_sum(sum_c[:, i:i + 1], xbuf[:, (i % 2) * D:(i % 2 + 1) * D], AX.X)
                        c_xfree.inc(v.reduce_sum(ssq_c[:, i:i + 1],
                                                 sqscr[:, (i % 2) * D:(i % 2 + 1) * D], AX.X))
                    vgd(v, v.tensor_scalar(m_c[:, :], sum_c[:, :], 1.0 / D, None, ALU.mult))
                    vgd(v, v.tensor_scalar(ssq_c[:, :], ssq_c[:, :], 1.0 / D, None, ALU.mult))
                    vgd(v, v.tensor_mul(st1[:, :], m_c[:, :], m_c[:, :]))
                    vgd(v, v.tensor_sub(var_c[:, :], ssq_c[:, :], st1[:, :]))
                    c_vg.inc(v.tensor_scalar(st1[:, :], var_c[:, :], 1e-5, None, ALU.add))

                @blk.scalar
                def _(s):
                    s = EP(s, c_ss)
                    s.wait_ge(c_vg.sem, vg0 + 1)
                    c_sg.inc(s.sqrt(st2[:, :], st1[:, :]))

                @blk.vector
                def _(v):
                    v = EP(v, c_vs)
                    v.wait_ge(c_sg.sem, c_sg.n)
                    vgd(v, v.reciprocal(st3[:, :], st2[:, :]))
                    vgd(v, v.tensor_mul(st2[:, :], st3[:, :], st3[:, :]))
                    vgd(v, v.tensor_mul(st2[:, :], st2[:, :], st1[:, :]))
                    vgd(v, v.tensor_scalar(st2[:, :], st2[:, :], -0.5, 1.5, ALU.mult, ALU.add))
                    vgd(v, v.tensor_mul(rstd_c[:, :], st3[:, :], st2[:, :]))
                    vgd(v, v.tensor_mul(st1[:, :], rstd_c[:, :], rstd_c[:, :]))
                    vgd(v, v.tensor_mul(st1[:, :], st1[:, :], var_c[:, :]))
                    c_vg.inc(v.tensor_scalar(st1[:, :], st1[:, :], EPS32, None, ALU.add))

                @blk.scalar
                def _(s):
                    s = EP(s, c_ss)
                    s.wait_ge(c_vg.sem, c_vg.n)
                    c_sg.inc(s.sqrt(st2[:, :], st1[:, :]))

                @blk.vector
                def _(v):
                    v = EP(v, c_vs)
                    v.wait_ge(c_sg.sem, c_sg.n)
                    vgd(v, v.reciprocal(st3[:, :], st2[:, :]))
                    vgd(v, v.tensor_mul(st2[:, :], st3[:, :], st3[:, :]))
                    vgd(v, v.tensor_mul(st2[:, :], st2[:, :], st1[:, :]))
                    vgd(v, v.tensor_scalar(st2[:, :], st2[:, :], -0.5, 1.5, ALU.mult, ALU.add))
                    vgd(v, v.tensor_mul(st3[:, :], st3[:, :], st2[:, :]))
                    vgd(v, v.tensor_mul(A_c[:, :], rstd_c[:, :], st3[:, :]))
                    vgd(v, v.tensor_scalar(st1[:, :], m_c[:, :], -1.0, None, ALU.mult))
                    vgd(v, v.tensor_mul(nmA_c[:, :], st1[:, :], A_c[:, :]))
                    c_vg.inc(v.tensor_mul(nmrs_c[:, :], st1[:, :], rstd_c[:, :]))

        if DEBUG:
            with nc.Block() as blk:
                @blk.gpsimd
                def _(g):
                    for j, tt in enumerate((sum_c, ssq_c, rstd_c, nmrs_c, A_c, nmA_c)):
                        c_stg.dinc(g.dma_start(dbg_stats[:, j * NT:(j + 1) * NT], tt[:, :]))
                    g.wait_ge(c_stg.sem, c_stg.n)

        with contextlib.ExitStack() as st_ffn:
            act8 = sb("act8", [128, NH * TOK], F8, st_ffn)

            with contextlib.ExitStack() as st_xnq:
                xnqT = sb("xnqT", [128, ND * TOK], F8, st_xnq)

                with contextlib.ExitStack() as st_p2:
                    if PH < 2:
                        return nc
                    xbuf = sb("xbuf2", [128, 2 * D], F32, st_p2)
                    xnf = sb("xnf", [128, D], F32, st_p2)
                    xq8 = sb("xq8", [128, D], F8, st_p2)
                    xqb = sb("xqb", [128, D], BF16, st_p2)
                    with nc.Block() as blk:
                        lx0 = [c.n for c in c_lx]
                        xf0, tin0, tpd0, tev0 = c_xfree.n, c_tin.n, c_tpd.n, c_tev.n
                        with nc.psum_tensor("ptA2", [128, 2048], BF16) as ptA, \
                             nc.psum_tensor("ptB2", [128, 2048], BF16) as ptB:
                            pps = [ptA, ptB]

                            @blk.sync
                            def _(sync):
                                for i in range(NTQ):
                                    if i >= 2:
                                        sync.wait_ge(c_xfree.sem, xf0 + (i - 1))
                                    c_lx[i % 2].dinc(sync.dma_start(xbuf[:, (i % 2) * D:(i % 2 + 1) * D],
                                                                    x_h[i * 128:(i + 1) * 128, :]))

                            @blk.vector
                            def _(v):
                                v = EP(v, c_vs)
                                for i in range(NTQ):
                                    v.wait_ge(c_lx[i % 2].sem, lx0[i % 2] + 16 * (i // 2 + 1))
                                    sl = xbuf[:, (i % 2) * D:(i % 2 + 1) * D]
                                    c_xfree.inc(v.tensor_scalar(xnf[:, :], sl, rstd_c[:, i:i + 1],
                                                                nmrs_c[:, i:i + 1], ALU.mult, ALU.add))
                                    v.wait_ge(c_xfree.sem, c_xfree.n)
                                    vgd(v, v.tensor_scalar(xq8[:, :], xnf[:, :], 1.0, None, ALU.mult))
                                    c_tin.inc(v.tensor_scalar(xqb[:, :], xq8[:, :], 1.0, None, ALU.mult))
                                    v.wait_ge(c_tpd.sem, tpd0 + (i + 1))
                                    pp = pps[i % 2]
                                    ins = None
                                    for d in range(ND):
                                        ins = v.tensor_scalar(xnqT[:, d * TOK + i * 128: d * TOK + (i + 1) * 128],
                                                              pp[:, d * 128:(d + 1) * 128], 1.0, None, ALU.mult)
                                    c_tev.inc(ins)

                            @blk.tensor
                            def _(t):
                                t.wait_ge(c_ps.sem, c_ps.n)
                                for i in range(NTQ):
                                    t.wait_ge(c_tin.sem, tin0 + (i + 1))
                                    if i >= 2:
                                        t.wait_ge(c_tev.sem, tev0 + (i - 1))
                                    pp = pps[i % 2]
                                    ins = None
                                    for d in range(ND):
                                        ins = t.transpose(pp[:, d * 128:(d + 1) * 128],
                                                          xqb[:, d * 128:(d + 1) * 128], ident[:, :])
                                    c_tpd.inc(ins)

                if DEBUG:
                    with nc.Block() as blk:
                        @blk.gpsimd
                        def _(g):
                            g.wait_ge(c_tev.sem, c_tev.n)
                            c_stg.dinc(g.dma_start(dbg_xnqT[:, :], xnqT[:, :]))
                            g.wait_ge(c_stg.sem, c_stg.n)

                with contextlib.ExitStack() as st_g:
                    if PH < 3:
                        return nc
                    wgq0 = sb("wgq0", [128, 2 * 16 * 512], F8, st_g)
                    wgq1 = sb("wgq1", [128, 2 * 16 * 512], F8, st_g)
                    gbuf = sb("gbuf", [128, TOK], BF16, st_g)
                    fbuf = sb("fbuf", [128, TOK], BF16, st_g)
                    tA = sb("tA", [128, TOK], BF16, st_g)
                    tB = sb("tB", [128, TOK], BF16, st_g)
                    tC = sb("tC", [128, TOK], BF16, st_g)
                    with nc.Block() as blk:
                        lw0 = [c.n for c in c_lw]
                        wcv0, mmd0, pev0, sg0, vg0 = (c_wcv.n, c_mmd.n,
                                                      c_pev.n, c_sg.n, c_vg.n)
                        with nc.psum_tensor("pgA", [128, 1024], F32) as pgA, \
                             nc.psum_tensor("pfA", [128, 1024], F32) as pfA, \
                             nc.psum_tensor("pgB", [128, 1024], F32) as pgB, \
                             nc.psum_tensor("pfB", [128, 1024], F32) as pfB:
                            pgs, pfs = [pgA, pgB], [pfA, pfB]

                            @blk.sync
                            def _(sync):
                                for jb in range(NHB):
                                    for gsel in range(2):
                                        for d in range(ND):
                                            k = (jb * 2 + gsel) * ND + d
                                            if k >= 4:
                                                sync.wait_ge(c_wcv.sem, wcv0 + (k - 3))
                                            src = wg_h[gsel, d * 128:(d + 1) * 128,
                                                       jb * 512:(jb + 1) * 512]
                                            c_lw[k % 4].dinc(sync.dma_start(
                                                wstg[:, (k % 4) * 512:(k % 4 + 1) * 512], src))

                            @blk.scalar
                            def _(s):
                                s = EP(s, c_ss)

                                def convs(jb):
                                    for gsel in range(2):
                                        for d in range(ND):
                                            k = (jb * 2 + gsel) * ND + d
                                            s.wait_ge(c_lw[k % 4].sem, lw0[k % 4] + 16 * (k // 4 + 1))
                                            if d == 0 and gsel == 0 and jb >= 2:
                                                s.wait_ge(c_mmd.sem, mmd0 + (jb - 1) * 4)
                                            dstt = wgq0 if gsel == 0 else wgq1
                                            dst = dstt[:, (jb % 2) * 8192 + d * 512:
                                                       (jb % 2) * 8192 + (d + 1) * 512]
                                            c_wcv.inc(s.copy(
                                                dst, wstg[:, (k % 4) * 512:(k % 4 + 1) * 512]))

                                convs(0)
                                convs(1)
                                for jb in range(NHB):
                                    for jj in range(jb * 4, jb * 4 + 4):
                                        s.wait_ge(c_vg.sem, vg0 + jj * 2 + 1)
                                        c_sg.inc(s.activation(tB[:, :], tC[:, :], AF.Tanh))
                                    if jb + 2 < NHB:
                                        convs(jb + 2)

                            @blk.vector
                            def _(v):
                                v = EP(v, c_vs)
                                for jb in range(NHB):
                                    for jj in range(jb * 4, jb * 4 + 4):
                                        v.wait_ge(c_mmd.sem, mmd0 + (jj + 1))
                                        pg, pf = pgs[jj % 2], pfs[jj % 2]
                                        vgd(v, v.tensor_scalar(gbuf[:, :], pg[:, :], 1.0, None, ALU.mult))
                                        c_pev.inc(v.tensor_scalar(fbuf[:, :], pf[:, :], 1.0, None, ALU.mult))
                                        vgd(v, v.tensor_mul(tA[:, :], gbuf[:, :], gbuf[:, :]))
                                        vgd(v, v.tensor_mul(tB[:, :], tA[:, :], gbuf[:, :]))
                                        vgd(v, v.tensor_scalar(tA[:, :], tB[:, :], C3, None, ALU.mult))
                                        vgd(v, v.tensor_add(tB[:, :], gbuf[:, :], tA[:, :]))
                                        c_vg.inc(v.tensor_scalar(tC[:, :], tB[:, :], CS, None, ALU.mult))
                                        v.wait_ge(c_sg.sem, sg0 + jj + 1)
                                        vgd(v, v.tensor_scalar(tC[:, :], tB[:, :], 0.5, 0.5, ALU.mult, ALU.add))
                                        vgd(v, v.tensor_mul(tB[:, :], gbuf[:, :], tC[:, :]))
                                        vgd(v, v.tensor_mul(tA[:, :], tB[:, :], fbuf[:, :]))
                                        c_vg.inc(v.tensor_scalar(act8[:, jj * TOK:(jj + 1) * TOK],
                                                                 tA[:, :], 1.0, None, ALU.mult))

                            @blk.tensor
                            def _(t):
                                t.wait_ge(c_tev.sem, c_tev.n)
                                for jj in range(NH):
                                    jb, sub = jj // 4, jj % 4
                                    t.wait_ge(c_wcv.sem, wcv0 + (jb + 1) * 32)
                                    if jj >= 2:
                                        t.wait_ge(c_pev.sem, pev0 + (jj - 1))
                                    pg, pf = pgs[jj % 2], pfs[jj % 2]
                                    ins = None
                                    for d in range(ND):
                                        off = (jb % 2) * 8192 + d * 512 + sub * 128
                                        l0 = wgq0[:, off:off + 128]
                                        l1 = wgq1[:, off:off + 128]
                                        r = xnqT[:, d * TOK:(d + 1) * TOK]
                                        stt, spp = (d == 0), (d == ND - 1)
                                        t.matmul(pg[:, 0:512], l0, r[:, 0:512], start=stt, stop=spp)
                                        t.matmul(pg[:, 512:1024], l0, r[:, 512:1024], start=stt, stop=spp)
                                        t.matmul(pf[:, 0:512], l1, r[:, 0:512], start=stt, stop=spp)
                                        ins = t.matmul(pf[:, 512:1024], l1, r[:, 512:1024],
                                                       start=stt, stop=spp)
                                    c_mmd.inc(ins)


            with contextlib.ExitStack() as st_d:
                if PH < 4:
                    return nc
                wlstg = sb("wlstg", [128, 2 * 512], F32, st_d)
                wlq = sb("wlq", [128, 2 * NH * 512], F8, st_d)
                fout = sb("fout", [128, 2 * 512], BF16, st_d)
                with nc.Block() as blk:
                    lw0 = [c.n for c in c_lw]
                    wcv0, mmd0, pev0, stg0 = (c_wcv.n, c_mmd.n, c_pev.n, c_stg.n)
                    with nc.psum_tensor("pdA", [128, 512], F32) as pdA, \
                         nc.psum_tensor("pdB", [128, 512], F32) as pdB:
                        pds = [pdA, pdB]

                        @blk.sync
                        def _(sync):
                            for q in range(4):
                                for jj in range(NH):
                                    kq = q * NH + jj
                                    if kq >= 2:
                                        sync.wait_ge(c_wcv.sem, wcv0 + kq - 1)
                                    srcap = wl_h[jj * 128:(jj + 1) * 128,
                                                 q * 512:(q + 1) * 512]
                                    c_lw[kq % 2].dinc(sync.dma_start(
                                        wlstg[:, (kq % 2) * 512:(kq % 2 + 1) * 512], srcap))

                        @blk.scalar
                        def _(s):
                            s = EP(s, c_ss)
                            for q in range(4):
                                for jj in range(NH):
                                    kq = q * NH + jj
                                    s.wait_ge(c_lw[kq % 2].sem, lw0[kq % 2] + 16 * (kq // 2 + 1))
                                    if jj == 0 and q >= 2:
                                        s.wait_ge(c_mmd.sem, mmd0 + (q - 1) * NTQ)
                                    dst = wlq[:, (q % 2) * (NH * 512) + jj * 512:
                                              (q % 2) * (NH * 512) + (jj + 1) * 512]
                                    c_wcv.inc(s.copy(dst,
                                                     wlstg[:, (kq % 2) * 512:(kq % 2 + 1) * 512]))

                        @blk.tensor
                        def _(t):
                            t.wait_ge(c_vg.sem, c_vg.n)
                            for q in range(4):
                                for ti in range(NTQ):
                                    gi = q * NTQ + ti
                                    if gi >= 2:
                                        t.wait_ge(c_pev.sem, pev0 + gi - 1)
                                    pd = pds[gi % 2]
                                    ins = None
                                    for jj in range(NH):
                                        if ti == 0:
                                            t.wait_ge(c_wcv.sem, wcv0 + q * NH + jj + 1)
                                        l = act8[:, jj * TOK + ti * 128: jj * TOK + (ti + 1) * 128]
                                        w = wlq[:, (q % 2) * (NH * 512) + jj * 512:
                                                (q % 2) * (NH * 512) + (jj + 1) * 512]
                                        ins = t.matmul(pd[:, :], l, w,
                                                       start=(jj == 0), stop=(jj == NH - 1))
                                    c_mmd.inc(ins)

                        @blk.vector
                        def _(v):
                            v = EP(v, c_vs)
                            for q in range(4):
                                for ti in range(NTQ):
                                    gi = q * NTQ + ti
                                    v.wait_ge(c_mmd.sem, mmd0 + gi + 1)
                                    if gi >= 2:
                                        v.wait_ge(c_stg.sem, stg0 + 16 * gi)
                                    c_pev.inc(v.tensor_scalar(
                                        fout[:, (gi % 2) * 512:(gi % 2 + 1) * 512],
                                        pds[gi % 2][:, :], 1.0, None, ALU.mult))

                        @blk.gpsimd
                        def _(g):
                            for q in range(4):
                                for ti in range(NTQ):
                                    gi = q * NTQ + ti
                                    g.wait_ge(c_pev.sem, pev0 + gi + 1)
                                    dst = ffn_hbm[ti * 128:(ti + 1) * 128,
                                                  q * 512:(q + 1) * 512]
                                    c_stg.dinc(g.dma_start(
                                        dst, fout[:, (gi % 2) * 512:(gi % 2 + 1) * 512]))
                            g.wait_ge(c_stg.sem, c_stg.n)

        if DEBUG:
            with nc.Block() as blk:
                @blk.gpsimd
                def _(g):
                    c_stg.dinc(g.dma_start(dbg_ffn[:, :], ffn_hbm[:, :]))
                    g.wait_ge(c_stg.sem, c_stg.n)

        with contextlib.ExitStack() as st_p4:
            if PH < 5:
                return nc
            xsT = sb("xsT", [128, ND * T], BF16, st_p4)
            wb = sb("wb", [128, ND * 4 * 512], BF16, st_p4)

            with contextlib.ExitStack() as st_a:
                xbuf = sb("xbuf4", [128, 2 * D], F32, st_a)
                xsb = sb("xsb", [128, D], BF16, st_a)
                with nc.Block() as blk:
                    lx0 = [c.n for c in c_lx]
                    lw0 = [c.n for c in c_lw]
                    xf0, tin0, tpd0, tev0, wcv0 = (
                        c_xfree.n, c_tin.n, c_tpd.n, c_tev.n, c_wcv.n)
                    with nc.psum_tensor("ptA4", [128, 2048], BF16) as ptA, \
                         nc.psum_tensor("ptB4", [128, 2048], BF16) as ptB:
                        pps = [ptA, ptB]

                        @blk.sync
                        def _(sync):
                            for i in range(NT):
                                if i >= 2:
                                    sync.wait_ge(c_xfree.sem, xf0 + (i - 1))
                                c_lx[i % 2].dinc(sync.dma_start(xbuf[:, (i % 2) * D:(i % 2 + 1) * D],
                                                                x_h[i * 128:(i + 1) * 128, :]))
                            for d in range(ND):
                                for cb in range(4):
                                    k = d * 4 + cb
                                    if k >= 4:
                                        sync.wait_ge(c_wcv.sem, wcv0 + (k - 3))
                                    src = wq_h[d * 128:(d + 1) * 128, cb * 512:(cb + 1) * 512]
                                    c_lw[k % 4].dinc(sync.dma_start(
                                        wstg[:, (k % 4) * 512:(k % 4 + 1) * 512], src))

                        @blk.vector
                        def _(v):
                            v = EP(v, c_vs)
                            for i in range(NT):
                                v.wait_ge(c_lx[i % 2].sem, lx0[i % 2] + 16 * (i // 2 + 1))
                                sl = xbuf[:, (i % 2) * D:(i % 2 + 1) * D]
                                c_tin.inc(v.tensor_scalar(xsb[:, :], sl, A_c[:, i:i + 1],
                                                           nmA_c[:, i:i + 1], ALU.mult, ALU.add))
                                c_xfree.inc(v.tensor_scalar(scal1[:1, 2:3], scal1[:1, 3:4], 1.0, None, ALU.mult))
                                v.wait_ge(c_tpd.sem, tpd0 + (i + 1))
                                pp = pps[i % 2]
                                ins = None
                                for d in range(ND):
                                    ins = v.tensor_scalar(xsT[:, d * T + i * 128: d * T + (i + 1) * 128],
                                                          pp[:, d * 128:(d + 1) * 128], 1.0, None, ALU.mult)
                                c_tev.inc(ins)
                            for d in range(ND):
                                for cb in range(4):
                                    k = d * 4 + cb
                                    v.wait_ge(c_lw[k % 4].sem, lw0[k % 4] + 16 * (k // 4 + 1))
                                    dst = wb[:, k * 512:(k + 1) * 512]
                                    c_wcv.inc(v.tensor_scalar(
                                        dst, wstg[:, (k % 4) * 512:(k % 4 + 1) * 512],
                                        gq_c[:, d:d + 1], None, ALU.mult))

                        @blk.tensor
                        def _(t):
                            for i in range(NT):
                                t.wait_ge(c_tin.sem, tin0 + (i + 1))
                                if i >= 2:
                                    t.wait_ge(c_tev.sem, tev0 + (i - 1))
                                pp = pps[i % 2]
                                ins = None
                                for d in range(ND):
                                    ins = t.transpose(pp[:, d * 128:(d + 1) * 128],
                                                      xsb[:, d * 128:(d + 1) * 128], ident[:, :])
                                c_tpd.inc(ins)

            with contextlib.ExitStack() as st_b:
                qf = sb("qf", [128, 2 * D], F32, st_b)
                knstg = sb("knstg", [128, 2 * D], BF16, st_b)
                qbf = sb("qbf", [128, 2 * D], BF16, st_b)
                rsq_t = sb("rsq_t", [128, H], F32, st_b)
                nst1 = sb("nst1", [128, H], F32, st_b)
                nst2 = sb("nst2", [128, H], F32, st_b)

                def proj_pass(n_tiles, out_kind, wcv_base, bias_row, bias_off, out_hbm):
                    with nc.Block() as blk:
                        mmd0, sg0, vg0, tpd0, tev0, pev0, stg0 = (
                            c_mmd.n, c_sg.n, c_vg.n, c_tpd.n, c_tev.n, c_pev.n, c_stg.n)
                        with nc.psum_tensor("pq_" + out_kind, [128, 2048], F32) as pq, \
                             nc.psum_tensor("pt1_" + out_kind, [128, 2048], BF16) as pt1, \
                             nc.psum_tensor("pt2_" + out_kind, [128, 2048], BF16) as pt2:
                            pts = [pt1, pt2]

                            @blk.tensor
                            def _(t):
                                t.wait_ge(c_tev.sem, tev0)

                                def transposes(tj):
                                    t.wait_ge(c_vg.sem, vg0 + 2 * tj + 2)
                                    if tj >= 2:
                                        t.wait_ge(c_tev.sem, tev0 + (tj - 1))
                                    pp = pts[tj % 2]
                                    ins2 = None
                                    for hh in range(H):
                                        ins2 = t.transpose(pp[:, hh * 128:(hh + 1) * 128],
                                                           qbf[:, (tj % 2) * D + hh * 128:
                                                               (tj % 2) * D + (hh + 1) * 128],
                                                           ident[:, :])
                                    c_tpd.inc(ins2)

                                for ti in range(n_tiles):
                                    if ti >= 1:
                                        t.wait_ge(c_pev.sem, pev0 + ti)
                                    for d in range(ND):
                                        if ti == 0:
                                            t.wait_ge(c_wcv.sem, wcv_base + 4 * (d + 1))
                                        l = xsT[:, d * T + ti * 128: d * T + (ti + 1) * 128]
                                        stt = (d == 0)
                                        for cb in range(4):
                                            t.matmul(pq[:, cb * 512:(cb + 1) * 512], l,
                                                     wb[:, (d * 4 + cb) * 512:(d * 4 + cb + 1) * 512],
                                                     start=stt, stop=False)
                                    ins = None
                                    for cb in range(4):
                                        ins = t.matmul(pq[:, cb * 512:(cb + 1) * 512], ones1f[:1, :],
                                                       bias_row[:1, bias_off + cb * 512:
                                                                bias_off + (cb + 1) * 512],
                                                       start=False, stop=True)
                                    c_mmd.inc(ins)
                                    if out_kind != 'v' and ti >= 1:
                                        transposes(ti - 1)
                                if out_kind != 'v':
                                    transposes(n_tiles - 1)

                            if out_kind != 'v':
                                @blk.scalar
                                def _(s):
                                    s = EP(s, c_ss)
                                    for ti in range(n_tiles):
                                        s.wait_ge(c_pev.sem, pev0 + ti + 1)
                                        if ti >= 1:
                                            s.wait_ge(c_vg.sem, vg0 + 2 * (ti - 1) + 1)
                                        c_sg.inc(s.square(wstg[:, :],
                                                          qf[:, (ti % 2) * D:(ti % 2 + 1) * D]))
                                        s.wait_ge(c_vg.sem, vg0 + 2 * ti + 1)
                                        c_sg.inc(s.sqrt(nst2[:, :], nst1[:, :]))

                                @blk.vector
                                def _(v):
                                    v = EP(v, c_vs)
                                    for ti in range(n_tiles):
                                        v.wait_ge(c_mmd.sem, mmd0 + ti + 1)
                                        if ti >= 2:
                                            v.wait_ge(c_sg.sem, sg0 + 2 * (ti - 2) + 1)
                                        qfs = qf[:, (ti % 2) * D:(ti % 2 + 1) * D]
                                        ins = None
                                        for cb in range(4):
                                            ins = v.tensor_scalar(qfs[:, cb * 512:(cb + 1) * 512],
                                                                  pq[:, cb * 512:(cb + 1) * 512],
                                                                  1.0, None, ALU.mult)
                                        c_pev.inc(ins)
                                        v.wait_ge(c_sg.sem, sg0 + 2 * ti + 1)
                                        sq = wstg[:, :]
                                        vgd(v, v.tensor_reduce(rsq_t[:, :],
                                                               sq.rearrange("p (h e) -> p h e", e=128),
                                                               AX.X, ALU.add))
                                        c_vg.inc(v.tensor_scalar(nst1[:, :], rsq_t[:, :], 1.0 / HD,
                                                                 EPS32, ALU.mult, ALU.add))
                                        v.wait_ge(c_sg.sem, sg0 + 2 * ti + 2)
                                        vgd(v, v.reciprocal(rsq_t[:, :], nst2[:, :]))
                                        vgd(v, v.tensor_mul(nst2[:, :], rsq_t[:, :], rsq_t[:, :]))
                                        vgd(v, v.tensor_mul(nst2[:, :], nst2[:, :], nst1[:, :]))
                                        vgd(v, v.tensor_scalar(nst2[:, :], nst2[:, :], -0.5, 1.5,
                                                               ALU.mult, ALU.add))
                                        vgd(v, v.tensor_mul(rsq_t[:, :], rsq_t[:, :], nst2[:, :]))
                                        ins = None
                                        for hh in range(H):
                                            ins = v.tensor_scalar(
                                                qbf[:, (ti % 2) * D + hh * 128:(ti % 2) * D + (hh + 1) * 128],
                                                qfs[:, hh * 128:(hh + 1) * 128],
                                                rsq_t[:, hh:hh + 1], None, ALU.mult)
                                        c_vg.inc(ins)
                                        v.wait_ge(c_tpd.sem, tpd0 + (ti + 1))
                                        if ti >= 2:
                                            v.wait_ge(c_stg.sem, stg0 + 16 * H * ti)
                                        pp = pts[ti % 2]
                                        ins = None
                                        for hh in range(H):
                                            dst = knstg[:, (ti % 2) * D + hh * 128:
                                                        (ti % 2) * D + (hh + 1) * 128]
                                            ins = v.tensor_scalar(dst, pp[:, hh * 128:(hh + 1) * 128],
                                                                  ghs_c[:, 0:1], None, ALU.mult)
                                        c_tev.inc(ins)
                            else:
                                @blk.vector
                                def _(v):
                                    v = EP(v, c_vs)
                                    for ti in range(n_tiles):
                                        v.wait_ge(c_mmd.sem, mmd0 + (ti + 1))
                                        if ti >= 2:
                                            v.wait_ge(c_stg.sem, stg0 + 16 * H * ti)
                                        sl = knstg[:, (ti % 2) * D:(ti % 2 + 1) * D]
                                        _nocp = os.environ.get("KVNOCOPY") == "1"
                                        ins = None
                                        for cb in range(4):
                                            ins = v.tensor_scalar(sl[:, cb * 512:(cb + 1) * 512],
                                                                  pq[:, cb * 512:(cb + 1) * 512],
                                                                  1.0, None, ALU.mult)
                                        c_pev.inc(ins)
                                        c_tev.inc(v.tensor_scalar(scal1[:1, 2:3], scal1[:1, 3:4], 1.0, None, ALU.mult))

                            import os as _os
                            _skipdma = out_kind == 'v' and _os.environ.get("KSKIPV") == "1"

                            @blk.gpsimd
                            def _(g):
                                for ti in range(n_tiles):
                                    g.wait_ge(c_tev.sem, tev0 + (ti + 1))
                                    sl = knstg[:, (ti % 2) * D:(ti % 2 + 1) * D]
                                    if _skipdma:
                                        continue
                                    _kst = os.environ.get("KVMODE", "") == "v2" and out_kind == 'v'
                                    for hh in range(H):
                                        if out_kind == 'v' and not _kst:
                                            dst = out_hbm[hh, ti * 128:(ti + 1) * 128, :]
                                        else:
                                            dst = out_hbm[hh, :, ti * 128:(ti + 1) * 128]
                                        c_stg.dinc(g.dma_start(dst, sl[:, hh * 128:(hh + 1) * 128]))

                if PH < 6:
                    return nc
                proj_pass(NTQ, 'q', c_wcv.n - 64, bq_r, 0, qn_hbm)

                def kv_fold(col_off):
                    with nc.Block() as blk:
                        lw0 = [c.n for c in c_lw]
                        wcv0 = c_wcv.n

                        @blk.sync
                        def _(sync):
                            for d in range(ND):
                                for cb in range(4):
                                    k = d * 4 + cb
                                    if k >= 4:
                                        sync.wait_ge(c_wcv.sem, wcv0 + (k - 3))
                                    src = wkv_h[d * 128:(d + 1) * 128,
                                                col_off + cb * 512:col_off + (cb + 1) * 512]
                                    c_lw[k % 4].dinc(sync.dma_start(
                                        wstg[:, (k % 4) * 512:(k % 4 + 1) * 512], src))

                        @blk.vector
                        def _(v):
                            v = EP(v, c_vs)
                            for d in range(ND):
                                for cb in range(4):
                                    k = d * 4 + cb
                                    v.wait_ge(c_lw[k % 4].sem, lw0[k % 4] + 16 * (k // 4 + 1))
                                    c_wcv.inc(v.tensor_scalar(
                                        wb[:, k * 512:(k + 1) * 512],
                                        wstg[:, (k % 4) * 512:(k % 4 + 1) * 512],
                                        gkv_c[:, d:d + 1], None, ALU.mult))

                if PH < 7:
                    return nc
                kv_fold(0)
                proj_pass(NT, 'k', c_wcv.n - 64, bkv_r, 0, kn_hbm)
                if PH < 8:
                    return nc
                kv_fold(D)
                if PH < 9:
                    return nc
                _kvm = os.environ.get("KVMODE", "")
                _vmode = 'k2' if _kvm == "k" else 'v'
                _vout = v_hbm
                if _kvm == "k":
                    _vout = kn_hbm
                elif _kvm == "v2":
                    _vout = kn_hbm
                proj_pass(int(os.environ.get("KVTILES", str(NT))), _vmode, c_wcv.n - 64, bkv_r, D, _vout)

        if DEBUG:
            with nc.Block() as blk:
                @blk.gpsimd
                def _(g):
                    for hh in range(H):
                        c_stg.dinc(g.dma_start(dbg_qnT[:, hh * TOK:(hh + 1) * TOK], qn_hbm[hh, :, :]))
                        c_stg.dinc(g.dma_start(dbg_knT[:, hh * T:(hh + 1) * T], kn_hbm[hh, :, :]))
                    g.wait_ge(c_stg.sem, c_stg.n)

        with contextlib.ExitStack() as st_wv:
            if PH < 10:
                return nc
            wvT = sb("wvT", [128, H * TOK], BF16, st_wv)

            with contextlib.ExitStack() as st_p5:
                expT = sb("expT", [128, NT * TOK], BF16, st_p5)
                knb = sb("knb", [128, 2 * T], BF16, st_p5)
                vhb = sb("vhb", [128, 2 * T], BF16, st_p5)
                qnb = sb("qnb", [128, 2 * TOK], BF16, st_p5)
                zrow = sb("zrow", [1, TOK], F32, st_p5)
                zrow2 = sb("zrow2", [1, TOK], F32, st_p5)
                zb = sb("zb", [128, TOK], F32, st_p5)
                with nc.Block() as blk:
                    lg0 = [c.n for c in c_lg]
                    mmd0, zw0, sg0, zr0, bc0, wvn0 = (
                        c_mmd.n, c_zw.n, c_sg.n, c_zr.n, c_bc.n, c_wvn.n)
                    with nc.psum_tensor("psA5", [128, 1024], F32) as psA, \
                         nc.psum_tensor("psB5", [128, 1024], F32) as psB, \
                         nc.psum_tensor("pwv", [128, 1024], F32) as pwv, \
                         nc.psum_tensor("pzA", [1, 512], F32) as pzA, \
                         nc.psum_tensor("pzB", [1, 512], F32) as pzB:
                        pss = [psA, psB]

                        @blk.gpsimd
                        def _(g):
                            g.wait_ge(c_stg.sem, c_stg.n)
                            for h in range(H):
                                if h >= 2:
                                    g.wait_ge(c_wvn.sem, wvn0 + (h - 1))
                                c_lg[h % 2].dinc(g.dma_start(knb[:, (h % 2) * T:(h % 2 + 1) * T],
                                                             kn_hbm[h, :, :]))
                                c_lg[h % 2].dinc(g.dma_start(qnb[:, (h % 2) * TOK:(h % 2 + 1) * TOK],
                                                             qn_hbm[h, :, :]))
                                c_lg[h % 2].dinc(g.dma_start(
                                    vhb[:, (h % 2) * T:(h % 2 + 1) * T].rearrange(
                                        "p (c v) -> p c v", v=128),
                                    v_hbm[h].rearrange("(c p) v -> p c v", p=128)))

                        def sc_wait(t, gi):
                            if gi >= 2:
                                t.wait_ge(c_sg.sem, sg0 + gi - 1)

                        @blk.tensor
                        def _(t):
                            for h in range(H):
                                t.wait_ge(c_lg[h % 2].sem, lg0[h % 2] + 48 * (h // 2 + 1))
                                gi0 = h * NT
                                sc_wait(t, gi0)
                                l = knb[:, (h % 2) * T + 0:(h % 2) * T + 128]
                                q0 = qnb[:, (h % 2) * TOK:(h % 2) * TOK + 512]
                                q1 = qnb[:, (h % 2) * TOK + 512:(h % 2 + 1) * TOK]
                                ps = pss[gi0 % 2]
                                t.matmul(ps[:, 0:512], l, q0, start=True, stop=True)
                                c_mmd.inc(t.matmul(ps[:, 512:1024], l, q1, start=True, stop=True))
                                if h >= 1:
                                    t.wait_ge(c_wvn.sem, wvn0 + h)
                                for i in range(NT):
                                    gi = h * NT + i
                                    if i + 1 < NT:
                                        gi2 = gi + 1
                                        sc_wait(t, gi2)
                                        if i + 1 == 1 and h >= 1:
                                            t.wait_ge(c_wvn.sem, wvn0 + h)
                                        l2 = knb[:, (h % 2) * T + (i + 1) * 128:
                                                 (h % 2) * T + (i + 2) * 128]
                                        ps2 = pss[gi2 % 2]
                                        t.matmul(ps2[:, 0:512], l2, q0, start=True, stop=True)
                                        c_mmd.inc(t.matmul(ps2[:, 512:1024], l2, q1,
                                                           start=True, stop=True))
                                    t.wait_ge(c_sg.sem, sg0 + gi + 1)
                                    e = expT[:, i * TOK:(i + 1) * TOK]
                                    stt, spp = (i == 0), (i == NT - 1)
                                    t.matmul(pzA[:1, :], onescol_b[:, 0:1], e[:, 0:512],
                                             start=stt, stop=spp)
                                    t.matmul(pzB[:1, :], onescol_b[:, 0:1], e[:, 512:1024],
                                             start=stt, stop=spp)
                                    vl = vhb[:, (h % 2) * T + i * 128:(h % 2) * T + (i + 1) * 128]
                                    t.matmul(pwv[:, 0:512], vl, e[:, 0:512], start=stt, stop=spp)
                                    c_zw.inc(t.matmul(pwv[:, 512:1024], vl, e[:, 512:1024],
                                                      start=stt, stop=spp))
                                t.wait_ge(c_zr.sem, zr0 + (h + 1))
                                t.matmul(pss[1][:, 0:512], ones1f[:1, :], zrow[:1, 0:512],
                                         start=True, stop=True)
                                c_bc.inc(t.matmul(pss[1][:, 512:1024], ones1f[:1, :],
                                                  zrow[:1, 512:1024], start=True, stop=True))

                        @blk.scalar
                        def _(s):
                            s = EP(s, c_ss)
                            s.wait_ge(c_vg.sem, c_vg.n)
                            for h in range(H):
                                for i in range(NT):
                                    gi = h * NT + i
                                    s.wait_ge(c_mmd.sem, mmd0 + gi + 1)
                                    ps = pss[gi % 2]
                                    e = expT[:, i * TOK:(i + 1) * TOK]
                                    c_sg.inc(s.activation(e[:, :], ps[:, :], AF.Exp,
                                                          bias=negM_c[:, 0:1], scale=1.0))

                        @blk.vector
                        def _(v):
                            v = EP(v, c_vs)
                            for h in range(H):
                                v.wait_ge(c_zw.sem, zw0 + (h + 1) * NT)
                                v.tensor_scalar(zrow2[:1, 0:512], pzA[:1, :], 1.0, None, ALU.mult)
                                vgd(v, v.tensor_scalar(zrow2[:1, 512:1024], pzB[:1, :], 1.0, None, ALU.mult))
                                c_zr.inc(v.reciprocal(zrow[:1, :], zrow2[:1, :]))
                                v.wait_ge(c_bc.sem, bc0 + (h + 1))
                                vgd(v, v.tensor_scalar(zb[:, :], pss[1][:, :], 1.0, None, ALU.mult))
                                c_wvn.inc(v.tensor_mul(wvT[:, h * TOK:(h + 1) * TOK],
                                                       pwv[:, :], zb[:, :]))

            if DEBUG:
                with nc.Block() as blk:
                    @blk.gpsimd
                    def _(g):
                        c_stg.dinc(g.dma_start(dbg_wvT[:, :], wvT[:, :]))
                        g.wait_ge(c_stg.sem, c_stg.n)

            with contextlib.ExitStack() as st_p6:
                if PH < 11:
                    return nc
                wob = sb("wob", [128, 4 * H * 512], BF16, st_p6)
                xrb = sb("xrb", [128, 2 * D], F32, st_p6)
                frb = sb("frb", [128, 2 * D], BF16, st_p6)
                yout = sb("yout", [128, 2 * D], F32, st_p6)
                with nc.Block() as blk:
                    lx0 = [c.n for c in c_lx]
                    lw0 = [c.n for c in c_lw]
                    lg0 = [c.n for c in c_lg]
                    wcv0, mmd0, pev0, stg0, xf0, vg0 = (
                        c_wcv.n, c_mmd.n, c_pev.n,
                        c_stg.n, c_xfree.n, c_vg.n)
                    with nc.psum_tensor("poA", [128, 512], F32) as poA, \
                         nc.psum_tensor("poB", [128, 512], F32) as poB:
                        pos = [poA, poB]

                        @blk.sync
                        def _(sync):
                            for cb in range(4):
                                for hh in range(H):
                                    k = cb * H + hh
                                    if k >= 4:
                                        sync.wait_ge(c_wcv.sem, wcv0 + (k - 3))
                                    src = wo_h[hh * 128:(hh + 1) * 128, cb * 512:(cb + 1) * 512]
                                    c_lw[k % 4].dinc(sync.dma_start(
                                        wstg[:, (k % 4) * 512:(k % 4 + 1) * 512], src))
                            for ti in range(NTQ):
                                if ti >= 2:
                                    sync.wait_ge(c_xfree.sem, xf0 + (ti - 1))
                                c_lx[ti % 2].dinc(sync.dma_start(xrb[:, (ti % 2) * D:(ti % 2 + 1) * D],
                                                                 x_h[ti * 128:(ti + 1) * 128, :]))

                        @blk.gpsimd
                        def _(g):
                            for ti in range(NTQ + 2):
                                if 0 <= ti - 2 < NTQ:
                                    tj = ti - 2
                                    g.wait_ge(c_pev.sem, pev0 + (tj + 1) * 4)
                                    c_stg.dinc(g.dma_start(y_h[tj * 128:(tj + 1) * 128, :],
                                                           yout[:, (tj % 2) * D:(tj % 2 + 1) * D]))
                                if ti < NTQ:
                                    if ti >= 2:
                                        g.wait_ge(c_vg.sem, vg0 + (ti - 1))
                                    c_lg[ti % 2].dinc(g.dma_start(frb[:, (ti % 2) * D:(ti % 2 + 1) * D],
                                                                  ffn_hbm[ti * 128:(ti + 1) * 128, :]))
                            g.wait_ge(c_stg.sem, c_stg.n)

                        @blk.vector
                        def _(v):
                            v = EP(v, c_vs)
                            for cb in range(4):
                                for hh in range(H):
                                    k = cb * H + hh
                                    v.wait_ge(c_lw[k % 4].sem, lw0[k % 4] + 16 * (k // 4 + 1))
                                    c_wcv.inc(v.tensor_scalar(wob[:, k * 512:(k + 1) * 512],
                                                              wstg[:, (k % 4) * 512:(k % 4 + 1) * 512],
                                                              1.0, None, ALU.mult))

                        @blk.tensor
                        def _(t):
                            t.wait_ge(c_wvn.sem, c_wvn.n)
                            for ti in range(NTQ):
                                for cb in range(4):
                                    gi = ti * 4 + cb
                                    t.wait_ge(c_wcv.sem, wcv0 + (cb + 1) * H)
                                    if gi >= 2:
                                        t.wait_ge(c_pev.sem, pev0 + (gi - 1))
                                    po = pos[gi % 2]
                                    for hh in range(H):
                                        l = wvT[:, hh * TOK + ti * 128: hh * TOK + (ti + 1) * 128]
                                        t.matmul(po[:, :], l,
                                                 wob[:, (cb * H + hh) * 512:(cb * H + hh + 1) * 512],
                                                 start=(hh == 0), stop=False)
                                    c_mmd.inc(t.matmul(po[:, :], ones1f[:1, :],
                                                       bo_r[:1, cb * 512:(cb + 1) * 512],
                                                       start=False, stop=True))

                        @blk.vector
                        def _(v):
                            v = EP(v, c_vs)
                            for ti in range(NTQ):
                                v.wait_ge(c_lx[ti % 2].sem, lx0[ti % 2] + 16 * (ti // 2 + 1))
                                v.wait_ge(c_lg[ti % 2].sem, lg0[ti % 2] + 16 * (ti // 2 + 1))
                                if ti >= 2:
                                    v.wait_ge(c_stg.sem, stg0 + 16 * ti)
                                for cb in range(4):
                                    gi = ti * 4 + cb
                                    v.wait_ge(c_mmd.sem, mmd0 + (gi + 1))
                                    po = pos[gi % 2]
                                    lo = (ti % 2) * D + cb * 512
                                    yl = yout[:, lo:lo + 512]
                                    vgd(v, v.tensor_add(yl, po[:, :], xrb[:, lo:lo + 512]))
                                    c_pev.inc(v.tensor_add(yl, yl, frb[:, lo:lo + 512]))
                                c_xfree.inc(v.tensor_scalar(scal1[:1, 2:3], scal1[:1, 3:4], 1.0, None, ALU.mult))
                                c_vg.inc(v.tensor_scalar(scal1[:1, 4:5], scal1[:1, 5:6], 1.0, None, ALU.mult))

    return nc


_NC_CACHE = {}


def _get_nc():
    key = ("nc", DEBUG)
    if key not in _NC_CACHE:
        _NC_CACHE[key] = build_nc()
    return _NC_CACHE[key]


def kernel(L0_attn_x, L0_attn_g_q, L0_attn_Wq, L0_attn_bq, L0_attn_g_kv,
           L0_attn_Wkv, L0_attn_bkv, L0_attn_g_head, L0_attn_Wo, L0_attn_bo,
           L0_ffn_w_gating, L0_ffn_w_linear):
    x = np.ascontiguousarray(np.asarray(L0_attn_x, dtype=np.float32))
    common = {
        "g_q": np.ascontiguousarray(L0_attn_g_q, dtype=np.float32),
        "Wq": np.ascontiguousarray(L0_attn_Wq, dtype=np.float32),
        "bq": np.ascontiguousarray(L0_attn_bq, dtype=np.float32),
        "g_kv": np.ascontiguousarray(L0_attn_g_kv, dtype=np.float32),
        "Wkv": np.ascontiguousarray(L0_attn_Wkv, dtype=np.float32),
        "bkv": np.ascontiguousarray(L0_attn_bkv, dtype=np.float32),
        "g_head": np.ascontiguousarray(L0_attn_g_head, dtype=np.float32),
        "Wo": np.ascontiguousarray(L0_attn_Wo, dtype=np.float32),
        "bo": np.ascontiguousarray(L0_attn_bo, dtype=np.float32),
        "w_gating": np.ascontiguousarray(L0_ffn_w_gating, dtype=np.float32),
        "w_linear": np.ascontiguousarray(L0_ffn_w_linear, dtype=np.float32),
    }
    in_maps = []
    for c in range(8):
        b, h = c // 2, c % 2
        xb = x[b]
        xc = np.concatenate([xb[h * TOK:(h + 1) * TOK], xb[(1 - h) * TOK:(2 - h) * TOK]], axis=0)
        m = dict(common)
        m["x"] = np.ascontiguousarray(xc)
        in_maps.append(m)
    nc = _get_nc()
    res = run_bass_kernel_spmd(nc, in_maps, core_ids=list(range(8)))
    out = np.empty((B, T, D), dtype=np.float32)
    for c in range(8):
        b, h = c // 2, c % 2
        out[b, h * TOK:(h + 1) * TOK] = res.results[c]["y"]
    return out


# version 07
# speedup vs baseline: 1.0220x; 1.0220x over previous
import contextlib
import os

import numpy as np
import ml_dtypes

import concourse.bass as bass
import concourse.mybir as mybir
from concourse.bass_utils import run_bass_kernel_spmd

BFD = ml_dtypes.bfloat16
F32 = mybir.dt.float32
BF16 = mybir.dt.bfloat16
F8 = mybir.dt.float8e4
AF = mybir.ActivationFunctionType
ALU = mybir.AluOpType
AX = mybir.AxisListType

B, T, D = 4, 2048, 2048
H, HD = 16, 128
HID = 8192
TOK = 1024
NT = T // 128
NTQ = TOK // 128
ND = D // 128
NH = HID // 128
NHB = 16
EPS32 = float(np.finfo(np.float32).eps)
C3 = float(np.float32(BFD(0.044715)))
CS = float(np.float32(BFD(np.sqrt(2.0 / np.pi))))
S4 = float(128.0 ** -0.25)
SQRT128 = float(np.sqrt(128.0))

DEBUG = False
PH = int(os.environ.get("KPH", "99"))


_INC_INFO = {}


class Ctr:
    def __init__(self, sem):
        self.sem = sem
        self.n = 0

    def inc(self, instr, k=1):
        instr.then_inc(self.sem, k)
        self.n += k
        _INC_INFO[id(instr)] = (self.sem, self.n, instr)
        return self.n

    def dinc(self, instr):
        return self.inc(instr, 16)


_EP_OPS = frozenset((
    "tensor_scalar", "tensor_tensor", "tensor_add", "tensor_sub", "tensor_mul",
    "tensor_max", "tensor_copy", "tensor_reduce", "reduce_sum", "reduce_max",
    "reciprocal", "scalar_tensor_tensor", "memset", "activation", "square",
    "sqrt", "mul", "add", "copy", "sign",
))


class EP:

    def __init__(self, eng, guard):
        self._e = eng
        self._g = guard
        self._pend = {}
        self._mark = {}

    def wait_ge(self, sem, n):
        self._mark[id(sem)] = max(self._mark.get(id(sem), -1), n)
        return self._e.wait_ge(sem, n)

    def _require(self, instr):
        info = _INC_INFO.get(id(instr))
        if info is None or info[2] is not instr:
            self._g.inc(instr)
            info = _INC_INFO[id(instr)]
        sem, n, _ = info
        if self._mark.get(id(sem), -1) < n:
            self._mark[id(sem)] = n
            self._e.wait_ge(sem, n)

    def __getattr__(self, name):
        f = getattr(self._e, name)
        if name not in _EP_OPS:
            return f

        def g(*args, **kw):
            aps = [a for a in args if isinstance(a, bass.AP)]
            aps += [a for a in kw.values() if isinstance(a, bass.AP)]
            if not aps:
                return f(*args, **kw)
            outn = aps[0].tensor.name
            for a in aps[1:]:
                w = self._pend.get(a.tensor.name)
                if w is not None:
                    self._require(w)
            w = self._pend.get(outn)
            if w is not None:
                self._require(w)
            instr = f(*args, **kw)
            self._pend[outn] = instr
            ac = kw.get("accum_out")
            if isinstance(ac, bass.AP):
                self._pend[ac.tensor.name] = instr
            return instr

        return g


def build_nc():
    nc = bass.Bass("TRN2", target_bir_lowering=False)

    x_h = nc.dram_tensor("x", [T, D], F32, kind="ExternalInput")
    gq_h = nc.dram_tensor("g_q", [D], F32, kind="ExternalInput")
    wq_h = nc.dram_tensor("Wq", [D, D], F32, kind="ExternalInput")
    bq_h = nc.dram_tensor("bq", [D], F32, kind="ExternalInput")
    gkv_h = nc.dram_tensor("g_kv", [D], F32, kind="ExternalInput")
    wkv_h = nc.dram_tensor("Wkv", [D, 2 * D], F32, kind="ExternalInput")
    bkv_h = nc.dram_tensor("bkv", [2 * D], F32, kind="ExternalInput")
    gh_h = nc.dram_tensor("g_head", [HD], F32, kind="ExternalInput")
    wo_h = nc.dram_tensor("Wo", [D, D], F32, kind="ExternalInput")
    bo_h = nc.dram_tensor("bo", [D], F32, kind="ExternalInput")
    wg_h = nc.dram_tensor("w_gating", [2, D, HID], F32, kind="ExternalInput")
    wl_h = nc.dram_tensor("w_linear", [HID, D], F32, kind="ExternalInput")
    y_h = nc.dram_tensor("y", [TOK, D], F32, kind="ExternalOutput")
    v_hbm = nc.dram_tensor("v_scr", [H, T, HD], BF16, kind="Internal")
    kn_hbm = nc.dram_tensor("kn_scr", [H, HD, T], BF16, kind="Internal")
    qn_hbm = nc.dram_tensor("qn_scr", [H, HD, TOK], BF16, kind="Internal")
    ffn_hbm = nc.dram_tensor("ffn_scr", [TOK, D], BF16, kind="Internal")
    if DEBUG:
        dbg_xnqT = nc.dram_tensor("dbg_xnqT", [128, ND * TOK], mybir.dt.uint8, kind="ExternalOutput")
        dbg_stats = nc.dram_tensor("dbg_stats", [128, 6 * NT], F32, kind="ExternalOutput")
        dbg_ffn = nc.dram_tensor("dbg_ffn", [TOK, D], BF16, kind="ExternalOutput")
        dbg_qnT = nc.dram_tensor("dbg_qnT", [128, H * TOK], BF16, kind="ExternalOutput")
        dbg_knT = nc.dram_tensor("dbg_knT", [128, H * T], BF16, kind="ExternalOutput")
        dbg_wvT = nc.dram_tensor("dbg_wvT", [128, H * TOK], BF16, kind="ExternalOutput")

    es = contextlib.ExitStack()
    with es:
        def sb(name, shape, dt, st=None):
            return (st or es).enter_context(nc.sbuf_tensor(name, shape, dt))

        def sem(name):
            return Ctr(es.enter_context(nc.semaphore(name)))

        c_lx = [sem("lx0"), sem("lx1")]
        c_lw = [sem("lw0"), sem("lw1"), sem("lw2"), sem("lw3")]
        c_ldx = sem("ldx")
        c_ldw = sem("ldw")
        c_lds = sem("lds")
        c_ldg = sem("ldg")
        c_lg = [sem("lg0"), sem("lg1")]
        c_stg = sem("stg")
        c_xfree = sem("xfree")
        c_tin = sem("tin")
        c_tpd = sem("tpd")
        c_tev = sem("tev")
        c_wcv = sem("wcv")
        c_mmd = sem("mmd")
        c_zw = sem("zw")
        c_pev = sem("pev")
        c_sg = sem("sg")
        c_vg = sem("vg")
        c_zr = sem("zr")
        c_bc = sem("bc")
        c_wvn = sem("wvn")
        c_vs = sem("vs")
        c_ss = sem("ss")
        c_ps = sem("ps")

        def vgd(eng, ins):
            c_vs.inc(ins)
            eng.wait_ge(c_vs.sem, c_vs.n)
            return ins

        ident = sb("ident", [128, 128], BF16)
        ones1f = sb("ones1f", [1, 128], F32)
        onescol_b = sb("onescol_b", [128, 1], BF16)
        ones1b = sb("ones1b", [1, 128], BF16)
        gq_c = sb("gq_c", [128, ND], F32)
        gkv_c = sb("gkv_c", [128, ND], F32)
        gh_c = sb("gh_c", [128, 1], F32)
        ghs_c = sb("ghs_c", [128, 1], F32)
        gh_r = sb("gh_r", [1, 128], F32)
        bq_r = sb("bq_r", [1, D], F32)
        bkv_r = sb("bkv_r", [1, 2 * D], F32)
        bo_r = sb("bo_r", [1, D], F32)
        negM_c = sb("negM_c", [128, 1], F32)
        scal1 = sb("scal1", [1, 128], F32)
        sum_c = sb("sum_c", [128, NT], F32)
        ssq_c = sb("ssq_c", [128, NT], F32)
        m_c = sb("m_c", [128, NT], F32)
        var_c = sb("var_c", [128, NT], F32)
        rstd_c = sb("rstd_c", [128, NT], F32)
        nmrs_c = sb("nmrs_c", [128, NT], F32)
        A_c = sb("A_c", [128, NT], F32)
        nmA_c = sb("nmA_c", [128, NT], F32)
        st1 = sb("st1", [128, NT], F32)
        st2 = sb("st2", [128, NT], F32)
        st3 = sb("st3", [128, NT], F32)
        wstg = sb("wstg", [128, 4 * 512], F32)

        with nc.Block() as blk:
            @blk.sync
            def _(sync):
                for dst, src in (
                    (gq_c[:, :], gq_h.ap().rearrange("(c p) -> p c", p=128)),
                    (gkv_c[:, :], gkv_h.ap().rearrange("(c p) -> p c", p=128)),
                    (gh_c[:, :], gh_h.ap().rearrange("(p one) -> p one", one=1)),
                    (gh_r[:, :], gh_h.ap().rearrange("(one p) -> one p", one=1)),
                    (bq_r[:, :], bq_h.ap().rearrange("(one d) -> one d", one=1)),
                    (bkv_r[:, :], bkv_h.ap().rearrange("(one d) -> one d", one=1)),
                    (bo_r[:, :], bo_h.ap().rearrange("(one d) -> one d", one=1)),
                ):
                    c_lds.dinc(sync.dma_start(dst, src, allow_slow_non_contiguous=True))

            @blk.gpsimd
            def _(g):
                g.memset(ident[:, :], 0.0).then_inc(c_ps.sem, 1)
                c_ps.n += 1
                g.memset(ones1f[:, :], 1.0)
                g.memset(onescol_b[:, :], 1.0)
                g.memset(ones1b[:, :], 1.0)
                g.wait_ge(c_ps.sem, c_ps.n)
                c_ps.inc(g.affine_select(out=ident[:, :], in_=ident[:, :],
                                         compare_op=ALU.not_equal, fill=1.0, base=0,
                                         pattern=[[-1, 128]], channel_multiplier=1))

            @blk.vector
            def _(v):
                v = EP(v, c_vs)
                v.wait_ge(c_lds.sem, c_lds.n)
                v.tensor_scalar(ghs_c[:, :], gh_c[:, :], S4, None, ALU.mult)
                vgd(v, v.tensor_mul(scal1[:1, :], gh_r[:1, :], gh_r[:1, :]))
                vgd(v, v.reduce_max(scal1[:1, :1], scal1[:1, :], AX.X))
                c_vg.inc(v.tensor_scalar(scal1[:1, :1], scal1[:1, :1], -SQRT128, None, ALU.mult))

        with nc.Block() as blk:
            with nc.psum_tensor("p0b", [128, 512], F32) as p0b:
                @blk.tensor
                def _(t):
                    t.wait_ge(c_vg.sem, c_vg.n)
                    c_mmd.inc(t.matmul(p0b[:, 0:1], ones1f[:1, :], scal1[:1, :1],
                                       start=True, stop=True))

                @blk.vector
                def _(v):
                    v = EP(v, c_vs)
                    v.wait_ge(c_mmd.sem, c_mmd.n)
                    c_vg.inc(v.tensor_scalar(negM_c[:, :], p0b[:, 0:1], 1.0, None, ALU.mult))

        with contextlib.ExitStack() as st_p1:
            if PH < 1:
                return nc
            xbuf = sb("xbuf1", [128, 2 * D], F32, st_p1)
            sqscr = sb("sqscr1", [128, 2 * D], F32, st_p1)
            with nc.Block() as blk:
                lx0 = [c.n for c in c_lx]
                xf0, sg0, vg0 = c_xfree.n, c_sg.n, c_vg.n

                @blk.sync
                def _(sync):
                    for i in range(NT):
                        if i >= 2:
                            sync.wait_ge(c_xfree.sem, xf0 + (i - 1))
                        c_lx[i % 2].dinc(sync.dma_start(xbuf[:, (i % 2) * D:(i % 2 + 1) * D],
                                                        x_h[i * 128:(i + 1) * 128, :]))

                @blk.scalar
                def _(s):
                    s = EP(s, c_ss)
                    for i in range(NT):
                        s.wait_ge(c_lx[i % 2].sem, lx0[i % 2] + 16 * (i // 2 + 1))
                        if i >= 2:
                            s.wait_ge(c_xfree.sem, xf0 + (i - 1))
                        c_sg.inc(s.square(sqscr[:, (i % 2) * D:(i % 2 + 1) * D],
                                          xbuf[:, (i % 2) * D:(i % 2 + 1) * D]))

                @blk.vector
                def _(v):
                    v = EP(v, c_vs)
                    for i in range(NT):
                        v.wait_ge(c_sg.sem, sg0 + (i + 1))
                        v.reduce_sum(sum_c[:, i:i + 1], xbuf[:, (i % 2) * D:(i % 2 + 1) * D], AX.X)
                        c_xfree.inc(v.reduce_sum(ssq_c[:, i:i + 1],
                                                 sqscr[:, (i % 2) * D:(i % 2 + 1) * D], AX.X))
                    vgd(v, v.tensor_scalar(m_c[:, :], sum_c[:, :], 1.0 / D, None, ALU.mult))
                    vgd(v, v.tensor_scalar(ssq_c[:, :], ssq_c[:, :], 1.0 / D, None, ALU.mult))
                    vgd(v, v.tensor_mul(st1[:, :], m_c[:, :], m_c[:, :]))
                    vgd(v, v.tensor_sub(var_c[:, :], ssq_c[:, :], st1[:, :]))
                    c_vg.inc(v.tensor_scalar(st1[:, :], var_c[:, :], 1e-5, None, ALU.add))

                @blk.scalar
                def _(s):
                    s = EP(s, c_ss)
                    s.wait_ge(c_vg.sem, vg0 + 1)
                    c_sg.inc(s.sqrt(st2[:, :], st1[:, :]))

                @blk.vector
                def _(v):
                    v = EP(v, c_vs)
                    v.wait_ge(c_sg.sem, c_sg.n)
                    vgd(v, v.reciprocal(st3[:, :], st2[:, :]))
                    vgd(v, v.tensor_mul(st2[:, :], st3[:, :], st3[:, :]))
                    vgd(v, v.tensor_mul(st2[:, :], st2[:, :], st1[:, :]))
                    vgd(v, v.tensor_scalar(st2[:, :], st2[:, :], -0.5, 1.5, ALU.mult, ALU.add))
                    vgd(v, v.tensor_mul(rstd_c[:, :], st3[:, :], st2[:, :]))
                    vgd(v, v.tensor_mul(st1[:, :], rstd_c[:, :], rstd_c[:, :]))
                    vgd(v, v.tensor_mul(st1[:, :], st1[:, :], var_c[:, :]))
                    c_vg.inc(v.tensor_scalar(st1[:, :], st1[:, :], EPS32, None, ALU.add))

                @blk.scalar
                def _(s):
                    s = EP(s, c_ss)
                    s.wait_ge(c_vg.sem, c_vg.n)
                    c_sg.inc(s.sqrt(st2[:, :], st1[:, :]))

                @blk.vector
                def _(v):
                    v = EP(v, c_vs)
                    v.wait_ge(c_sg.sem, c_sg.n)
                    vgd(v, v.reciprocal(st3[:, :], st2[:, :]))
                    vgd(v, v.tensor_mul(st2[:, :], st3[:, :], st3[:, :]))
                    vgd(v, v.tensor_mul(st2[:, :], st2[:, :], st1[:, :]))
                    vgd(v, v.tensor_scalar(st2[:, :], st2[:, :], -0.5, 1.5, ALU.mult, ALU.add))
                    vgd(v, v.tensor_mul(st3[:, :], st3[:, :], st2[:, :]))
                    vgd(v, v.tensor_mul(A_c[:, :], rstd_c[:, :], st3[:, :]))
                    vgd(v, v.tensor_scalar(st1[:, :], m_c[:, :], -1.0, None, ALU.mult))
                    vgd(v, v.tensor_mul(nmA_c[:, :], st1[:, :], A_c[:, :]))
                    c_vg.inc(v.tensor_mul(nmrs_c[:, :], st1[:, :], rstd_c[:, :]))

        if DEBUG:
            with nc.Block() as blk:
                @blk.gpsimd
                def _(g):
                    for j, tt in enumerate((sum_c, ssq_c, rstd_c, nmrs_c, A_c, nmA_c)):
                        c_stg.dinc(g.dma_start(dbg_stats[:, j * NT:(j + 1) * NT], tt[:, :]))
                    g.wait_ge(c_stg.sem, c_stg.n)

        with contextlib.ExitStack() as st_ffn:
            act8 = sb("act8", [128, NH * TOK], F8, st_ffn)

            with contextlib.ExitStack() as st_xnq:
                xnqT = sb("xnqT", [128, ND * TOK], F8, st_xnq)

                with contextlib.ExitStack() as st_p2:
                    if PH < 2:
                        return nc
                    xbuf = sb("xbuf2", [128, 2 * D], F32, st_p2)
                    xnf = sb("xnf", [128, D], F32, st_p2)
                    xq8 = sb("xq8", [128, D], F8, st_p2)
                    xqb = sb("xqb", [128, D], BF16, st_p2)
                    with nc.Block() as blk:
                        lx0 = [c.n for c in c_lx]
                        xf0, tin0, tpd0, tev0 = c_xfree.n, c_tin.n, c_tpd.n, c_tev.n
                        with nc.psum_tensor("ptA2", [128, 2048], BF16) as ptA, \
                             nc.psum_tensor("ptB2", [128, 2048], BF16) as ptB:
                            pps = [ptA, ptB]

                            @blk.sync
                            def _(sync):
                                for i in range(NTQ):
                                    if i >= 2:
                                        sync.wait_ge(c_xfree.sem, xf0 + (i - 1))
                                    c_lx[i % 2].dinc(sync.dma_start(xbuf[:, (i % 2) * D:(i % 2 + 1) * D],
                                                                    x_h[i * 128:(i + 1) * 128, :]))

                            @blk.vector
                            def _(v):
                                v = EP(v, c_vs)
                                for i in range(NTQ):
                                    v.wait_ge(c_lx[i % 2].sem, lx0[i % 2] + 16 * (i // 2 + 1))
                                    sl = xbuf[:, (i % 2) * D:(i % 2 + 1) * D]
                                    c_xfree.inc(v.tensor_scalar(xnf[:, :], sl, rstd_c[:, i:i + 1],
                                                                nmrs_c[:, i:i + 1], ALU.mult, ALU.add))
                                    v.wait_ge(c_xfree.sem, c_xfree.n)
                                    vgd(v, v.tensor_scalar(xq8[:, :], xnf[:, :], 1.0, None, ALU.mult))
                                    c_tin.inc(v.tensor_scalar(xqb[:, :], xq8[:, :], 1.0, None, ALU.mult))
                                    v.wait_ge(c_tpd.sem, tpd0 + (i + 1))
                                    pp = pps[i % 2]
                                    ins = None
                                    for d in range(ND):
                                        ins = v.tensor_scalar(xnqT[:, d * TOK + i * 128: d * TOK + (i + 1) * 128],
                                                              pp[:, d * 128:(d + 1) * 128], 1.0, None, ALU.mult)
                                    c_tev.inc(ins)

                            @blk.tensor
                            def _(t):
                                t.wait_ge(c_ps.sem, c_ps.n)
                                for i in range(NTQ):
                                    t.wait_ge(c_tin.sem, tin0 + (i + 1))
                                    if i >= 2:
                                        t.wait_ge(c_tev.sem, tev0 + (i - 1))
                                    pp = pps[i % 2]
                                    ins = None
                                    for d in range(ND):
                                        ins = t.transpose(pp[:, d * 128:(d + 1) * 128],
                                                          xqb[:, d * 128:(d + 1) * 128], ident[:, :])
                                    c_tpd.inc(ins)

                if DEBUG:
                    with nc.Block() as blk:
                        @blk.gpsimd
                        def _(g):
                            g.wait_ge(c_tev.sem, c_tev.n)
                            c_stg.dinc(g.dma_start(dbg_xnqT[:, :], xnqT[:, :]))
                            g.wait_ge(c_stg.sem, c_stg.n)

                with contextlib.ExitStack() as st_g:
                    if PH < 3:
                        return nc
                    wgq0 = sb("wgq0", [128, 2 * 16 * 512], F8, st_g)
                    wgq1 = sb("wgq1", [128, 2 * 16 * 512], F8, st_g)
                    gbuf = sb("gbuf", [128, TOK], BF16, st_g)
                    fbuf = sb("fbuf", [128, TOK], BF16, st_g)
                    tA = sb("tA", [128, TOK], BF16, st_g)
                    tB = sb("tB", [128, TOK], BF16, st_g)
                    tC = sb("tC", [128, TOK], BF16, st_g)
                    with nc.Block() as blk:
                        lw0 = [c.n for c in c_lw]
                        wcv0, mmd0, pev0, sg0, vg0 = (c_wcv.n, c_mmd.n,
                                                      c_pev.n, c_sg.n, c_vg.n)
                        with nc.psum_tensor("pgA", [128, 1024], F32) as pgA, \
                             nc.psum_tensor("pfA", [128, 1024], F32) as pfA, \
                             nc.psum_tensor("pgB", [128, 1024], F32) as pgB, \
                             nc.psum_tensor("pfB", [128, 1024], F32) as pfB:
                            pgs, pfs = [pgA, pgB], [pfA, pfB]

                            @blk.sync
                            def _(sync):
                                for jb in range(NHB):
                                    for gsel in range(2):
                                        for d in range(ND):
                                            k = (jb * 2 + gsel) * ND + d
                                            if k >= 4:
                                                sync.wait_ge(c_wcv.sem, wcv0 + (k - 3))
                                            src = wg_h[gsel, d * 128:(d + 1) * 128,
                                                       jb * 512:(jb + 1) * 512]
                                            c_lw[k % 4].dinc(sync.dma_start(
                                                wstg[:, (k % 4) * 512:(k % 4 + 1) * 512], src))

                            @blk.scalar
                            def _(s):
                                s = EP(s, c_ss)

                                def convs(jb):
                                    for gsel in range(2):
                                        for d in range(ND):
                                            k = (jb * 2 + gsel) * ND + d
                                            s.wait_ge(c_lw[k % 4].sem, lw0[k % 4] + 16 * (k // 4 + 1))
                                            if d == 0 and gsel == 0 and jb >= 2:
                                                s.wait_ge(c_mmd.sem, mmd0 + (jb - 1) * 4)
                                            dstt = wgq0 if gsel == 0 else wgq1
                                            dst = dstt[:, (jb % 2) * 8192 + d * 512:
                                                       (jb % 2) * 8192 + (d + 1) * 512]
                                            c_wcv.inc(s.copy(
                                                dst, wstg[:, (k % 4) * 512:(k % 4 + 1) * 512]))

                                convs(0)
                                convs(1)
                                for jb in range(NHB):
                                    for jj in range(jb * 4, jb * 4 + 4):
                                        s.wait_ge(c_vg.sem, vg0 + jj * 2 + 1)
                                        c_sg.inc(s.activation(tB[:, :], tC[:, :], AF.Tanh))
                                    if jb + 2 < NHB:
                                        convs(jb + 2)

                            @blk.vector
                            def _(v):
                                v = EP(v, c_vs)
                                for jb in range(NHB):
                                    for jj in range(jb * 4, jb * 4 + 4):
                                        v.wait_ge(c_mmd.sem, mmd0 + (jj + 1))
                                        pg, pf = pgs[jj % 2], pfs[jj % 2]
                                        vgd(v, v.tensor_scalar(gbuf[:, :], pg[:, :], 1.0, None, ALU.mult))
                                        c_pev.inc(v.tensor_scalar(fbuf[:, :], pf[:, :], 1.0, None, ALU.mult))
                                        vgd(v, v.tensor_mul(tA[:, :], gbuf[:, :], gbuf[:, :]))
                                        vgd(v, v.tensor_mul(tB[:, :], tA[:, :], gbuf[:, :]))
                                        vgd(v, v.tensor_scalar(tA[:, :], tB[:, :], C3, None, ALU.mult))
                                        vgd(v, v.tensor_add(tB[:, :], gbuf[:, :], tA[:, :]))
                                        c_vg.inc(v.tensor_scalar(tC[:, :], tB[:, :], CS, None, ALU.mult))
                                        v.wait_ge(c_sg.sem, sg0 + jj + 1)
                                        vgd(v, v.tensor_scalar(tC[:, :], tB[:, :], 0.5, 0.5, ALU.mult, ALU.add))
                                        vgd(v, v.tensor_mul(tB[:, :], gbuf[:, :], tC[:, :]))
                                        vgd(v, v.tensor_mul(tA[:, :], tB[:, :], fbuf[:, :]))
                                        c_vg.inc(v.tensor_scalar(act8[:, jj * TOK:(jj + 1) * TOK],
                                                                 tA[:, :], 1.0, None, ALU.mult))

                            @blk.tensor
                            def _(t):
                                t.wait_ge(c_tev.sem, c_tev.n)
                                for jj in range(NH):
                                    jb, sub = jj // 4, jj % 4
                                    t.wait_ge(c_wcv.sem, wcv0 + (jb + 1) * 32)
                                    if jj >= 2:
                                        t.wait_ge(c_pev.sem, pev0 + (jj - 1))
                                    pg, pf = pgs[jj % 2], pfs[jj % 2]
                                    ins = None
                                    for d in range(ND):
                                        off = (jb % 2) * 8192 + d * 512 + sub * 128
                                        l0 = wgq0[:, off:off + 128]
                                        l1 = wgq1[:, off:off + 128]
                                        r = xnqT[:, d * TOK:(d + 1) * TOK]
                                        stt, spp = (d == 0), (d == ND - 1)
                                        t.matmul(pg[:, 0:512], l0, r[:, 0:512], start=stt, stop=spp)
                                        t.matmul(pg[:, 512:1024], l0, r[:, 512:1024], start=stt, stop=spp)
                                        t.matmul(pf[:, 0:512], l1, r[:, 0:512], start=stt, stop=spp)
                                        ins = t.matmul(pf[:, 512:1024], l1, r[:, 512:1024],
                                                       start=stt, stop=spp)
                                    c_mmd.inc(ins)


            with contextlib.ExitStack() as st_d:
                if PH < 4:
                    return nc
                wlstg = sb("wlstg", [128, 2 * 512], F32, st_d)
                wlq = sb("wlq", [128, 2 * NH * 512], F8, st_d)
                fout = sb("fout", [128, 2 * 512], BF16, st_d)
                with nc.Block() as blk:
                    lw0 = [c.n for c in c_lw]
                    wcv0, mmd0, pev0, stg0 = (c_wcv.n, c_mmd.n, c_pev.n, c_stg.n)
                    with nc.psum_tensor("pdA", [128, 512], F32) as pdA, \
                         nc.psum_tensor("pdB", [128, 512], F32) as pdB:
                        pds = [pdA, pdB]

                        @blk.sync
                        def _(sync):
                            for q in range(4):
                                for jj in range(NH):
                                    kq = q * NH + jj
                                    if kq >= 2:
                                        sync.wait_ge(c_wcv.sem, wcv0 + kq - 1)
                                    srcap = wl_h[jj * 128:(jj + 1) * 128,
                                                 q * 512:(q + 1) * 512]
                                    c_lw[kq % 2].dinc(sync.dma_start(
                                        wlstg[:, (kq % 2) * 512:(kq % 2 + 1) * 512], srcap))

                        @blk.scalar
                        def _(s):
                            s = EP(s, c_ss)
                            for q in range(4):
                                for jj in range(NH):
                                    kq = q * NH + jj
                                    s.wait_ge(c_lw[kq % 2].sem, lw0[kq % 2] + 16 * (kq // 2 + 1))
                                    if jj == 0 and q >= 2:
                                        s.wait_ge(c_mmd.sem, mmd0 + (q - 1) * NTQ)
                                    dst = wlq[:, (q % 2) * (NH * 512) + jj * 512:
                                              (q % 2) * (NH * 512) + (jj + 1) * 512]
                                    c_wcv.inc(s.copy(dst,
                                                     wlstg[:, (kq % 2) * 512:(kq % 2 + 1) * 512]))

                        @blk.tensor
                        def _(t):
                            t.wait_ge(c_vg.sem, c_vg.n)
                            for q in range(4):
                                for ti in range(NTQ):
                                    gi = q * NTQ + ti
                                    if gi >= 2:
                                        t.wait_ge(c_pev.sem, pev0 + gi - 1)
                                    pd = pds[gi % 2]
                                    ins = None
                                    for jj in range(NH):
                                        if ti == 0:
                                            t.wait_ge(c_wcv.sem, wcv0 + q * NH + jj + 1)
                                        l = act8[:, jj * TOK + ti * 128: jj * TOK + (ti + 1) * 128]
                                        w = wlq[:, (q % 2) * (NH * 512) + jj * 512:
                                                (q % 2) * (NH * 512) + (jj + 1) * 512]
                                        ins = t.matmul(pd[:, :], l, w,
                                                       start=(jj == 0), stop=(jj == NH - 1))
                                    c_mmd.inc(ins)

                        @blk.vector
                        def _(v):
                            v = EP(v, c_vs)
                            for q in range(4):
                                for ti in range(NTQ):
                                    gi = q * NTQ + ti
                                    v.wait_ge(c_mmd.sem, mmd0 + gi + 1)
                                    if gi >= 2:
                                        v.wait_ge(c_stg.sem, stg0 + 16 * gi)
                                    c_pev.inc(v.tensor_scalar(
                                        fout[:, (gi % 2) * 512:(gi % 2 + 1) * 512],
                                        pds[gi % 2][:, :], 1.0, None, ALU.mult))

                        @blk.gpsimd
                        def _(g):
                            for q in range(4):
                                for ti in range(NTQ):
                                    gi = q * NTQ + ti
                                    g.wait_ge(c_pev.sem, pev0 + gi + 1)
                                    dst = ffn_hbm[ti * 128:(ti + 1) * 128,
                                                  q * 512:(q + 1) * 512]
                                    c_stg.dinc(g.dma_start(
                                        dst, fout[:, (gi % 2) * 512:(gi % 2 + 1) * 512]))
                            g.wait_ge(c_stg.sem, c_stg.n)

        if DEBUG:
            with nc.Block() as blk:
                @blk.gpsimd
                def _(g):
                    c_stg.dinc(g.dma_start(dbg_ffn[:, :], ffn_hbm[:, :]))
                    g.wait_ge(c_stg.sem, c_stg.n)

        with contextlib.ExitStack() as st_p4:
            if PH < 5:
                return nc
            xsT = sb("xsT", [128, ND * T], BF16, st_p4)
            wb = sb("wb", [128, ND * 4 * 512], BF16, st_p4)

            with contextlib.ExitStack() as st_a:
                xbuf = sb("xbuf4", [128, 2 * D], F32, st_a)
                xsb = sb("xsb", [128, D], BF16, st_a)
                with nc.Block() as blk:
                    lx0 = [c.n for c in c_lx]
                    lw0 = [c.n for c in c_lw]
                    xf0, tin0, tpd0, tev0, wcv0 = (
                        c_xfree.n, c_tin.n, c_tpd.n, c_tev.n, c_wcv.n)
                    with nc.psum_tensor("ptA4", [128, 2048], BF16) as ptA, \
                         nc.psum_tensor("ptB4", [128, 2048], BF16) as ptB:
                        pps = [ptA, ptB]

                        @blk.sync
                        def _(sync):
                            for i in range(NT):
                                if i >= 2:
                                    sync.wait_ge(c_xfree.sem, xf0 + (i - 1))
                                c_lx[i % 2].dinc(sync.dma_start(xbuf[:, (i % 2) * D:(i % 2 + 1) * D],
                                                                x_h[i * 128:(i + 1) * 128, :]))
                            for d in range(ND):
                                for cb in range(4):
                                    k = d * 4 + cb
                                    if k >= 4:
                                        sync.wait_ge(c_wcv.sem, wcv0 + (k - 3))
                                    src = wq_h[d * 128:(d + 1) * 128, cb * 512:(cb + 1) * 512]
                                    c_lw[k % 4].dinc(sync.dma_start(
                                        wstg[:, (k % 4) * 512:(k % 4 + 1) * 512], src))

                        @blk.vector
                        def _(v):
                            v = EP(v, c_vs)
                            for i in range(NT):
                                v.wait_ge(c_lx[i % 2].sem, lx0[i % 2] + 16 * (i // 2 + 1))
                                sl = xbuf[:, (i % 2) * D:(i % 2 + 1) * D]
                                c_tin.inc(v.tensor_scalar(xsb[:, :], sl, A_c[:, i:i + 1],
                                                           nmA_c[:, i:i + 1], ALU.mult, ALU.add))
                                c_xfree.inc(v.tensor_scalar(scal1[:1, 2:3], scal1[:1, 3:4], 1.0, None, ALU.mult))
                                v.wait_ge(c_tpd.sem, tpd0 + (i + 1))
                                pp = pps[i % 2]
                                ins = None
                                for d in range(ND):
                                    ins = v.tensor_scalar(xsT[:, d * T + i * 128: d * T + (i + 1) * 128],
                                                          pp[:, d * 128:(d + 1) * 128], 1.0, None, ALU.mult)
                                c_tev.inc(ins)
                            for d in range(ND):
                                for cb in range(4):
                                    k = d * 4 + cb
                                    v.wait_ge(c_lw[k % 4].sem, lw0[k % 4] + 16 * (k // 4 + 1))
                                    dst = wb[:, k * 512:(k + 1) * 512]
                                    c_wcv.inc(v.tensor_scalar(
                                        dst, wstg[:, (k % 4) * 512:(k % 4 + 1) * 512],
                                        gq_c[:, d:d + 1], None, ALU.mult))

                        @blk.tensor
                        def _(t):
                            for i in range(NT):
                                t.wait_ge(c_tin.sem, tin0 + (i + 1))
                                if i >= 2:
                                    t.wait_ge(c_tev.sem, tev0 + (i - 1))
                                pp = pps[i % 2]
                                ins = None
                                for d in range(ND):
                                    ins = t.transpose(pp[:, d * 128:(d + 1) * 128],
                                                      xsb[:, d * 128:(d + 1) * 128], ident[:, :])
                                c_tpd.inc(ins)

            with contextlib.ExitStack() as st_b:
                qf = sb("qf", [128, 2 * D], F32, st_b)
                knstg = sb("knstg", [128, 2 * D], BF16, st_b)
                qbf = sb("qbf", [128, 2 * D], BF16, st_b)
                rsq_t = sb("rsq_t", [128, H], F32, st_b)
                nst1 = sb("nst1", [128, H], F32, st_b)
                nst2 = sb("nst2", [128, H], F32, st_b)

                def proj_pass(n_tiles, out_kind, wcv_base, bias_row, bias_off, out_hbm):
                    with nc.Block() as blk:
                        mmd0, sg0, vg0, tpd0, tev0, pev0, stg0 = (
                            c_mmd.n, c_sg.n, c_vg.n, c_tpd.n, c_tev.n, c_pev.n, c_stg.n)
                        with nc.psum_tensor("pq_" + out_kind, [128, 2048], F32) as pq, \
                             nc.psum_tensor("pt1_" + out_kind, [128, 2048], BF16) as pt1, \
                             nc.psum_tensor("pt2_" + out_kind, [128, 2048], BF16) as pt2:
                            pts = [pt1, pt2]

                            @blk.tensor
                            def _(t):
                                t.wait_ge(c_tev.sem, tev0)

                                def transposes(tj):
                                    t.wait_ge(c_vg.sem, vg0 + 2 * tj + 2)
                                    if tj >= 2:
                                        t.wait_ge(c_tev.sem, tev0 + (tj - 1))
                                    pp = pts[tj % 2]
                                    ins2 = None
                                    for hh in range(H):
                                        ins2 = t.transpose(pp[:, hh * 128:(hh + 1) * 128],
                                                           qbf[:, (tj % 2) * D + hh * 128:
                                                               (tj % 2) * D + (hh + 1) * 128],
                                                           ident[:, :])
                                    c_tpd.inc(ins2)

                                for ti in range(n_tiles):
                                    if ti >= 1:
                                        t.wait_ge(c_pev.sem, pev0 + ti)
                                    for d in range(ND):
                                        if ti == 0:
                                            t.wait_ge(c_wcv.sem, wcv_base + 4 * (d + 1))
                                        l = xsT[:, d * T + ti * 128: d * T + (ti + 1) * 128]
                                        stt = (d == 0)
                                        for cb in range(4):
                                            t.matmul(pq[:, cb * 512:(cb + 1) * 512], l,
                                                     wb[:, (d * 4 + cb) * 512:(d * 4 + cb + 1) * 512],
                                                     start=stt, stop=False)
                                    ins = None
                                    for cb in range(4):
                                        ins = t.matmul(pq[:, cb * 512:(cb + 1) * 512], ones1f[:1, :],
                                                       bias_row[:1, bias_off + cb * 512:
                                                                bias_off + (cb + 1) * 512],
                                                       start=False, stop=True)
                                    c_mmd.inc(ins)
                                    if out_kind != 'v' and ti >= 1:
                                        transposes(ti - 1)
                                if out_kind != 'v':
                                    transposes(n_tiles - 1)

                            if out_kind != 'v':
                                @blk.scalar
                                def _(s):
                                    s = EP(s, c_ss)
                                    for ti in range(n_tiles):
                                        s.wait_ge(c_pev.sem, pev0 + ti + 1)
                                        if ti >= 1:
                                            s.wait_ge(c_vg.sem, vg0 + 2 * (ti - 1) + 1)
                                        c_sg.inc(s.square(wstg[:, :],
                                                          qf[:, (ti % 2) * D:(ti % 2 + 1) * D]))
                                        s.wait_ge(c_vg.sem, vg0 + 2 * ti + 1)
                                        c_sg.inc(s.sqrt(nst2[:, :], nst1[:, :]))

                                @blk.vector
                                def _(v):
                                    v = EP(v, c_vs)
                                    for ti in range(n_tiles):
                                        v.wait_ge(c_mmd.sem, mmd0 + ti + 1)
                                        if ti >= 2:
                                            v.wait_ge(c_sg.sem, sg0 + 2 * (ti - 2) + 1)
                                        qfs = qf[:, (ti % 2) * D:(ti % 2 + 1) * D]
                                        ins = None
                                        for cb in range(4):
                                            ins = v.tensor_scalar(qfs[:, cb * 512:(cb + 1) * 512],
                                                                  pq[:, cb * 512:(cb + 1) * 512],
                                                                  1.0, None, ALU.mult)
                                        c_pev.inc(ins)
                                        v.wait_ge(c_sg.sem, sg0 + 2 * ti + 1)
                                        sq = wstg[:, :]
                                        vgd(v, v.tensor_reduce(rsq_t[:, :],
                                                               sq.rearrange("p (h e) -> p h e", e=128),
                                                               AX.X, ALU.add))
                                        c_vg.inc(v.tensor_scalar(nst1[:, :], rsq_t[:, :], 1.0 / HD,
                                                                 EPS32, ALU.mult, ALU.add))
                                        v.wait_ge(c_sg.sem, sg0 + 2 * ti + 2)
                                        vgd(v, v.reciprocal(rsq_t[:, :], nst2[:, :]))
                                        vgd(v, v.tensor_mul(nst2[:, :], rsq_t[:, :], rsq_t[:, :]))
                                        vgd(v, v.tensor_mul(nst2[:, :], nst2[:, :], nst1[:, :]))
                                        vgd(v, v.tensor_scalar(nst2[:, :], nst2[:, :], -0.5, 1.5,
                                                               ALU.mult, ALU.add))
                                        vgd(v, v.tensor_mul(rsq_t[:, :], rsq_t[:, :], nst2[:, :]))
                                        ins = None
                                        for hh in range(H):
                                            ins = v.tensor_scalar(
                                                qbf[:, (ti % 2) * D + hh * 128:(ti % 2) * D + (hh + 1) * 128],
                                                qfs[:, hh * 128:(hh + 1) * 128],
                                                rsq_t[:, hh:hh + 1], None, ALU.mult)
                                        c_vg.inc(ins)
                                        v.wait_ge(c_tpd.sem, tpd0 + (ti + 1))
                                        if ti >= 2:
                                            v.wait_ge(c_stg.sem, stg0 + 16 * H * ti)
                                        pp = pts[ti % 2]
                                        ins = None
                                        for hh in range(H):
                                            dst = knstg[:, (ti % 2) * D + hh * 128:
                                                        (ti % 2) * D + (hh + 1) * 128]
                                            ins = v.tensor_scalar(dst, pp[:, hh * 128:(hh + 1) * 128],
                                                                  ghs_c[:, 0:1], None, ALU.mult)
                                        c_tev.inc(ins)
                            else:
                                @blk.vector
                                def _(v):
                                    v = EP(v, c_vs)
                                    for ti in range(n_tiles):
                                        v.wait_ge(c_mmd.sem, mmd0 + (ti + 1))
                                        if ti >= 2:
                                            v.wait_ge(c_stg.sem, stg0 + 16 * H * ti)
                                        sl = knstg[:, (ti % 2) * D:(ti % 2 + 1) * D]
                                        _nocp = os.environ.get("KVNOCOPY") == "1"
                                        ins = None
                                        for cb in range(4):
                                            ins = v.tensor_scalar(sl[:, cb * 512:(cb + 1) * 512],
                                                                  pq[:, cb * 512:(cb + 1) * 512],
                                                                  1.0, None, ALU.mult)
                                        c_pev.inc(ins)
                                        c_tev.inc(v.tensor_scalar(scal1[:1, 2:3], scal1[:1, 3:4], 1.0, None, ALU.mult))

                            import os as _os
                            _skipdma = out_kind == 'v' and _os.environ.get("KSKIPV") == "1"

                            @blk.gpsimd
                            def _(g):
                                for ti in range(n_tiles):
                                    g.wait_ge(c_tev.sem, tev0 + (ti + 1))
                                    sl = knstg[:, (ti % 2) * D:(ti % 2 + 1) * D]
                                    if _skipdma:
                                        continue
                                    _kst = os.environ.get("KVMODE", "") == "v2" and out_kind == 'v'
                                    for hh in range(H):
                                        if out_kind == 'v' and not _kst:
                                            dst = out_hbm[hh, ti * 128:(ti + 1) * 128, :]
                                        else:
                                            dst = out_hbm[hh, :, ti * 128:(ti + 1) * 128]
                                        c_stg.dinc(g.dma_start(dst, sl[:, hh * 128:(hh + 1) * 128]))

                if PH < 6:
                    return nc
                proj_pass(NTQ, 'q', c_wcv.n - 64, bq_r, 0, qn_hbm)

                def kv_fold(col_off):
                    with nc.Block() as blk:
                        lw0 = [c.n for c in c_lw]
                        wcv0 = c_wcv.n

                        @blk.sync
                        def _(sync):
                            for d in range(ND):
                                for cb in range(4):
                                    k = d * 4 + cb
                                    if k >= 4:
                                        sync.wait_ge(c_wcv.sem, wcv0 + (k - 3))
                                    src = wkv_h[d * 128:(d + 1) * 128,
                                                col_off + cb * 512:col_off + (cb + 1) * 512]
                                    c_lw[k % 4].dinc(sync.dma_start(
                                        wstg[:, (k % 4) * 512:(k % 4 + 1) * 512], src))

                        @blk.vector
                        def _(v):
                            v = EP(v, c_vs)
                            for d in range(ND):
                                for cb in range(4):
                                    k = d * 4 + cb
                                    v.wait_ge(c_lw[k % 4].sem, lw0[k % 4] + 16 * (k // 4 + 1))
                                    c_wcv.inc(v.tensor_scalar(
                                        wb[:, k * 512:(k + 1) * 512],
                                        wstg[:, (k % 4) * 512:(k % 4 + 1) * 512],
                                        gkv_c[:, d:d + 1], None, ALU.mult))

                if PH < 7:
                    return nc
                kv_fold(0)
                proj_pass(NT, 'k', c_wcv.n - 64, bkv_r, 0, kn_hbm)
                if PH < 8:
                    return nc
                kv_fold(D)
                if PH < 9:
                    return nc
                _kvm = os.environ.get("KVMODE", "")
                _vmode = 'k2' if _kvm == "k" else 'v'
                _vout = v_hbm
                if _kvm == "k":
                    _vout = kn_hbm
                elif _kvm == "v2":
                    _vout = kn_hbm
                proj_pass(int(os.environ.get("KVTILES", str(NT))), _vmode, c_wcv.n - 64, bkv_r, D, _vout)

        if DEBUG:
            with nc.Block() as blk:
                @blk.gpsimd
                def _(g):
                    for hh in range(H):
                        c_stg.dinc(g.dma_start(dbg_qnT[:, hh * TOK:(hh + 1) * TOK], qn_hbm[hh, :, :]))
                        c_stg.dinc(g.dma_start(dbg_knT[:, hh * T:(hh + 1) * T], kn_hbm[hh, :, :]))
                    g.wait_ge(c_stg.sem, c_stg.n)

        with contextlib.ExitStack() as st_wv:
            if PH < 10:
                return nc
            wvT = sb("wvT", [128, H * TOK], BF16, st_wv)
            wob = sb("wob", [128, 4 * H * 512], BF16, st_wv)

            with contextlib.ExitStack() as st_p5:
                expT = sb("expT", [128, NT * TOK], BF16, st_p5)
                knb = sb("knb", [128, 2 * T], BF16, st_p5)
                vhb = sb("vhb", [128, 2 * T], BF16, st_p5)
                qnb = sb("qnb", [128, 2 * TOK], BF16, st_p5)
                zrow = sb("zrow", [1, TOK], F32, st_p5)
                zrow2 = sb("zrow2", [1, TOK], F32, st_p5)
                zb = sb("zb", [128, TOK], F32, st_p5)
                wvf = sb("wvf", [128, TOK], F32, st_p5)
                with nc.Block() as blk:
                    lg0 = [c.n for c in c_lg]
                    lw0 = [c.n for c in c_lw]
                    mmd0, zw0, sg0, zr0, bc0, wvn0, pev0, wcv0 = (
                        c_mmd.n, c_zw.n, c_sg.n, c_zr.n, c_bc.n, c_wvn.n, c_pev.n, c_wcv.n)
                    with nc.psum_tensor("psA5", [128, 1024], F32) as psA, \
                         nc.psum_tensor("psB5", [128, 1024], F32) as psB, \
                         nc.psum_tensor("pwv", [128, 1024], F32) as pwv, \
                         nc.psum_tensor("pzA", [1, 512], F32) as pzA, \
                         nc.psum_tensor("pzB", [1, 512], F32) as pzB:
                        pss = [psA, psB]

                        @blk.sync
                        def _(sync):
                            for cb in range(4):
                                for hh in range(H):
                                    k = cb * H + hh
                                    if k >= 4:
                                        sync.wait_ge(c_wcv.sem, wcv0 + (k - 3))
                                    srcw = wo_h[hh * 128:(hh + 1) * 128, cb * 512:(cb + 1) * 512]
                                    c_lw[k % 4].dinc(sync.dma_start(
                                        wstg[:, (k % 4) * 512:(k % 4 + 1) * 512], srcw))

                        @blk.gpsimd
                        def _(g):
                            g.wait_ge(c_stg.sem, c_stg.n)
                            for h in range(H):
                                if h >= 2:
                                    g.wait_ge(c_wvn.sem, wvn0 + (h - 1))
                                c_lg[h % 2].dinc(g.dma_start(knb[:, (h % 2) * T:(h % 2 + 1) * T],
                                                             kn_hbm[h, :, :]))
                                c_lg[h % 2].dinc(g.dma_start(qnb[:, (h % 2) * TOK:(h % 2 + 1) * TOK],
                                                             qn_hbm[h, :, :]))
                                c_lg[h % 2].dinc(g.dma_start(
                                    vhb[:, (h % 2) * T:(h % 2 + 1) * T].rearrange(
                                        "p (c v) -> p c v", v=128),
                                    v_hbm[h].rearrange("(c p) v -> p c v", p=128)))

                        def sc_wait(t, gi):
                            if gi >= 2:
                                t.wait_ge(c_sg.sem, sg0 + gi - 1)

                        @blk.tensor
                        def _(t):
                            for h in range(H):
                                t.wait_ge(c_lg[h % 2].sem, lg0[h % 2] + 48 * (h // 2 + 1))
                                gi0 = h * NT
                                sc_wait(t, gi0)
                                l = knb[:, (h % 2) * T + 0:(h % 2) * T + 128]
                                q0 = qnb[:, (h % 2) * TOK:(h % 2) * TOK + 512]
                                q1 = qnb[:, (h % 2) * TOK + 512:(h % 2 + 1) * TOK]
                                ps = pss[gi0 % 2]
                                t.matmul(ps[:, 0:512], l, q0, start=True, stop=True)
                                c_mmd.inc(t.matmul(ps[:, 512:1024], l, q1, start=True, stop=True))
                                if h >= 1:
                                    t.wait_ge(c_pev.sem, pev0 + h)
                                for i in range(NT):
                                    gi = h * NT + i
                                    if i + 1 < NT:
                                        gi2 = gi + 1
                                        sc_wait(t, gi2)
                                        if i + 1 == 1 and h >= 1:
                                            t.wait_ge(c_wvn.sem, wvn0 + h)
                                        l2 = knb[:, (h % 2) * T + (i + 1) * 128:
                                                 (h % 2) * T + (i + 2) * 128]
                                        ps2 = pss[gi2 % 2]
                                        t.matmul(ps2[:, 0:512], l2, q0, start=True, stop=True)
                                        c_mmd.inc(t.matmul(ps2[:, 512:1024], l2, q1,
                                                           start=True, stop=True))
                                    t.wait_ge(c_sg.sem, sg0 + gi + 1)
                                    e = expT[:, i * TOK:(i + 1) * TOK]
                                    stt, spp = (i == 0), (i == NT - 1)
                                    t.matmul(pzA[:1, :], onescol_b[:, 0:1], e[:, 0:512],
                                             start=stt, stop=spp)
                                    t.matmul(pzB[:1, :], onescol_b[:, 0:1], e[:, 512:1024],
                                             start=stt, stop=spp)
                                    vl = vhb[:, (h % 2) * T + i * 128:(h % 2) * T + (i + 1) * 128]
                                    t.matmul(pwv[:, 0:512], vl, e[:, 0:512], start=stt, stop=spp)
                                    c_zw.inc(t.matmul(pwv[:, 512:1024], vl, e[:, 512:1024],
                                                      start=stt, stop=spp))
                                t.wait_ge(c_zr.sem, zr0 + (h + 1))
                                t.matmul(pss[1][:, 0:512], ones1f[:1, :], zrow[:1, 0:512],
                                         start=True, stop=True)
                                c_bc.inc(t.matmul(pss[1][:, 512:1024], ones1f[:1, :],
                                                  zrow[:1, 512:1024], start=True, stop=True))

                        @blk.scalar
                        def _(s):
                            s = EP(s, c_ss)
                            s.wait_ge(c_vg.sem, c_vg.n)
                            for h in range(H):
                                for kk in range(4 * h, 4 * h + 4):
                                    s.wait_ge(c_lw[kk % 4].sem, lw0[kk % 4] + 16 * (kk // 4 + 1))
                                    c_wcv.inc(s.copy(wob[:, kk * 512:(kk + 1) * 512],
                                                     wstg[:, (kk % 4) * 512:(kk % 4 + 1) * 512]))
                                for i in range(NT):
                                    gi = h * NT + i
                                    s.wait_ge(c_mmd.sem, mmd0 + gi + 1)
                                    ps = pss[gi % 2]
                                    e = expT[:, i * TOK:(i + 1) * TOK]
                                    c_sg.inc(s.activation(e[:, :], ps[:, :], AF.Exp,
                                                          bias=negM_c[:, 0:1], scale=1.0))

                        @blk.vector
                        def _(v):
                            v = EP(v, c_vs)
                            for h in range(H):
                                v.wait_ge(c_zw.sem, zw0 + (h + 1) * NT)
                                wvs = wvf[:, :]
                                v.tensor_scalar(wvs[:, 0:512], pwv[:, 0:512], 1.0, None, ALU.mult)
                                v.tensor_scalar(wvs[:, 512:1024], pwv[:, 512:1024], 1.0, None, ALU.mult)
                                v.tensor_scalar(zrow2[:1, 0:512], pzA[:1, :], 1.0, None, ALU.mult)
                                c_pev.inc(v.tensor_scalar(zrow2[:1, 512:1024], pzB[:1, :],
                                                          1.0, None, ALU.mult))
                                v.wait_ge(c_vs.sem, c_vs.n)
                                c_zr.inc(v.reciprocal(zrow[:1, :], zrow2[:1, :]))
                                v.wait_ge(c_bc.sem, bc0 + (h + 1))
                                vgd(v, v.tensor_scalar(zb[:, :], pss[1][:, :], 1.0, None, ALU.mult))
                                c_wvn.inc(v.tensor_mul(wvT[:, h * TOK:(h + 1) * TOK],
                                                       wvs[:, :], zb[:, :]))

            if DEBUG:
                with nc.Block() as blk:
                    @blk.gpsimd
                    def _(g):
                        c_stg.dinc(g.dma_start(dbg_wvT[:, :], wvT[:, :]))
                        g.wait_ge(c_stg.sem, c_stg.n)

            with contextlib.ExitStack() as st_p6:
                if PH < 11:
                    return nc
                xrb = sb("xrb", [128, 2 * D], F32, st_p6)
                frb = sb("frb", [128, 2 * D], BF16, st_p6)
                yout = sb("yout", [128, 2 * D], F32, st_p6)
                with nc.Block() as blk:
                    lx0 = [c.n for c in c_lx]
                    lw0 = [c.n for c in c_lw]
                    lg0 = [c.n for c in c_lg]
                    wcv0, mmd0, pev0, stg0, xf0, vg0 = (
                        c_wcv.n, c_mmd.n, c_pev.n,
                        c_stg.n, c_xfree.n, c_vg.n)
                    with nc.psum_tensor("poA", [128, 512], F32) as poA, \
                         nc.psum_tensor("poB", [128, 512], F32) as poB:
                        pos = [poA, poB]

                        @blk.sync
                        def _(sync):
                            for ti in range(NTQ):
                                if ti >= 2:
                                    sync.wait_ge(c_xfree.sem, xf0 + (ti - 1))
                                c_lx[ti % 2].dinc(sync.dma_start(xrb[:, (ti % 2) * D:(ti % 2 + 1) * D],
                                                                 x_h[ti * 128:(ti + 1) * 128, :]))

                        @blk.gpsimd
                        def _(g):
                            for ti in range(NTQ + 2):
                                if 0 <= ti - 2 < NTQ:
                                    tj = ti - 2
                                    g.wait_ge(c_pev.sem, pev0 + (tj + 1) * 4)
                                    c_stg.dinc(g.dma_start(y_h[tj * 128:(tj + 1) * 128, :],
                                                           yout[:, (tj % 2) * D:(tj % 2 + 1) * D]))
                                if ti < NTQ:
                                    if ti >= 2:
                                        g.wait_ge(c_vg.sem, vg0 + (ti - 1))
                                    c_lg[ti % 2].dinc(g.dma_start(frb[:, (ti % 2) * D:(ti % 2 + 1) * D],
                                                                  ffn_hbm[ti * 128:(ti + 1) * 128, :]))
                            g.wait_ge(c_stg.sem, c_stg.n)

                        @blk.tensor
                        def _(t):
                            t.wait_ge(c_wvn.sem, c_wvn.n)
                            t.wait_ge(c_wcv.sem, wcv0)
                            for ti in range(NTQ):
                                for cb in range(4):
                                    gi = ti * 4 + cb
                                    if gi >= 2:
                                        t.wait_ge(c_pev.sem, pev0 + (gi - 1))
                                    po = pos[gi % 2]
                                    for hh in range(H):
                                        l = wvT[:, hh * TOK + ti * 128: hh * TOK + (ti + 1) * 128]
                                        t.matmul(po[:, :], l,
                                                 wob[:, (cb * H + hh) * 512:(cb * H + hh + 1) * 512],
                                                 start=(hh == 0), stop=False)
                                    c_mmd.inc(t.matmul(po[:, :], ones1f[:1, :],
                                                       bo_r[:1, cb * 512:(cb + 1) * 512],
                                                       start=False, stop=True))

                        @blk.vector
                        def _(v):
                            v = EP(v, c_vs)
                            for ti in range(NTQ):
                                v.wait_ge(c_lx[ti % 2].sem, lx0[ti % 2] + 16 * (ti // 2 + 1))
                                v.wait_ge(c_lg[ti % 2].sem, lg0[ti % 2] + 16 * (ti // 2 + 1))
                                if ti >= 2:
                                    v.wait_ge(c_stg.sem, stg0 + 16 * ti)
                                for cb in range(4):
                                    gi = ti * 4 + cb
                                    v.wait_ge(c_mmd.sem, mmd0 + (gi + 1))
                                    po = pos[gi % 2]
                                    lo = (ti % 2) * D + cb * 512
                                    yl = yout[:, lo:lo + 512]
                                    vgd(v, v.tensor_add(yl, po[:, :], xrb[:, lo:lo + 512]))
                                    c_pev.inc(v.tensor_add(yl, yl, frb[:, lo:lo + 512]))
                                c_xfree.inc(v.tensor_scalar(scal1[:1, 2:3], scal1[:1, 3:4], 1.0, None, ALU.mult))
                                c_vg.inc(v.tensor_scalar(scal1[:1, 4:5], scal1[:1, 5:6], 1.0, None, ALU.mult))

    return nc


_NC_CACHE = {}


def _get_nc():
    key = ("nc", DEBUG)
    if key not in _NC_CACHE:
        _NC_CACHE[key] = build_nc()
    return _NC_CACHE[key]


def kernel(L0_attn_x, L0_attn_g_q, L0_attn_Wq, L0_attn_bq, L0_attn_g_kv,
           L0_attn_Wkv, L0_attn_bkv, L0_attn_g_head, L0_attn_Wo, L0_attn_bo,
           L0_ffn_w_gating, L0_ffn_w_linear):
    x = np.ascontiguousarray(np.asarray(L0_attn_x, dtype=np.float32))
    common = {
        "g_q": np.ascontiguousarray(L0_attn_g_q, dtype=np.float32),
        "Wq": np.ascontiguousarray(L0_attn_Wq, dtype=np.float32),
        "bq": np.ascontiguousarray(L0_attn_bq, dtype=np.float32),
        "g_kv": np.ascontiguousarray(L0_attn_g_kv, dtype=np.float32),
        "Wkv": np.ascontiguousarray(L0_attn_Wkv, dtype=np.float32),
        "bkv": np.ascontiguousarray(L0_attn_bkv, dtype=np.float32),
        "g_head": np.ascontiguousarray(L0_attn_g_head, dtype=np.float32),
        "Wo": np.ascontiguousarray(L0_attn_Wo, dtype=np.float32),
        "bo": np.ascontiguousarray(L0_attn_bo, dtype=np.float32),
        "w_gating": np.ascontiguousarray(L0_ffn_w_gating, dtype=np.float32),
        "w_linear": np.ascontiguousarray(L0_ffn_w_linear, dtype=np.float32),
    }
    in_maps = []
    for c in range(8):
        b, h = c // 2, c % 2
        xb = x[b]
        xc = np.concatenate([xb[h * TOK:(h + 1) * TOK], xb[(1 - h) * TOK:(2 - h) * TOK]], axis=0)
        m = dict(common)
        m["x"] = np.ascontiguousarray(xc)
        in_maps.append(m)
    nc = _get_nc()
    res = run_bass_kernel_spmd(nc, in_maps, core_ids=list(range(8)))
    out = np.empty((B, T, D), dtype=np.float32)
    for c in range(8):
        b, h = c // 2, c % 2
        out[b, h * TOK:(h + 1) * TOK] = res.results[c]["y"]
    return out


# version 08
# speedup vs baseline: 1.0261x; 1.0040x over previous
import contextlib
import os

import numpy as np
import ml_dtypes

import concourse.bass as bass
import concourse.mybir as mybir
from concourse.bass_utils import run_bass_kernel_spmd

BFD = ml_dtypes.bfloat16
F32 = mybir.dt.float32
BF16 = mybir.dt.bfloat16
F8 = mybir.dt.float8e4
AF = mybir.ActivationFunctionType
ALU = mybir.AluOpType
AX = mybir.AxisListType

B, T, D = 4, 2048, 2048
H, HD = 16, 128
HID = 8192
TOK = 1024
NT = T // 128
NTQ = TOK // 128
ND = D // 128
NH = HID // 128
NHB = 16
EPS32 = float(np.finfo(np.float32).eps)
C3 = float(np.float32(BFD(0.044715)))
CS = float(np.float32(BFD(np.sqrt(2.0 / np.pi))))
S4 = float(128.0 ** -0.25)
SQRT128 = float(np.sqrt(128.0))

DEBUG = False
PH = int(os.environ.get("KPH", "99"))


_INC_INFO = {}


class Ctr:
    def __init__(self, sem):
        self.sem = sem
        self.n = 0

    def inc(self, instr, k=1):
        instr.then_inc(self.sem, k)
        self.n += k
        _INC_INFO[id(instr)] = (self.sem, self.n, instr)
        return self.n

    def dinc(self, instr):
        return self.inc(instr, 16)


_EP_OPS = frozenset((
    "tensor_scalar", "tensor_tensor", "tensor_add", "tensor_sub", "tensor_mul",
    "tensor_max", "tensor_copy", "tensor_reduce", "reduce_sum", "reduce_max",
    "reciprocal", "scalar_tensor_tensor", "memset", "activation", "square",
    "sqrt", "mul", "add", "copy", "sign",
))


class EP:

    def __init__(self, eng, guard):
        self._e = eng
        self._g = guard
        self._pend = {}
        self._mark = {}

    def wait_ge(self, sem, n):
        self._mark[id(sem)] = max(self._mark.get(id(sem), -1), n)
        return self._e.wait_ge(sem, n)

    def _require(self, instr):
        info = _INC_INFO.get(id(instr))
        if info is None or info[2] is not instr:
            self._g.inc(instr)
            info = _INC_INFO[id(instr)]
        sem, n, _ = info
        if self._mark.get(id(sem), -1) < n:
            self._mark[id(sem)] = n
            self._e.wait_ge(sem, n)

    def __getattr__(self, name):
        f = getattr(self._e, name)
        if name not in _EP_OPS:
            return f

        def g(*args, **kw):
            aps = [a for a in args if isinstance(a, bass.AP)]
            aps += [a for a in kw.values() if isinstance(a, bass.AP)]
            if not aps:
                return f(*args, **kw)
            outn = aps[0].tensor.name
            for a in aps[1:]:
                w = self._pend.get(a.tensor.name)
                if w is not None:
                    self._require(w)
            w = self._pend.get(outn)
            if w is not None:
                self._require(w)
            instr = f(*args, **kw)
            self._pend[outn] = instr
            ac = kw.get("accum_out")
            if isinstance(ac, bass.AP):
                self._pend[ac.tensor.name] = instr
            return instr

        return g


def build_nc():
    nc = bass.Bass("TRN2", target_bir_lowering=False)

    x_h = nc.dram_tensor("x", [T, D], F32, kind="ExternalInput")
    gq_h = nc.dram_tensor("g_q", [D], F32, kind="ExternalInput")
    wq_h = nc.dram_tensor("Wq", [D, D], F32, kind="ExternalInput")
    bq_h = nc.dram_tensor("bq", [D], F32, kind="ExternalInput")
    gkv_h = nc.dram_tensor("g_kv", [D], F32, kind="ExternalInput")
    wkv_h = nc.dram_tensor("Wkv", [D, 2 * D], F32, kind="ExternalInput")
    bkv_h = nc.dram_tensor("bkv", [2 * D], F32, kind="ExternalInput")
    gh_h = nc.dram_tensor("g_head", [HD], F32, kind="ExternalInput")
    wo_h = nc.dram_tensor("Wo", [D, D], F32, kind="ExternalInput")
    bo_h = nc.dram_tensor("bo", [D], F32, kind="ExternalInput")
    wg_h = nc.dram_tensor("w_gating", [2, D, HID], F32, kind="ExternalInput")
    wl_h = nc.dram_tensor("w_linear", [HID, D], F32, kind="ExternalInput")
    y_h = nc.dram_tensor("y", [TOK, D], F32, kind="ExternalOutput")
    v_hbm = nc.dram_tensor("v_scr", [H, T, HD], BF16, kind="Internal")
    kn_hbm = nc.dram_tensor("kn_scr", [H, HD, T], BF16, kind="Internal")
    qn_hbm = nc.dram_tensor("qn_scr", [H, HD, TOK], BF16, kind="Internal")
    ffn_hbm = nc.dram_tensor("ffn_scr", [TOK, D], BF16, kind="Internal")
    if DEBUG:
        dbg_xnqT = nc.dram_tensor("dbg_xnqT", [128, ND * TOK], mybir.dt.uint8, kind="ExternalOutput")
        dbg_stats = nc.dram_tensor("dbg_stats", [128, 6 * NT], F32, kind="ExternalOutput")
        dbg_ffn = nc.dram_tensor("dbg_ffn", [TOK, D], BF16, kind="ExternalOutput")
        dbg_qnT = nc.dram_tensor("dbg_qnT", [128, H * TOK], BF16, kind="ExternalOutput")
        dbg_knT = nc.dram_tensor("dbg_knT", [128, H * T], BF16, kind="ExternalOutput")
        dbg_wvT = nc.dram_tensor("dbg_wvT", [128, H * TOK], BF16, kind="ExternalOutput")

    es = contextlib.ExitStack()
    with es:
        def sb(name, shape, dt, st=None):
            return (st or es).enter_context(nc.sbuf_tensor(name, shape, dt))

        def sem(name):
            return Ctr(es.enter_context(nc.semaphore(name)))

        c_lx = [sem("lx0"), sem("lx1")]
        c_lw = [sem("lw0"), sem("lw1"), sem("lw2"), sem("lw3")]
        c_ldx = sem("ldx")
        c_ldw = sem("ldw")
        c_lds = sem("lds")
        c_ldg = sem("ldg")
        c_lg = [sem("lg0"), sem("lg1")]
        c_stg = sem("stg")
        c_xfree = sem("xfree")
        c_tin = sem("tin")
        c_tpd = sem("tpd")
        c_tev = sem("tev")
        c_wcv = sem("wcv")
        c_mmd = sem("mmd")
        c_zw = sem("zw")
        c_pev = sem("pev")
        c_sg = sem("sg")
        c_vg = sem("vg")
        c_zr = sem("zr")
        c_bc = sem("bc")
        c_wvn = sem("wvn")
        c_vs = sem("vs")
        c_ss = sem("ss")
        c_ps = sem("ps")

        def vgd(eng, ins):
            c_vs.inc(ins)
            eng.wait_ge(c_vs.sem, c_vs.n)
            return ins

        ident = sb("ident", [128, 128], BF16)
        ones1f = sb("ones1f", [1, 128], F32)
        onescol_b = sb("onescol_b", [128, 1], BF16)
        ones1b = sb("ones1b", [1, 128], BF16)
        gq_c = sb("gq_c", [128, ND], F32)
        gkv_c = sb("gkv_c", [128, ND], F32)
        gh_c = sb("gh_c", [128, 1], F32)
        ghs_c = sb("ghs_c", [128, 1], F32)
        gh_r = sb("gh_r", [1, 128], F32)
        bq_r = sb("bq_r", [1, D], F32)
        bkv_r = sb("bkv_r", [1, 2 * D], F32)
        bo_r = sb("bo_r", [1, D], F32)
        negM_c = sb("negM_c", [128, 1], F32)
        scal1 = sb("scal1", [1, 128], F32)
        sum_c = sb("sum_c", [128, NT], F32)
        ssq_c = sb("ssq_c", [128, NT], F32)
        m_c = sb("m_c", [128, NT], F32)
        var_c = sb("var_c", [128, NT], F32)
        rstd_c = sb("rstd_c", [128, NT], F32)
        nmrs_c = sb("nmrs_c", [128, NT], F32)
        A_c = sb("A_c", [128, NT], F32)
        nmA_c = sb("nmA_c", [128, NT], F32)
        st1 = sb("st1", [128, NT], F32)
        st2 = sb("st2", [128, NT], F32)
        st3 = sb("st3", [128, NT], F32)
        wstg = sb("wstg", [128, 4 * 512], F32)

        with nc.Block() as blk:
            @blk.sync
            def _(sync):
                for dst, src in (
                    (gq_c[:, :], gq_h.ap().rearrange("(c p) -> p c", p=128)),
                    (gkv_c[:, :], gkv_h.ap().rearrange("(c p) -> p c", p=128)),
                    (gh_c[:, :], gh_h.ap().rearrange("(p one) -> p one", one=1)),
                    (gh_r[:, :], gh_h.ap().rearrange("(one p) -> one p", one=1)),
                    (bq_r[:, :], bq_h.ap().rearrange("(one d) -> one d", one=1)),
                    (bkv_r[:, :], bkv_h.ap().rearrange("(one d) -> one d", one=1)),
                    (bo_r[:, :], bo_h.ap().rearrange("(one d) -> one d", one=1)),
                ):
                    c_lds.dinc(sync.dma_start(dst, src, allow_slow_non_contiguous=True))

            @blk.gpsimd
            def _(g):
                g.memset(ident[:, :], 0.0).then_inc(c_ps.sem, 1)
                c_ps.n += 1
                g.memset(ones1f[:, :], 1.0)
                g.memset(onescol_b[:, :], 1.0)
                g.memset(ones1b[:, :], 1.0)
                g.wait_ge(c_ps.sem, c_ps.n)
                c_ps.inc(g.affine_select(out=ident[:, :], in_=ident[:, :],
                                         compare_op=ALU.not_equal, fill=1.0, base=0,
                                         pattern=[[-1, 128]], channel_multiplier=1))

            @blk.vector
            def _(v):
                v = EP(v, c_vs)
                v.wait_ge(c_lds.sem, c_lds.n)
                v.tensor_scalar(ghs_c[:, :], gh_c[:, :], S4, None, ALU.mult)
                vgd(v, v.tensor_mul(scal1[:1, :], gh_r[:1, :], gh_r[:1, :]))
                vgd(v, v.reduce_max(scal1[:1, :1], scal1[:1, :], AX.X))
                c_vg.inc(v.tensor_scalar(scal1[:1, :1], scal1[:1, :1], -SQRT128, None, ALU.mult))

        with nc.Block() as blk:
            with nc.psum_tensor("p0b", [128, 512], F32) as p0b:
                @blk.tensor
                def _(t):
                    t.wait_ge(c_vg.sem, c_vg.n)
                    c_mmd.inc(t.matmul(p0b[:, 0:1], ones1f[:1, :], scal1[:1, :1],
                                       start=True, stop=True))

                @blk.vector
                def _(v):
                    v = EP(v, c_vs)
                    v.wait_ge(c_mmd.sem, c_mmd.n)
                    c_vg.inc(v.tensor_scalar(negM_c[:, :], p0b[:, 0:1], 1.0, None, ALU.mult))

        with contextlib.ExitStack() as st_p1:
            if PH < 1:
                return nc
            xbuf = sb("xbuf1", [128, 2 * D], F32, st_p1)
            sqscr = sb("sqscr1", [128, 2 * D], F32, st_p1)
            with nc.Block() as blk:
                lx0 = [c.n for c in c_lx]
                xf0, sg0, vg0 = c_xfree.n, c_sg.n, c_vg.n

                @blk.sync
                def _(sync):
                    for i in range(NT):
                        if i >= 2:
                            sync.wait_ge(c_xfree.sem, xf0 + (i - 1))
                        c_lx[i % 2].dinc(sync.dma_start(xbuf[:, (i % 2) * D:(i % 2 + 1) * D],
                                                        x_h[i * 128:(i + 1) * 128, :]))

                @blk.scalar
                def _(s):
                    s = EP(s, c_ss)
                    for i in range(NT):
                        s.wait_ge(c_lx[i % 2].sem, lx0[i % 2] + 16 * (i // 2 + 1))
                        if i >= 2:
                            s.wait_ge(c_xfree.sem, xf0 + (i - 1))
                        c_sg.inc(s.square(sqscr[:, (i % 2) * D:(i % 2 + 1) * D],
                                          xbuf[:, (i % 2) * D:(i % 2 + 1) * D]))

                @blk.vector
                def _(v):
                    v = EP(v, c_vs)
                    for i in range(NT):
                        v.wait_ge(c_sg.sem, sg0 + (i + 1))
                        v.reduce_sum(sum_c[:, i:i + 1], xbuf[:, (i % 2) * D:(i % 2 + 1) * D], AX.X)
                        c_xfree.inc(v.reduce_sum(ssq_c[:, i:i + 1],
                                                 sqscr[:, (i % 2) * D:(i % 2 + 1) * D], AX.X))
                    vgd(v, v.tensor_scalar(m_c[:, :], sum_c[:, :], 1.0 / D, None, ALU.mult))
                    vgd(v, v.tensor_scalar(ssq_c[:, :], ssq_c[:, :], 1.0 / D, None, ALU.mult))
                    vgd(v, v.tensor_mul(st1[:, :], m_c[:, :], m_c[:, :]))
                    vgd(v, v.tensor_sub(var_c[:, :], ssq_c[:, :], st1[:, :]))
                    c_vg.inc(v.tensor_scalar(st1[:, :], var_c[:, :], 1e-5, None, ALU.add))

                @blk.scalar
                def _(s):
                    s = EP(s, c_ss)
                    s.wait_ge(c_vg.sem, vg0 + 1)
                    c_sg.inc(s.sqrt(st2[:, :], st1[:, :]))

                @blk.vector
                def _(v):
                    v = EP(v, c_vs)
                    v.wait_ge(c_sg.sem, c_sg.n)
                    vgd(v, v.reciprocal(st3[:, :], st2[:, :]))
                    vgd(v, v.tensor_mul(st2[:, :], st3[:, :], st3[:, :]))
                    vgd(v, v.tensor_mul(st2[:, :], st2[:, :], st1[:, :]))
                    vgd(v, v.tensor_scalar(st2[:, :], st2[:, :], -0.5, 1.5, ALU.mult, ALU.add))
                    vgd(v, v.tensor_mul(rstd_c[:, :], st3[:, :], st2[:, :]))
                    vgd(v, v.tensor_mul(st1[:, :], rstd_c[:, :], rstd_c[:, :]))
                    vgd(v, v.tensor_mul(st1[:, :], st1[:, :], var_c[:, :]))
                    c_vg.inc(v.tensor_scalar(st1[:, :], st1[:, :], EPS32, None, ALU.add))

                @blk.scalar
                def _(s):
                    s = EP(s, c_ss)
                    s.wait_ge(c_vg.sem, c_vg.n)
                    c_sg.inc(s.sqrt(st2[:, :], st1[:, :]))

                @blk.vector
                def _(v):
                    v = EP(v, c_vs)
                    v.wait_ge(c_sg.sem, c_sg.n)
                    vgd(v, v.reciprocal(st3[:, :], st2[:, :]))
                    vgd(v, v.tensor_mul(st2[:, :], st3[:, :], st3[:, :]))
                    vgd(v, v.tensor_mul(st2[:, :], st2[:, :], st1[:, :]))
                    vgd(v, v.tensor_scalar(st2[:, :], st2[:, :], -0.5, 1.5, ALU.mult, ALU.add))
                    vgd(v, v.tensor_mul(st3[:, :], st3[:, :], st2[:, :]))
                    vgd(v, v.tensor_mul(A_c[:, :], rstd_c[:, :], st3[:, :]))
                    vgd(v, v.tensor_scalar(st1[:, :], m_c[:, :], -1.0, None, ALU.mult))
                    vgd(v, v.tensor_mul(nmA_c[:, :], st1[:, :], A_c[:, :]))
                    c_vg.inc(v.tensor_mul(nmrs_c[:, :], st1[:, :], rstd_c[:, :]))

        if DEBUG:
            with nc.Block() as blk:
                @blk.gpsimd
                def _(g):
                    for j, tt in enumerate((sum_c, ssq_c, rstd_c, nmrs_c, A_c, nmA_c)):
                        c_stg.dinc(g.dma_start(dbg_stats[:, j * NT:(j + 1) * NT], tt[:, :]))
                    g.wait_ge(c_stg.sem, c_stg.n)

        with contextlib.ExitStack() as st_ffn:
            act8 = sb("act8", [128, NH * TOK], F8, st_ffn)

            with contextlib.ExitStack() as st_xnq:
                xnqT = sb("xnqT", [128, ND * TOK], F8, st_xnq)

                with contextlib.ExitStack() as st_p2:
                    if PH < 2:
                        return nc
                    xbuf = sb("xbuf2", [128, 2 * D], F32, st_p2)
                    xnf = sb("xnf", [128, D], F32, st_p2)
                    xq8 = sb("xq8", [128, D], F8, st_p2)
                    xqb = sb("xqb", [128, D], BF16, st_p2)
                    with nc.Block() as blk:
                        lx0 = [c.n for c in c_lx]
                        xf0, tin0, tpd0, tev0 = c_xfree.n, c_tin.n, c_tpd.n, c_tev.n
                        with nc.psum_tensor("ptA2", [128, 2048], BF16) as ptA, \
                             nc.psum_tensor("ptB2", [128, 2048], BF16) as ptB:
                            pps = [ptA, ptB]

                            @blk.sync
                            def _(sync):
                                for i in range(NTQ):
                                    if i >= 2:
                                        sync.wait_ge(c_xfree.sem, xf0 + (i - 1))
                                    c_lx[i % 2].dinc(sync.dma_start(xbuf[:, (i % 2) * D:(i % 2 + 1) * D],
                                                                    x_h[i * 128:(i + 1) * 128, :]))

                            @blk.vector
                            def _(v):
                                v = EP(v, c_vs)
                                for i in range(NTQ):
                                    v.wait_ge(c_lx[i % 2].sem, lx0[i % 2] + 16 * (i // 2 + 1))
                                    sl = xbuf[:, (i % 2) * D:(i % 2 + 1) * D]
                                    c_xfree.inc(v.tensor_scalar(xnf[:, :], sl, rstd_c[:, i:i + 1],
                                                                nmrs_c[:, i:i + 1], ALU.mult, ALU.add))
                                    v.wait_ge(c_xfree.sem, c_xfree.n)
                                    vgd(v, v.tensor_scalar(xq8[:, :], xnf[:, :], 1.0, None, ALU.mult))
                                    c_tin.inc(v.tensor_scalar(xqb[:, :], xq8[:, :], 1.0, None, ALU.mult))
                                    v.wait_ge(c_tpd.sem, tpd0 + (i + 1))
                                    pp = pps[i % 2]
                                    ins = None
                                    for d in range(ND):
                                        ins = v.tensor_scalar(xnqT[:, d * TOK + i * 128: d * TOK + (i + 1) * 128],
                                                              pp[:, d * 128:(d + 1) * 128], 1.0, None, ALU.mult)
                                    c_tev.inc(ins)

                            @blk.tensor
                            def _(t):
                                t.wait_ge(c_ps.sem, c_ps.n)
                                for i in range(NTQ):
                                    t.wait_ge(c_tin.sem, tin0 + (i + 1))
                                    if i >= 2:
                                        t.wait_ge(c_tev.sem, tev0 + (i - 1))
                                    pp = pps[i % 2]
                                    ins = None
                                    for d in range(ND):
                                        ins = t.transpose(pp[:, d * 128:(d + 1) * 128],
                                                          xqb[:, d * 128:(d + 1) * 128], ident[:, :])
                                    c_tpd.inc(ins)

                if DEBUG:
                    with nc.Block() as blk:
                        @blk.gpsimd
                        def _(g):
                            g.wait_ge(c_tev.sem, c_tev.n)
                            c_stg.dinc(g.dma_start(dbg_xnqT[:, :], xnqT[:, :]))
                            g.wait_ge(c_stg.sem, c_stg.n)

                with contextlib.ExitStack() as st_g:
                    if PH < 3:
                        return nc
                    wgq0 = sb("wgq0", [128, 2 * 16 * 512], F8, st_g)
                    wgq1 = sb("wgq1", [128, 2 * 16 * 512], F8, st_g)
                    gbuf = sb("gbuf", [128, TOK], BF16, st_g)
                    fbuf = sb("fbuf", [128, TOK], BF16, st_g)
                    tA = sb("tA", [128, TOK], BF16, st_g)
                    tB = sb("tB", [128, TOK], BF16, st_g)
                    tC = sb("tC", [128, TOK], BF16, st_g)
                    with nc.Block() as blk:
                        lw0 = [c.n for c in c_lw]
                        wcv0, mmd0, pev0, sg0, vg0 = (c_wcv.n, c_mmd.n,
                                                      c_pev.n, c_sg.n, c_vg.n)
                        with nc.psum_tensor("pgA", [128, 1024], F32) as pgA, \
                             nc.psum_tensor("pfA", [128, 1024], F32) as pfA, \
                             nc.psum_tensor("pgB", [128, 1024], F32) as pgB, \
                             nc.psum_tensor("pfB", [128, 1024], F32) as pfB:
                            pgs, pfs = [pgA, pgB], [pfA, pfB]

                            @blk.sync
                            def _(sync):
                                for jb in range(NHB):
                                    for gsel in range(2):
                                        for d in range(ND):
                                            k = (jb * 2 + gsel) * ND + d
                                            if k >= 4:
                                                sync.wait_ge(c_wcv.sem, wcv0 + (k - 3))
                                            src = wg_h[gsel, d * 128:(d + 1) * 128,
                                                       jb * 512:(jb + 1) * 512]
                                            c_lw[k % 4].dinc(sync.dma_start(
                                                wstg[:, (k % 4) * 512:(k % 4 + 1) * 512], src))

                            @blk.scalar
                            def _(s):
                                s = EP(s, c_ss)

                                def convs(jb):
                                    for gsel in range(2):
                                        for d in range(ND):
                                            k = (jb * 2 + gsel) * ND + d
                                            s.wait_ge(c_lw[k % 4].sem, lw0[k % 4] + 16 * (k // 4 + 1))
                                            if d == 0 and gsel == 0 and jb >= 2:
                                                s.wait_ge(c_mmd.sem, mmd0 + (jb - 1) * 4)
                                            dstt = wgq0 if gsel == 0 else wgq1
                                            dst = dstt[:, (jb % 2) * 8192 + d * 512:
                                                       (jb % 2) * 8192 + (d + 1) * 512]
                                            c_wcv.inc(s.copy(
                                                dst, wstg[:, (k % 4) * 512:(k % 4 + 1) * 512]))

                                convs(0)
                                convs(1)
                                for jb in range(NHB):
                                    for jj in range(jb * 4, jb * 4 + 4):
                                        s.wait_ge(c_vg.sem, vg0 + jj * 2 + 1)
                                        c_sg.inc(s.activation(tB[:, :], tC[:, :], AF.Tanh))
                                    if jb + 2 < NHB:
                                        convs(jb + 2)

                            @blk.vector
                            def _(v):
                                v = EP(v, c_vs)
                                for jb in range(NHB):
                                    for jj in range(jb * 4, jb * 4 + 4):
                                        v.wait_ge(c_mmd.sem, mmd0 + (jj + 1))
                                        pg, pf = pgs[jj % 2], pfs[jj % 2]
                                        vgd(v, v.tensor_scalar(gbuf[:, :], pg[:, :], 1.0, None, ALU.mult))
                                        c_pev.inc(v.tensor_scalar(fbuf[:, :], pf[:, :], 1.0, None, ALU.mult))
                                        vgd(v, v.tensor_mul(tA[:, :], gbuf[:, :], gbuf[:, :]))
                                        vgd(v, v.tensor_mul(tB[:, :], tA[:, :], gbuf[:, :]))
                                        vgd(v, v.tensor_scalar(tA[:, :], tB[:, :], C3, None, ALU.mult))
                                        vgd(v, v.tensor_add(tB[:, :], gbuf[:, :], tA[:, :]))
                                        c_vg.inc(v.tensor_scalar(tC[:, :], tB[:, :], CS, None, ALU.mult))
                                        v.wait_ge(c_sg.sem, sg0 + jj + 1)
                                        vgd(v, v.tensor_scalar(tC[:, :], tB[:, :], 0.5, 0.5, ALU.mult, ALU.add))
                                        vgd(v, v.tensor_mul(tB[:, :], gbuf[:, :], tC[:, :]))
                                        vgd(v, v.tensor_mul(tA[:, :], tB[:, :], fbuf[:, :]))
                                        c_vg.inc(v.tensor_scalar(act8[:, jj * TOK:(jj + 1) * TOK],
                                                                 tA[:, :], 1.0, None, ALU.mult))

                            @blk.tensor
                            def _(t):
                                t.wait_ge(c_tev.sem, c_tev.n)
                                for jj in range(NH):
                                    jb, sub = jj // 4, jj % 4
                                    t.wait_ge(c_wcv.sem, wcv0 + (jb + 1) * 32)
                                    if jj >= 2:
                                        t.wait_ge(c_pev.sem, pev0 + (jj - 1))
                                    pg, pf = pgs[jj % 2], pfs[jj % 2]
                                    ins = None
                                    for d in range(ND):
                                        off = (jb % 2) * 8192 + d * 512 + sub * 128
                                        l0 = wgq0[:, off:off + 128]
                                        l1 = wgq1[:, off:off + 128]
                                        r = xnqT[:, d * TOK:(d + 1) * TOK]
                                        stt, spp = (d == 0), (d == ND - 1)
                                        t.matmul(pg[:, 0:512], l0, r[:, 0:512], start=stt, stop=spp)
                                        t.matmul(pg[:, 512:1024], l0, r[:, 512:1024], start=stt, stop=spp)
                                        t.matmul(pf[:, 0:512], l1, r[:, 0:512], start=stt, stop=spp)
                                        ins = t.matmul(pf[:, 512:1024], l1, r[:, 512:1024],
                                                       start=stt, stop=spp)
                                    c_mmd.inc(ins)


            with contextlib.ExitStack() as st_d:
                if PH < 4:
                    return nc
                wlstg = sb("wlstg", [128, 2 * 512], F32, st_d)
                wlq = sb("wlq", [128, 2 * NH * 512], F8, st_d)
                fout = sb("fout", [128, 2 * 512], BF16, st_d)
                with nc.Block() as blk:
                    lw0 = [c.n for c in c_lw]
                    wcv0, mmd0, pev0, stg0 = (c_wcv.n, c_mmd.n, c_pev.n, c_stg.n)
                    with nc.psum_tensor("pdA", [128, 512], F32) as pdA, \
                         nc.psum_tensor("pdB", [128, 512], F32) as pdB:
                        pds = [pdA, pdB]

                        @blk.sync
                        def _(sync):
                            for q in range(4):
                                for jj in range(NH):
                                    kq = q * NH + jj
                                    if kq >= 2:
                                        sync.wait_ge(c_wcv.sem, wcv0 + kq - 1)
                                    srcap = wl_h[jj * 128:(jj + 1) * 128,
                                                 q * 512:(q + 1) * 512]
                                    c_lw[kq % 2].dinc(sync.dma_start(
                                        wlstg[:, (kq % 2) * 512:(kq % 2 + 1) * 512], srcap))

                        @blk.scalar
                        def _(s):
                            s = EP(s, c_ss)
                            for q in range(4):
                                for jj in range(NH):
                                    kq = q * NH + jj
                                    s.wait_ge(c_lw[kq % 2].sem, lw0[kq % 2] + 16 * (kq // 2 + 1))
                                    if jj == 0 and q >= 2:
                                        s.wait_ge(c_mmd.sem, mmd0 + (q - 1) * NTQ)
                                    dst = wlq[:, (q % 2) * (NH * 512) + jj * 512:
                                              (q % 2) * (NH * 512) + (jj + 1) * 512]
                                    c_wcv.inc(s.copy(dst,
                                                     wlstg[:, (kq % 2) * 512:(kq % 2 + 1) * 512]))

                        @blk.tensor
                        def _(t):
                            t.wait_ge(c_vg.sem, c_vg.n)
                            for q in range(4):
                                for ti in range(NTQ):
                                    gi = q * NTQ + ti
                                    if gi >= 2:
                                        t.wait_ge(c_pev.sem, pev0 + gi - 1)
                                    pd = pds[gi % 2]
                                    ins = None
                                    for jj in range(NH):
                                        if ti == 0:
                                            t.wait_ge(c_wcv.sem, wcv0 + q * NH + jj + 1)
                                        l = act8[:, jj * TOK + ti * 128: jj * TOK + (ti + 1) * 128]
                                        w = wlq[:, (q % 2) * (NH * 512) + jj * 512:
                                                (q % 2) * (NH * 512) + (jj + 1) * 512]
                                        ins = t.matmul(pd[:, :], l, w,
                                                       start=(jj == 0), stop=(jj == NH - 1))
                                    c_mmd.inc(ins)

                        @blk.vector
                        def _(v):
                            v = EP(v, c_vs)
                            for q in range(4):
                                for ti in range(NTQ):
                                    gi = q * NTQ + ti
                                    v.wait_ge(c_mmd.sem, mmd0 + gi + 1)
                                    if gi >= 2:
                                        v.wait_ge(c_stg.sem, stg0 + 16 * gi)
                                    c_pev.inc(v.tensor_scalar(
                                        fout[:, (gi % 2) * 512:(gi % 2 + 1) * 512],
                                        pds[gi % 2][:, :], 1.0, None, ALU.mult))

                        @blk.gpsimd
                        def _(g):
                            for q in range(4):
                                for ti in range(NTQ):
                                    gi = q * NTQ + ti
                                    g.wait_ge(c_pev.sem, pev0 + gi + 1)
                                    dst = ffn_hbm[ti * 128:(ti + 1) * 128,
                                                  q * 512:(q + 1) * 512]
                                    c_stg.dinc(g.dma_start(
                                        dst, fout[:, (gi % 2) * 512:(gi % 2 + 1) * 512]))
                            g.wait_ge(c_stg.sem, c_stg.n)

        if DEBUG:
            with nc.Block() as blk:
                @blk.gpsimd
                def _(g):
                    c_stg.dinc(g.dma_start(dbg_ffn[:, :], ffn_hbm[:, :]))
                    g.wait_ge(c_stg.sem, c_stg.n)

        with contextlib.ExitStack() as st_p4:
            if PH < 5:
                return nc
            xsT = sb("xsT", [128, ND * T], BF16, st_p4)
            wb = sb("wb", [128, ND * 4 * 512], BF16, st_p4)

            with contextlib.ExitStack() as st_a:
                xbuf = sb("xbuf4", [128, 2 * D], F32, st_a)
                xsb = sb("xsb", [128, D], BF16, st_a)
                with nc.Block() as blk:
                    lx0 = [c.n for c in c_lx]
                    lw0 = [c.n for c in c_lw]
                    xf0, tin0, tpd0, tev0, wcv0 = (
                        c_xfree.n, c_tin.n, c_tpd.n, c_tev.n, c_wcv.n)
                    with nc.psum_tensor("ptA4", [128, 2048], BF16) as ptA, \
                         nc.psum_tensor("ptB4", [128, 2048], BF16) as ptB:
                        pps = [ptA, ptB]

                        @blk.sync
                        def _(sync):
                            for i in range(NT):
                                if i >= 2:
                                    sync.wait_ge(c_xfree.sem, xf0 + (i - 1))
                                c_lx[i % 2].dinc(sync.dma_start(xbuf[:, (i % 2) * D:(i % 2 + 1) * D],
                                                                x_h[i * 128:(i + 1) * 128, :]))
                            for d in range(ND):
                                for cb in range(4):
                                    k = d * 4 + cb
                                    if k >= 4:
                                        sync.wait_ge(c_wcv.sem, wcv0 + (k - 3))
                                    src = wq_h[d * 128:(d + 1) * 128, cb * 512:(cb + 1) * 512]
                                    c_lw[k % 4].dinc(sync.dma_start(
                                        wstg[:, (k % 4) * 512:(k % 4 + 1) * 512], src))

                        @blk.vector
                        def _(v):
                            v = EP(v, c_vs)
                            for i in range(NT):
                                v.wait_ge(c_lx[i % 2].sem, lx0[i % 2] + 16 * (i // 2 + 1))
                                sl = xbuf[:, (i % 2) * D:(i % 2 + 1) * D]
                                c_tin.inc(v.tensor_scalar(xsb[:, :], sl, A_c[:, i:i + 1],
                                                           nmA_c[:, i:i + 1], ALU.mult, ALU.add))
                                c_xfree.inc(v.tensor_scalar(scal1[:1, 2:3], scal1[:1, 3:4], 1.0, None, ALU.mult))
                                v.wait_ge(c_tpd.sem, tpd0 + (i + 1))
                                pp = pps[i % 2]
                                ins = None
                                for d in range(ND):
                                    ins = v.tensor_scalar(xsT[:, d * T + i * 128: d * T + (i + 1) * 128],
                                                          pp[:, d * 128:(d + 1) * 128], 1.0, None, ALU.mult)
                                c_tev.inc(ins)
                            for d in range(ND):
                                for cb in range(4):
                                    k = d * 4 + cb
                                    v.wait_ge(c_lw[k % 4].sem, lw0[k % 4] + 16 * (k // 4 + 1))
                                    dst = wb[:, k * 512:(k + 1) * 512]
                                    c_wcv.inc(v.tensor_scalar(
                                        dst, wstg[:, (k % 4) * 512:(k % 4 + 1) * 512],
                                        gq_c[:, d:d + 1], None, ALU.mult))

                        @blk.tensor
                        def _(t):
                            for i in range(NT):
                                t.wait_ge(c_tin.sem, tin0 + (i + 1))
                                if i >= 2:
                                    t.wait_ge(c_tev.sem, tev0 + (i - 1))
                                pp = pps[i % 2]
                                ins = None
                                for d in range(ND):
                                    ins = t.transpose(pp[:, d * 128:(d + 1) * 128],
                                                      xsb[:, d * 128:(d + 1) * 128], ident[:, :])
                                c_tpd.inc(ins)

            with contextlib.ExitStack() as st_b:
                qf = sb("qf", [128, 2 * D], F32, st_b)
                knstg = sb("knstg", [128, 2 * D], BF16, st_b)
                qbf = sb("qbf", [128, 2 * D], BF16, st_b)
                rsq_t = sb("rsq_t", [128, H], F32, st_b)
                nst1 = sb("nst1", [128, H], F32, st_b)
                nst2 = sb("nst2", [128, H], F32, st_b)

                def proj_pass(n_tiles, out_kind, wcv_base, bias_row, bias_off, out_hbm):
                    with nc.Block() as blk:
                        mmd0, sg0, vg0, tpd0, tev0, pev0, stg0 = (
                            c_mmd.n, c_sg.n, c_vg.n, c_tpd.n, c_tev.n, c_pev.n, c_stg.n)
                        with nc.psum_tensor("pq_" + out_kind, [128, 2048], F32) as pq, \
                             nc.psum_tensor("pt1_" + out_kind, [128, 2048], BF16) as pt1, \
                             nc.psum_tensor("pt2_" + out_kind, [128, 2048], BF16) as pt2:
                            pts = [pt1, pt2]

                            @blk.tensor
                            def _(t):
                                t.wait_ge(c_tev.sem, tev0)

                                def transposes(tj):
                                    t.wait_ge(c_vg.sem, vg0 + 2 * tj + 2)
                                    if tj >= 2:
                                        t.wait_ge(c_tev.sem, tev0 + (tj - 1))
                                    pp = pts[tj % 2]
                                    ins2 = None
                                    for hh in range(H):
                                        ins2 = t.transpose(pp[:, hh * 128:(hh + 1) * 128],
                                                           qbf[:, (tj % 2) * D + hh * 128:
                                                               (tj % 2) * D + (hh + 1) * 128],
                                                           ident[:, :])
                                    c_tpd.inc(ins2)

                                for ti in range(n_tiles):
                                    if ti >= 1:
                                        t.wait_ge(c_pev.sem, pev0 + ti)
                                    for d in range(ND):
                                        if ti == 0:
                                            t.wait_ge(c_wcv.sem, wcv_base + 4 * (d + 1))
                                        l = xsT[:, d * T + ti * 128: d * T + (ti + 1) * 128]
                                        stt = (d == 0)
                                        for cb in range(4):
                                            t.matmul(pq[:, cb * 512:(cb + 1) * 512], l,
                                                     wb[:, (d * 4 + cb) * 512:(d * 4 + cb + 1) * 512],
                                                     start=stt, stop=False)
                                    ins = None
                                    for cb in range(4):
                                        ins = t.matmul(pq[:, cb * 512:(cb + 1) * 512], ones1f[:1, :],
                                                       bias_row[:1, bias_off + cb * 512:
                                                                bias_off + (cb + 1) * 512],
                                                       start=False, stop=True)
                                    c_mmd.inc(ins)
                                    if out_kind != 'v' and ti >= 1:
                                        transposes(ti - 1)
                                if out_kind != 'v':
                                    transposes(n_tiles - 1)

                            if out_kind != 'v':
                                @blk.scalar
                                def _(s):
                                    s = EP(s, c_ss)
                                    for ti in range(n_tiles):
                                        s.wait_ge(c_pev.sem, pev0 + ti + 1)
                                        if ti >= 1:
                                            s.wait_ge(c_vg.sem, vg0 + 2 * (ti - 1) + 1)
                                        c_sg.inc(s.square(wstg[:, :],
                                                          qf[:, (ti % 2) * D:(ti % 2 + 1) * D]))
                                        s.wait_ge(c_vg.sem, vg0 + 2 * ti + 1)
                                        c_sg.inc(s.sqrt(nst2[:, :], nst1[:, :]))

                                @blk.vector
                                def _(v):
                                    v = EP(v, c_vs)
                                    for ti in range(n_tiles):
                                        v.wait_ge(c_mmd.sem, mmd0 + ti + 1)
                                        if ti >= 2:
                                            v.wait_ge(c_sg.sem, sg0 + 2 * (ti - 2) + 1)
                                        qfs = qf[:, (ti % 2) * D:(ti % 2 + 1) * D]
                                        ins = None
                                        for cb in range(4):
                                            ins = v.tensor_scalar(qfs[:, cb * 512:(cb + 1) * 512],
                                                                  pq[:, cb * 512:(cb + 1) * 512],
                                                                  1.0, None, ALU.mult)
                                        c_pev.inc(ins)
                                        v.wait_ge(c_sg.sem, sg0 + 2 * ti + 1)
                                        sq = wstg[:, :]
                                        vgd(v, v.tensor_reduce(rsq_t[:, :],
                                                               sq.rearrange("p (h e) -> p h e", e=128),
                                                               AX.X, ALU.add))
                                        c_vg.inc(v.tensor_scalar(nst1[:, :], rsq_t[:, :], 1.0 / HD,
                                                                 EPS32, ALU.mult, ALU.add))
                                        v.wait_ge(c_sg.sem, sg0 + 2 * ti + 2)
                                        vgd(v, v.reciprocal(rsq_t[:, :], nst2[:, :]))
                                        vgd(v, v.tensor_mul(nst2[:, :], rsq_t[:, :], rsq_t[:, :]))
                                        vgd(v, v.tensor_mul(nst2[:, :], nst2[:, :], nst1[:, :]))
                                        vgd(v, v.tensor_scalar(nst2[:, :], nst2[:, :], -0.5, 1.5,
                                                               ALU.mult, ALU.add))
                                        vgd(v, v.tensor_mul(rsq_t[:, :], rsq_t[:, :], nst2[:, :]))
                                        ins = None
                                        for hh in range(H):
                                            ins = v.tensor_scalar(
                                                qbf[:, (ti % 2) * D + hh * 128:(ti % 2) * D + (hh + 1) * 128],
                                                qfs[:, hh * 128:(hh + 1) * 128],
                                                rsq_t[:, hh:hh + 1], None, ALU.mult)
                                        c_vg.inc(ins)
                                        v.wait_ge(c_tpd.sem, tpd0 + (ti + 1))
                                        v.wait_ge(c_stg.sem, stg0 + 16 * H * ti if ti >= 2 else stg0)
                                        pp = pts[ti % 2]
                                        ins = None
                                        for hh in range(H):
                                            dst = knstg[:, (ti % 2) * D + hh * 128:
                                                        (ti % 2) * D + (hh + 1) * 128]
                                            ins = v.tensor_scalar(dst, pp[:, hh * 128:(hh + 1) * 128],
                                                                  ghs_c[:, 0:1], None, ALU.mult)
                                        c_tev.inc(ins)
                            else:
                                @blk.vector
                                def _(v):
                                    v = EP(v, c_vs)
                                    for ti in range(n_tiles):
                                        v.wait_ge(c_mmd.sem, mmd0 + (ti + 1))
                                        v.wait_ge(c_stg.sem, stg0 + 16 * H * ti if ti >= 2 else stg0)
                                        sl = knstg[:, (ti % 2) * D:(ti % 2 + 1) * D]
                                        _nocp = os.environ.get("KVNOCOPY") == "1"
                                        ins = None
                                        for cb in range(4):
                                            ins = v.tensor_scalar(sl[:, cb * 512:(cb + 1) * 512],
                                                                  pq[:, cb * 512:(cb + 1) * 512],
                                                                  1.0, None, ALU.mult)
                                        c_pev.inc(ins)
                                        c_tev.inc(v.tensor_scalar(scal1[:1, 2:3], scal1[:1, 3:4], 1.0, None, ALU.mult))

                            import os as _os
                            _skipdma = out_kind == 'v' and _os.environ.get("KSKIPV") == "1"

                            @blk.gpsimd
                            def _(g):
                                for ti in range(n_tiles):
                                    g.wait_ge(c_tev.sem, tev0 + (ti + 1))
                                    sl = knstg[:, (ti % 2) * D:(ti % 2 + 1) * D]
                                    if _skipdma:
                                        continue
                                    _kst = os.environ.get("KVMODE", "") == "v2" and out_kind == 'v'
                                    for hh in range(H):
                                        if out_kind == 'v' and not _kst:
                                            dst = out_hbm[hh, ti * 128:(ti + 1) * 128, :]
                                        else:
                                            dst = out_hbm[hh, :, ti * 128:(ti + 1) * 128]
                                        c_stg.dinc(g.dma_start(dst, sl[:, hh * 128:(hh + 1) * 128]))

                if PH < 6:
                    return nc
                proj_pass(NTQ, 'q', c_wcv.n - 64, bq_r, 0, qn_hbm)

                def kv_fold(col_off):
                    with nc.Block() as blk:
                        lw0 = [c.n for c in c_lw]
                        wcv0 = c_wcv.n

                        @blk.sync
                        def _(sync):
                            for d in range(ND):
                                for cb in range(4):
                                    k = d * 4 + cb
                                    if k >= 4:
                                        sync.wait_ge(c_wcv.sem, wcv0 + (k - 3))
                                    src = wkv_h[d * 128:(d + 1) * 128,
                                                col_off + cb * 512:col_off + (cb + 1) * 512]
                                    c_lw[k % 4].dinc(sync.dma_start(
                                        wstg[:, (k % 4) * 512:(k % 4 + 1) * 512], src))

                        @blk.vector
                        def _(v):
                            v = EP(v, c_vs)
                            for d in range(ND):
                                for cb in range(4):
                                    k = d * 4 + cb
                                    v.wait_ge(c_lw[k % 4].sem, lw0[k % 4] + 16 * (k // 4 + 1))
                                    c_wcv.inc(v.tensor_scalar(
                                        wb[:, k * 512:(k + 1) * 512],
                                        wstg[:, (k % 4) * 512:(k % 4 + 1) * 512],
                                        gkv_c[:, d:d + 1], None, ALU.mult))

                if PH < 7:
                    return nc
                kv_fold(0)
                proj_pass(NT, 'k', c_wcv.n - 64, bkv_r, 0, kn_hbm)
                if PH < 8:
                    return nc
                kv_fold(D)
                if PH < 9:
                    return nc
                _kvm = os.environ.get("KVMODE", "")
                _vmode = 'k2' if _kvm == "k" else 'v'
                _vout = v_hbm
                if _kvm == "k":
                    _vout = kn_hbm
                elif _kvm == "v2":
                    _vout = kn_hbm
                proj_pass(int(os.environ.get("KVTILES", str(NT))), _vmode, c_wcv.n - 64, bkv_r, D, _vout)

        if DEBUG:
            with nc.Block() as blk:
                @blk.gpsimd
                def _(g):
                    for hh in range(H):
                        c_stg.dinc(g.dma_start(dbg_qnT[:, hh * TOK:(hh + 1) * TOK], qn_hbm[hh, :, :]))
                        c_stg.dinc(g.dma_start(dbg_knT[:, hh * T:(hh + 1) * T], kn_hbm[hh, :, :]))
                    g.wait_ge(c_stg.sem, c_stg.n)

        with contextlib.ExitStack() as st_wv:
            if PH < 10:
                return nc
            wvT = sb("wvT", [128, H * TOK], BF16, st_wv)
            wob = sb("wob", [128, 4 * H * 512], BF16, st_wv)

            with contextlib.ExitStack() as st_p5:
                expT = sb("expT", [128, NT * TOK], BF16, st_p5)
                knb = sb("knb", [128, 2 * T], BF16, st_p5)
                vhb = sb("vhb", [128, 2 * T], BF16, st_p5)
                qnb = sb("qnb", [128, 2 * TOK], BF16, st_p5)
                zrow = sb("zrow", [1, TOK], F32, st_p5)
                zrow2 = sb("zrow2", [1, TOK], F32, st_p5)
                zb = sb("zb", [128, TOK], F32, st_p5)
                wvf = sb("wvf", [128, TOK], F32, st_p5)
                with nc.Block() as blk:
                    lg0 = [c.n for c in c_lg]
                    lw0 = [c.n for c in c_lw]
                    mmd0, zw0, sg0, zr0, bc0, wvn0, pev0, wcv0 = (
                        c_mmd.n, c_zw.n, c_sg.n, c_zr.n, c_bc.n, c_wvn.n, c_pev.n, c_wcv.n)
                    with nc.psum_tensor("psA5", [128, 1024], F32) as psA, \
                         nc.psum_tensor("psB5", [128, 1024], F32) as psB, \
                         nc.psum_tensor("pwv", [128, 1024], F32) as pwv, \
                         nc.psum_tensor("pzA", [1, 512], F32) as pzA, \
                         nc.psum_tensor("pzB", [1, 512], F32) as pzB:
                        pss = [psA, psB]

                        @blk.sync
                        def _(sync):
                            for cb in range(4):
                                for hh in range(H):
                                    k = cb * H + hh
                                    if k >= 4:
                                        sync.wait_ge(c_wcv.sem, wcv0 + (k - 3))
                                    srcw = wo_h[hh * 128:(hh + 1) * 128, cb * 512:(cb + 1) * 512]
                                    c_lw[k % 4].dinc(sync.dma_start(
                                        wstg[:, (k % 4) * 512:(k % 4 + 1) * 512], srcw))

                        @blk.gpsimd
                        def _(g):
                            g.wait_ge(c_stg.sem, c_stg.n)
                            for h in range(H):
                                if h >= 2:
                                    g.wait_ge(c_wvn.sem, wvn0 + (h - 1))
                                c_lg[h % 2].dinc(g.dma_start(knb[:, (h % 2) * T:(h % 2 + 1) * T],
                                                             kn_hbm[h, :, :]))
                                c_lg[h % 2].dinc(g.dma_start(qnb[:, (h % 2) * TOK:(h % 2 + 1) * TOK],
                                                             qn_hbm[h, :, :]))
                                c_lg[h % 2].dinc(g.dma_start(
                                    vhb[:, (h % 2) * T:(h % 2 + 1) * T].rearrange(
                                        "p (c v) -> p c v", v=128),
                                    v_hbm[h].rearrange("(c p) v -> p c v", p=128)))

                        def sc_wait(t, gi):
                            if gi >= 2:
                                t.wait_ge(c_sg.sem, sg0 + gi - 1)

                        @blk.tensor
                        def _(t):
                            for h in range(H):
                                t.wait_ge(c_lg[h % 2].sem, lg0[h % 2] + 48 * (h // 2 + 1))
                                gi0 = h * NT
                                sc_wait(t, gi0)
                                l = knb[:, (h % 2) * T + 0:(h % 2) * T + 128]
                                q0 = qnb[:, (h % 2) * TOK:(h % 2) * TOK + 512]
                                q1 = qnb[:, (h % 2) * TOK + 512:(h % 2 + 1) * TOK]
                                ps = pss[gi0 % 2]
                                t.matmul(ps[:, 0:512], l, q0, start=True, stop=True)
                                c_mmd.inc(t.matmul(ps[:, 512:1024], l, q1, start=True, stop=True))
                                if h >= 1:
                                    t.wait_ge(c_pev.sem, pev0 + h)
                                for i in range(NT):
                                    gi = h * NT + i
                                    if i + 1 < NT:
                                        gi2 = gi + 1
                                        sc_wait(t, gi2)
                                        if i + 1 == 1 and h >= 1:
                                            t.wait_ge(c_wvn.sem, wvn0 + h)
                                        l2 = knb[:, (h % 2) * T + (i + 1) * 128:
                                                 (h % 2) * T + (i + 2) * 128]
                                        ps2 = pss[gi2 % 2]
                                        t.matmul(ps2[:, 0:512], l2, q0, start=True, stop=True)
                                        c_mmd.inc(t.matmul(ps2[:, 512:1024], l2, q1,
                                                           start=True, stop=True))
                                    t.wait_ge(c_sg.sem, sg0 + gi + 1)
                                    e = expT[:, i * TOK:(i + 1) * TOK]
                                    stt, spp = (i == 0), (i == NT - 1)
                                    t.matmul(pzA[:1, :], onescol_b[:, 0:1], e[:, 0:512],
                                             start=stt, stop=spp)
                                    t.matmul(pzB[:1, :], onescol_b[:, 0:1], e[:, 512:1024],
                                             start=stt, stop=spp)
                                    vl = vhb[:, (h % 2) * T + i * 128:(h % 2) * T + (i + 1) * 128]
                                    t.matmul(pwv[:, 0:512], vl, e[:, 0:512], start=stt, stop=spp)
                                    c_zw.inc(t.matmul(pwv[:, 512:1024], vl, e[:, 512:1024],
                                                      start=stt, stop=spp))
                                t.wait_ge(c_zr.sem, zr0 + (h + 1))
                                t.matmul(pss[1][:, 0:512], ones1f[:1, :], zrow[:1, 0:512],
                                         start=True, stop=True)
                                c_bc.inc(t.matmul(pss[1][:, 512:1024], ones1f[:1, :],
                                                  zrow[:1, 512:1024], start=True, stop=True))

                        @blk.scalar
                        def _(s):
                            s = EP(s, c_ss)
                            s.wait_ge(c_vg.sem, c_vg.n)
                            for h in range(H):
                                for kk in range(4 * h, 4 * h + 4):
                                    s.wait_ge(c_lw[kk % 4].sem, lw0[kk % 4] + 16 * (kk // 4 + 1))
                                    c_wcv.inc(s.copy(wob[:, kk * 512:(kk + 1) * 512],
                                                     wstg[:, (kk % 4) * 512:(kk % 4 + 1) * 512]))
                                for i in range(NT):
                                    gi = h * NT + i
                                    s.wait_ge(c_mmd.sem, mmd0 + gi + 1)
                                    ps = pss[gi % 2]
                                    e = expT[:, i * TOK:(i + 1) * TOK]
                                    c_sg.inc(s.activation(e[:, :], ps[:, :], AF.Exp,
                                                          bias=negM_c[:, 0:1], scale=1.0))

                        @blk.vector
                        def _(v):
                            v = EP(v, c_vs)
                            for h in range(H):
                                v.wait_ge(c_zw.sem, zw0 + (h + 1) * NT)
                                wvs = wvf[:, :]
                                v.tensor_scalar(wvs[:, 0:512], pwv[:, 0:512], 1.0, None, ALU.mult)
                                v.tensor_scalar(wvs[:, 512:1024], pwv[:, 512:1024], 1.0, None, ALU.mult)
                                v.tensor_scalar(zrow2[:1, 0:512], pzA[:1, :], 1.0, None, ALU.mult)
                                c_pev.inc(v.tensor_scalar(zrow2[:1, 512:1024], pzB[:1, :],
                                                          1.0, None, ALU.mult))
                                v.wait_ge(c_vs.sem, c_vs.n)
                                c_zr.inc(v.reciprocal(zrow[:1, :], zrow2[:1, :]))
                                v.wait_ge(c_bc.sem, bc0 + (h + 1))
                                vgd(v, v.tensor_scalar(zb[:, :], pss[1][:, :], 1.0, None, ALU.mult))
                                c_wvn.inc(v.tensor_mul(wvT[:, h * TOK:(h + 1) * TOK],
                                                       wvs[:, :], zb[:, :]))

            if DEBUG:
                with nc.Block() as blk:
                    @blk.gpsimd
                    def _(g):
                        c_stg.dinc(g.dma_start(dbg_wvT[:, :], wvT[:, :]))
                        g.wait_ge(c_stg.sem, c_stg.n)

            with contextlib.ExitStack() as st_p6:
                if PH < 11:
                    return nc
                xrb = sb("xrb", [128, 2 * D], F32, st_p6)
                frb = sb("frb", [128, 2 * D], BF16, st_p6)
                yout = sb("yout", [128, 2 * D], F32, st_p6)
                with nc.Block() as blk:
                    lx0 = [c.n for c in c_lx]
                    lw0 = [c.n for c in c_lw]
                    lg0 = [c.n for c in c_lg]
                    wcv0, mmd0, pev0, stg0, xf0, vg0 = (
                        c_wcv.n, c_mmd.n, c_pev.n,
                        c_stg.n, c_xfree.n, c_vg.n)
                    with nc.psum_tensor("poA", [128, 512], F32) as poA, \
                         nc.psum_tensor("poB", [128, 512], F32) as poB:
                        pos = [poA, poB]

                        @blk.sync
                        def _(sync):
                            for ti in range(NTQ):
                                if ti >= 2:
                                    sync.wait_ge(c_xfree.sem, xf0 + (ti - 1))
                                c_lx[ti % 2].dinc(sync.dma_start(xrb[:, (ti % 2) * D:(ti % 2 + 1) * D],
                                                                 x_h[ti * 128:(ti + 1) * 128, :]))

                        @blk.gpsimd
                        def _(g):
                            for ti in range(NTQ + 2):
                                if 0 <= ti - 2 < NTQ:
                                    tj = ti - 2
                                    g.wait_ge(c_pev.sem, pev0 + (tj + 1) * 4)
                                    c_stg.dinc(g.dma_start(y_h[tj * 128:(tj + 1) * 128, :],
                                                           yout[:, (tj % 2) * D:(tj % 2 + 1) * D]))
                                if ti < NTQ:
                                    if ti >= 2:
                                        g.wait_ge(c_vg.sem, vg0 + (ti - 1))
                                    c_lg[ti % 2].dinc(g.dma_start(frb[:, (ti % 2) * D:(ti % 2 + 1) * D],
                                                                  ffn_hbm[ti * 128:(ti + 1) * 128, :]))
                            g.wait_ge(c_stg.sem, c_stg.n)

                        @blk.tensor
                        def _(t):
                            t.wait_ge(c_wvn.sem, c_wvn.n)
                            t.wait_ge(c_wcv.sem, wcv0)
                            for ti in range(NTQ):
                                for cb in range(4):
                                    gi = ti * 4 + cb
                                    if gi >= 2:
                                        t.wait_ge(c_pev.sem, pev0 + (gi - 1))
                                    po = pos[gi % 2]
                                    for hh in range(H):
                                        l = wvT[:, hh * TOK + ti * 128: hh * TOK + (ti + 1) * 128]
                                        t.matmul(po[:, :], l,
                                                 wob[:, (cb * H + hh) * 512:(cb * H + hh + 1) * 512],
                                                 start=(hh == 0), stop=False)
                                    c_mmd.inc(t.matmul(po[:, :], ones1f[:1, :],
                                                       bo_r[:1, cb * 512:(cb + 1) * 512],
                                                       start=False, stop=True))

                        @blk.vector
                        def _(v):
                            v = EP(v, c_vs)
                            for ti in range(NTQ):
                                v.wait_ge(c_lx[ti % 2].sem, lx0[ti % 2] + 16 * (ti // 2 + 1))
                                v.wait_ge(c_lg[ti % 2].sem, lg0[ti % 2] + 16 * (ti // 2 + 1))
                                if ti >= 2:
                                    v.wait_ge(c_stg.sem, stg0 + 16 * ti)
                                for cb in range(4):
                                    gi = ti * 4 + cb
                                    v.wait_ge(c_mmd.sem, mmd0 + (gi + 1))
                                    po = pos[gi % 2]
                                    lo = (ti % 2) * D + cb * 512
                                    yl = yout[:, lo:lo + 512]
                                    vgd(v, v.tensor_add(yl, po[:, :], xrb[:, lo:lo + 512]))
                                    c_pev.inc(v.tensor_add(yl, yl, frb[:, lo:lo + 512]))
                                c_xfree.inc(v.tensor_scalar(scal1[:1, 2:3], scal1[:1, 3:4], 1.0, None, ALU.mult))
                                c_vg.inc(v.tensor_scalar(scal1[:1, 4:5], scal1[:1, 5:6], 1.0, None, ALU.mult))

    return nc


_NC_CACHE = {}


def _get_nc():
    key = ("nc", DEBUG)
    if key not in _NC_CACHE:
        _NC_CACHE[key] = build_nc()
    return _NC_CACHE[key]


def kernel(L0_attn_x, L0_attn_g_q, L0_attn_Wq, L0_attn_bq, L0_attn_g_kv,
           L0_attn_Wkv, L0_attn_bkv, L0_attn_g_head, L0_attn_Wo, L0_attn_bo,
           L0_ffn_w_gating, L0_ffn_w_linear):
    x = np.ascontiguousarray(np.asarray(L0_attn_x, dtype=np.float32))
    common = {
        "g_q": np.ascontiguousarray(L0_attn_g_q, dtype=np.float32),
        "Wq": np.ascontiguousarray(L0_attn_Wq, dtype=np.float32),
        "bq": np.ascontiguousarray(L0_attn_bq, dtype=np.float32),
        "g_kv": np.ascontiguousarray(L0_attn_g_kv, dtype=np.float32),
        "Wkv": np.ascontiguousarray(L0_attn_Wkv, dtype=np.float32),
        "bkv": np.ascontiguousarray(L0_attn_bkv, dtype=np.float32),
        "g_head": np.ascontiguousarray(L0_attn_g_head, dtype=np.float32),
        "Wo": np.ascontiguousarray(L0_attn_Wo, dtype=np.float32),
        "bo": np.ascontiguousarray(L0_attn_bo, dtype=np.float32),
        "w_gating": np.ascontiguousarray(L0_ffn_w_gating, dtype=np.float32),
        "w_linear": np.ascontiguousarray(L0_ffn_w_linear, dtype=np.float32),
    }
    in_maps = []
    for c in range(8):
        b, h = c // 2, c % 2
        xb = x[b]
        xc = np.concatenate([xb[h * TOK:(h + 1) * TOK], xb[(1 - h) * TOK:(2 - h) * TOK]], axis=0)
        m = dict(common)
        m["x"] = np.ascontiguousarray(xc)
        in_maps.append(m)
    nc = _get_nc()
    res = run_bass_kernel_spmd(nc, in_maps, core_ids=list(range(8)))
    out = np.empty((B, T, D), dtype=np.float32)
    for c in range(8):
        b, h = c // 2, c % 2
        out[b, h * TOK:(h + 1) * TOK] = res.results[c]["y"]
    return out


# version 09
# speedup vs baseline: 1.0544x; 1.0276x over previous
import contextlib
import os

import numpy as np
import ml_dtypes

import concourse.bass as bass
import concourse.mybir as mybir
from concourse.bass_utils import run_bass_kernel_spmd

BFD = ml_dtypes.bfloat16
F32 = mybir.dt.float32
BF16 = mybir.dt.bfloat16
F8 = mybir.dt.float8e4
AF = mybir.ActivationFunctionType
ALU = mybir.AluOpType
AX = mybir.AxisListType

B, T, D = 4, 2048, 2048
H, HD = 16, 128
HID = 8192
TOK = 1024
NT = T // 128
NTQ = TOK // 128
ND = D // 128
NH = HID // 128
NHB = 16
EPS32 = float(np.finfo(np.float32).eps)
C3 = float(np.float32(BFD(0.044715)))
CS = float(np.float32(BFD(np.sqrt(2.0 / np.pi))))
S4 = float(128.0 ** -0.25)
SQRT128 = float(np.sqrt(128.0))

DEBUG = False
PH = int(os.environ.get("KPH", "99"))


_INC_INFO = {}


class Ctr:
    def __init__(self, sem):
        self.sem = sem
        self.n = 0

    def inc(self, instr, k=1):
        instr.then_inc(self.sem, k)
        self.n += k
        _INC_INFO[id(instr)] = (self.sem, self.n, instr)
        return self.n

    def dinc(self, instr):
        return self.inc(instr, 16)


_EP_OPS = frozenset((
    "tensor_scalar", "tensor_tensor", "tensor_add", "tensor_sub", "tensor_mul",
    "tensor_max", "tensor_copy", "tensor_reduce", "reduce_sum", "reduce_max",
    "reciprocal", "scalar_tensor_tensor", "memset", "activation", "square",
    "sqrt", "mul", "add", "copy", "sign",
))


class EP:

    def __init__(self, eng, guard):
        self._e = eng
        self._g = guard
        self._pend = {}
        self._mark = {}

    def wait_ge(self, sem, n):
        self._mark[id(sem)] = max(self._mark.get(id(sem), -1), n)
        return self._e.wait_ge(sem, n)

    def _require(self, instr):
        info = _INC_INFO.get(id(instr))
        if info is None or info[2] is not instr:
            self._g.inc(instr)
            info = _INC_INFO[id(instr)]
        sem, n, _ = info
        if self._mark.get(id(sem), -1) < n:
            self._mark[id(sem)] = n
            self._e.wait_ge(sem, n)

    def __getattr__(self, name):
        f = getattr(self._e, name)
        if name not in _EP_OPS:
            return f

        def g(*args, **kw):
            aps = [a for a in args if isinstance(a, bass.AP)]
            aps += [a for a in kw.values() if isinstance(a, bass.AP)]
            if not aps:
                return f(*args, **kw)
            outn = aps[0].tensor.name
            for a in aps[1:]:
                w = self._pend.get(a.tensor.name)
                if w is not None:
                    self._require(w)
            w = self._pend.get(outn)
            if w is not None:
                self._require(w)
            instr = f(*args, **kw)
            self._pend[outn] = instr
            ac = kw.get("accum_out")
            if isinstance(ac, bass.AP):
                self._pend[ac.tensor.name] = instr
            return instr

        return g


def build_nc():
    nc = bass.Bass("TRN2", target_bir_lowering=False)

    x_h = nc.dram_tensor("x", [T, D], F32, kind="ExternalInput")
    gq_h = nc.dram_tensor("g_q", [D], F32, kind="ExternalInput")
    wq_h = nc.dram_tensor("Wq", [D, D], F32, kind="ExternalInput")
    bq_h = nc.dram_tensor("bq", [D], F32, kind="ExternalInput")
    gkv_h = nc.dram_tensor("g_kv", [D], F32, kind="ExternalInput")
    wkv_h = nc.dram_tensor("Wkv", [D, 2 * D], F32, kind="ExternalInput")
    bkv_h = nc.dram_tensor("bkv", [2 * D], F32, kind="ExternalInput")
    gh_h = nc.dram_tensor("g_head", [HD], F32, kind="ExternalInput")
    wo_h = nc.dram_tensor("Wo", [D, D], F32, kind="ExternalInput")
    bo_h = nc.dram_tensor("bo", [D], F32, kind="ExternalInput")
    wg_h = nc.dram_tensor("w_gating", [2, D, HID], F32, kind="ExternalInput")
    wl_h = nc.dram_tensor("w_linear", [HID, D], F32, kind="ExternalInput")
    y_h = nc.dram_tensor("y", [TOK, D], F32, kind="ExternalOutput")
    v_hbm = nc.dram_tensor("v_scr", [H, T, HD], BF16, kind="Internal")
    kn_hbm = nc.dram_tensor("kn_scr", [H, HD, T], BF16, kind="Internal")
    qn_hbm = nc.dram_tensor("qn_scr", [H, HD, TOK], BF16, kind="Internal")
    ffn_hbm = nc.dram_tensor("ffn_scr", [TOK, D], BF16, kind="Internal")
    if DEBUG:
        dbg_xnqT = nc.dram_tensor("dbg_xnqT", [128, ND * TOK], mybir.dt.uint8, kind="ExternalOutput")
        dbg_stats = nc.dram_tensor("dbg_stats", [128, 6 * NT], F32, kind="ExternalOutput")
        dbg_ffn = nc.dram_tensor("dbg_ffn", [TOK, D], BF16, kind="ExternalOutput")
        dbg_qnT = nc.dram_tensor("dbg_qnT", [128, H * TOK], BF16, kind="ExternalOutput")
        dbg_knT = nc.dram_tensor("dbg_knT", [128, H * T], BF16, kind="ExternalOutput")
        dbg_wvT = nc.dram_tensor("dbg_wvT", [128, H * TOK], BF16, kind="ExternalOutput")

    es = contextlib.ExitStack()
    with es:
        def sb(name, shape, dt, st=None):
            return (st or es).enter_context(nc.sbuf_tensor(name, shape, dt))

        def sem(name):
            return Ctr(es.enter_context(nc.semaphore(name)))

        c_lx = [sem("lx0"), sem("lx1")]
        c_lw = [sem("lw0"), sem("lw1"), sem("lw2"), sem("lw3")]
        c_ldx = sem("ldx")
        c_ldw = sem("ldw")
        c_lds = sem("lds")
        c_ldg = sem("ldg")
        c_lg = [sem("lg0"), sem("lg1")]
        c_stg = sem("stg")
        c_xfree = sem("xfree")
        c_tin = sem("tin")
        c_tpd = sem("tpd")
        c_tev = sem("tev")
        c_wcv = sem("wcv")
        c_mmd = sem("mmd")
        c_zw = sem("zw")
        c_pev = sem("pev")
        c_sg = sem("sg")
        c_vg = sem("vg")
        c_zr = sem("zr")
        c_bc = sem("bc")
        c_wvn = sem("wvn")
        c_vs = sem("vs")
        c_ss = sem("ss")
        c_ps = sem("ps")

        def vgd(eng, ins):
            c_vs.inc(ins)
            eng.wait_ge(c_vs.sem, c_vs.n)
            return ins

        ident = sb("ident", [128, 128], BF16)
        ones1f = sb("ones1f", [1, 128], F32)
        onescol_b = sb("onescol_b", [128, 1], BF16)
        ones1b = sb("ones1b", [1, 128], BF16)
        gq_c = sb("gq_c", [128, ND], F32)
        gkv_c = sb("gkv_c", [128, ND], F32)
        gh_c = sb("gh_c", [128, 1], F32)
        ghs_c = sb("ghs_c", [128, 1], F32)
        gh_r = sb("gh_r", [1, 128], F32)
        bq_r = sb("bq_r", [1, D], F32)
        bkv_r = sb("bkv_r", [1, 2 * D], F32)
        bo_r = sb("bo_r", [1, D], F32)
        negM_c = sb("negM_c", [128, 1], F32)
        scal1 = sb("scal1", [1, 128], F32)
        sum_c = sb("sum_c", [128, NT], F32)
        ssq_c = sb("ssq_c", [128, NT], F32)
        m_c = sb("m_c", [128, NT], F32)
        var_c = sb("var_c", [128, NT], F32)
        rstd_c = sb("rstd_c", [128, NT], F32)
        nmrs_c = sb("nmrs_c", [128, NT], F32)
        A_c = sb("A_c", [128, NT], F32)
        nmA_c = sb("nmA_c", [128, NT], F32)
        st1 = sb("st1", [128, NT], F32)
        st2 = sb("st2", [128, NT], F32)
        st3 = sb("st3", [128, NT], F32)
        wstg = sb("wstg", [128, 4 * 512], F32)

        with nc.Block() as blk:
            @blk.sync
            def _(sync):
                for dst, src in (
                    (gq_c[:, :], gq_h.ap().rearrange("(c p) -> p c", p=128)),
                    (gkv_c[:, :], gkv_h.ap().rearrange("(c p) -> p c", p=128)),
                    (gh_c[:, :], gh_h.ap().rearrange("(p one) -> p one", one=1)),
                    (gh_r[:, :], gh_h.ap().rearrange("(one p) -> one p", one=1)),
                    (bq_r[:, :], bq_h.ap().rearrange("(one d) -> one d", one=1)),
                    (bkv_r[:, :], bkv_h.ap().rearrange("(one d) -> one d", one=1)),
                    (bo_r[:, :], bo_h.ap().rearrange("(one d) -> one d", one=1)),
                ):
                    c_lds.dinc(sync.dma_start(dst, src, allow_slow_non_contiguous=True))

            @blk.gpsimd
            def _(g):
                g.memset(ident[:, :], 0.0).then_inc(c_ps.sem, 1)
                c_ps.n += 1
                g.memset(ones1f[:, :], 1.0)
                g.memset(onescol_b[:, :], 1.0)
                g.memset(ones1b[:, :], 1.0)
                g.wait_ge(c_ps.sem, c_ps.n)
                c_ps.inc(g.affine_select(out=ident[:, :], in_=ident[:, :],
                                         compare_op=ALU.not_equal, fill=1.0, base=0,
                                         pattern=[[-1, 128]], channel_multiplier=1))

            @blk.vector
            def _(v):
                v = EP(v, c_vs)
                v.wait_ge(c_lds.sem, c_lds.n)
                v.tensor_scalar(ghs_c[:, :], gh_c[:, :], S4, None, ALU.mult)
                vgd(v, v.tensor_mul(scal1[:1, :], gh_r[:1, :], gh_r[:1, :]))
                vgd(v, v.reduce_max(scal1[:1, :1], scal1[:1, :], AX.X))
                c_vg.inc(v.tensor_scalar(scal1[:1, :1], scal1[:1, :1], -SQRT128, None, ALU.mult))

        with nc.Block() as blk:
            with nc.psum_tensor("p0b", [128, 512], F32) as p0b:
                @blk.tensor
                def _(t):
                    t.wait_ge(c_vg.sem, c_vg.n)
                    c_mmd.inc(t.matmul(p0b[:, 0:1], ones1f[:1, :], scal1[:1, :1],
                                       start=True, stop=True))

                @blk.vector
                def _(v):
                    v = EP(v, c_vs)
                    v.wait_ge(c_mmd.sem, c_mmd.n)
                    c_vg.inc(v.tensor_scalar(negM_c[:, :], p0b[:, 0:1], 1.0, None, ALU.mult))

        with contextlib.ExitStack() as st_p1:
            if PH < 1:
                return nc
            xbuf = sb("xbuf1", [128, 2 * D], F32, st_p1)
            sqscr = sb("sqscr1", [128, 2 * D], F32, st_p1)
            with nc.Block() as blk:
                lx0 = [c.n for c in c_lx]
                xf0, sg0, vg0 = c_xfree.n, c_sg.n, c_vg.n

                @blk.sync
                def _(sync):
                    for i in range(NT):
                        if i >= 2:
                            sync.wait_ge(c_xfree.sem, xf0 + (i - 1))
                        c_lx[i % 2].dinc(sync.dma_start(xbuf[:, (i % 2) * D:(i % 2 + 1) * D],
                                                        x_h[i * 128:(i + 1) * 128, :]))

                @blk.scalar
                def _(s):
                    s = EP(s, c_ss)
                    for i in range(NT):
                        s.wait_ge(c_lx[i % 2].sem, lx0[i % 2] + 16 * (i // 2 + 1))
                        if i >= 2:
                            s.wait_ge(c_xfree.sem, xf0 + (i - 1))
                        c_sg.inc(s.square(sqscr[:, (i % 2) * D:(i % 2 + 1) * D],
                                          xbuf[:, (i % 2) * D:(i % 2 + 1) * D]))

                @blk.vector
                def _(v):
                    v = EP(v, c_vs)
                    for i in range(NT):
                        v.wait_ge(c_sg.sem, sg0 + (i + 1))
                        v.reduce_sum(sum_c[:, i:i + 1], xbuf[:, (i % 2) * D:(i % 2 + 1) * D], AX.X)
                        c_xfree.inc(v.reduce_sum(ssq_c[:, i:i + 1],
                                                 sqscr[:, (i % 2) * D:(i % 2 + 1) * D], AX.X))
                    vgd(v, v.tensor_scalar(m_c[:, :], sum_c[:, :], 1.0 / D, None, ALU.mult))
                    vgd(v, v.tensor_scalar(ssq_c[:, :], ssq_c[:, :], 1.0 / D, None, ALU.mult))
                    vgd(v, v.tensor_mul(st1[:, :], m_c[:, :], m_c[:, :]))
                    vgd(v, v.tensor_sub(var_c[:, :], ssq_c[:, :], st1[:, :]))
                    c_vg.inc(v.tensor_scalar(st1[:, :], var_c[:, :], 1e-5, None, ALU.add))

                @blk.scalar
                def _(s):
                    s = EP(s, c_ss)
                    s.wait_ge(c_vg.sem, vg0 + 1)
                    c_sg.inc(s.sqrt(st2[:, :], st1[:, :]))

                @blk.vector
                def _(v):
                    v = EP(v, c_vs)
                    v.wait_ge(c_sg.sem, c_sg.n)
                    vgd(v, v.reciprocal(st3[:, :], st2[:, :]))
                    vgd(v, v.tensor_mul(st2[:, :], st3[:, :], st3[:, :]))
                    vgd(v, v.tensor_mul(st2[:, :], st2[:, :], st1[:, :]))
                    vgd(v, v.tensor_scalar(st2[:, :], st2[:, :], -0.5, 1.5, ALU.mult, ALU.add))
                    vgd(v, v.tensor_mul(rstd_c[:, :], st3[:, :], st2[:, :]))
                    vgd(v, v.tensor_mul(st1[:, :], rstd_c[:, :], rstd_c[:, :]))
                    vgd(v, v.tensor_mul(st1[:, :], st1[:, :], var_c[:, :]))
                    c_vg.inc(v.tensor_scalar(st1[:, :], st1[:, :], EPS32, None, ALU.add))

                @blk.scalar
                def _(s):
                    s = EP(s, c_ss)
                    s.wait_ge(c_vg.sem, c_vg.n)
                    c_sg.inc(s.sqrt(st2[:, :], st1[:, :]))

                @blk.vector
                def _(v):
                    v = EP(v, c_vs)
                    v.wait_ge(c_sg.sem, c_sg.n)
                    vgd(v, v.reciprocal(st3[:, :], st2[:, :]))
                    vgd(v, v.tensor_mul(st2[:, :], st3[:, :], st3[:, :]))
                    vgd(v, v.tensor_mul(st2[:, :], st2[:, :], st1[:, :]))
                    vgd(v, v.tensor_scalar(st2[:, :], st2[:, :], -0.5, 1.5, ALU.mult, ALU.add))
                    vgd(v, v.tensor_mul(st3[:, :], st3[:, :], st2[:, :]))
                    vgd(v, v.tensor_mul(A_c[:, :], rstd_c[:, :], st3[:, :]))
                    vgd(v, v.tensor_scalar(st1[:, :], m_c[:, :], -1.0, None, ALU.mult))
                    vgd(v, v.tensor_mul(nmA_c[:, :], st1[:, :], A_c[:, :]))
                    c_vg.inc(v.tensor_mul(nmrs_c[:, :], st1[:, :], rstd_c[:, :]))

        if DEBUG:
            with nc.Block() as blk:
                @blk.gpsimd
                def _(g):
                    for j, tt in enumerate((sum_c, ssq_c, rstd_c, nmrs_c, A_c, nmA_c)):
                        c_stg.dinc(g.dma_start(dbg_stats[:, j * NT:(j + 1) * NT], tt[:, :]))
                    g.wait_ge(c_stg.sem, c_stg.n)

        with contextlib.ExitStack() as st_ffn:
            act8 = sb("act8", [128, NH * TOK], F8, st_ffn)

            with contextlib.ExitStack() as st_xnq:
                xnqT = sb("xnqT", [128, ND * TOK], F8, st_xnq)

                with contextlib.ExitStack() as st_p2:
                    if PH < 2:
                        return nc
                    xbuf = sb("xbuf2", [128, 2 * D], F32, st_p2)
                    xnf = sb("xnf", [128, D], F32, st_p2)
                    xq8 = sb("xq8", [128, D], F8, st_p2)
                    xqb = sb("xqb", [128, D], BF16, st_p2)
                    with nc.Block() as blk:
                        lx0 = [c.n for c in c_lx]
                        xf0, tin0, tpd0, tev0 = c_xfree.n, c_tin.n, c_tpd.n, c_tev.n
                        with nc.psum_tensor("ptA2", [128, 2048], BF16) as ptA, \
                             nc.psum_tensor("ptB2", [128, 2048], BF16) as ptB:
                            pps = [ptA, ptB]

                            @blk.sync
                            def _(sync):
                                for i in range(NTQ):
                                    if i >= 2:
                                        sync.wait_ge(c_xfree.sem, xf0 + (i - 1))
                                    c_lx[i % 2].dinc(sync.dma_start(xbuf[:, (i % 2) * D:(i % 2 + 1) * D],
                                                                    x_h[i * 128:(i + 1) * 128, :]))

                            @blk.vector
                            def _(v):
                                v = EP(v, c_vs)
                                for i in range(NTQ):
                                    v.wait_ge(c_lx[i % 2].sem, lx0[i % 2] + 16 * (i // 2 + 1))
                                    sl = xbuf[:, (i % 2) * D:(i % 2 + 1) * D]
                                    c_xfree.inc(v.tensor_scalar(xnf[:, :], sl, rstd_c[:, i:i + 1],
                                                                nmrs_c[:, i:i + 1], ALU.mult, ALU.add))
                                    v.wait_ge(c_xfree.sem, c_xfree.n)
                                    vgd(v, v.tensor_scalar(xq8[:, :], xnf[:, :], 1.0, None, ALU.mult))
                                    c_tin.inc(v.tensor_scalar(xqb[:, :], xq8[:, :], 1.0, None, ALU.mult))
                                    v.wait_ge(c_tpd.sem, tpd0 + (i + 1))
                                    pp = pps[i % 2]
                                    ins = None
                                    for d in range(ND):
                                        ins = v.tensor_scalar(xnqT[:, d * TOK + i * 128: d * TOK + (i + 1) * 128],
                                                              pp[:, d * 128:(d + 1) * 128], 1.0, None, ALU.mult)
                                    c_tev.inc(ins)

                            @blk.tensor
                            def _(t):
                                t.wait_ge(c_ps.sem, c_ps.n)
                                for i in range(NTQ):
                                    t.wait_ge(c_tin.sem, tin0 + (i + 1))
                                    if i >= 2:
                                        t.wait_ge(c_tev.sem, tev0 + (i - 1))
                                    pp = pps[i % 2]
                                    ins = None
                                    for d in range(ND):
                                        ins = t.transpose(pp[:, d * 128:(d + 1) * 128],
                                                          xqb[:, d * 128:(d + 1) * 128], ident[:, :])
                                    c_tpd.inc(ins)

                if DEBUG:
                    with nc.Block() as blk:
                        @blk.gpsimd
                        def _(g):
                            g.wait_ge(c_tev.sem, c_tev.n)
                            c_stg.dinc(g.dma_start(dbg_xnqT[:, :], xnqT[:, :]))
                            g.wait_ge(c_stg.sem, c_stg.n)

                with contextlib.ExitStack() as st_g:
                    if PH < 3:
                        return nc
                    wgq0 = sb("wgq0", [128, 2 * 16 * 512], F8, st_g)
                    wgq1 = sb("wgq1", [128, 2 * 16 * 512], F8, st_g)
                    gbuf = sb("gbuf", [128, TOK], BF16, st_g)
                    fbuf = sb("fbuf", [128, TOK], BF16, st_g)
                    tA = sb("tA", [128, TOK], BF16, st_g)
                    tB = sb("tB", [128, TOK], BF16, st_g)
                    tC = sb("tC", [128, TOK], BF16, st_g)
                    with nc.Block() as blk:
                        lw0 = [c.n for c in c_lw]
                        wcv0, mmd0, pev0, sg0, vg0 = (c_wcv.n, c_mmd.n,
                                                      c_pev.n, c_sg.n, c_vg.n)
                        with nc.psum_tensor("pgA", [128, 1024], F32) as pgA, \
                             nc.psum_tensor("pfA", [128, 1024], F32) as pfA, \
                             nc.psum_tensor("pgB", [128, 1024], F32) as pgB, \
                             nc.psum_tensor("pfB", [128, 1024], F32) as pfB:
                            pgs, pfs = [pgA, pgB], [pfA, pfB]

                            @blk.sync
                            def _(sync):
                                for jb in range(NHB):
                                    for gsel in range(2):
                                        for d in range(ND):
                                            k = (jb * 2 + gsel) * ND + d
                                            if k >= 4:
                                                sync.wait_ge(c_wcv.sem, wcv0 + (k - 3))
                                            src = wg_h[gsel, d * 128:(d + 1) * 128,
                                                       jb * 512:(jb + 1) * 512]
                                            c_lw[k % 4].dinc(sync.dma_start(
                                                wstg[:, (k % 4) * 512:(k % 4 + 1) * 512], src))

                            @blk.scalar
                            def _(s):
                                s = EP(s, c_ss)

                                def convs(jb):
                                    for gsel in range(2):
                                        for d in range(ND):
                                            k = (jb * 2 + gsel) * ND + d
                                            s.wait_ge(c_lw[k % 4].sem, lw0[k % 4] + 16 * (k // 4 + 1))
                                            if d == 0 and gsel == 0 and jb >= 2:
                                                s.wait_ge(c_mmd.sem, mmd0 + (jb - 1) * 4)
                                            dstt = wgq0 if gsel == 0 else wgq1
                                            dst = dstt[:, (jb % 2) * 8192 + d * 512:
                                                       (jb % 2) * 8192 + (d + 1) * 512]
                                            c_wcv.inc(s.copy(
                                                dst, wstg[:, (k % 4) * 512:(k % 4 + 1) * 512]))

                                convs(0)
                                convs(1)
                                for jb in range(NHB):
                                    for jj in range(jb * 4, jb * 4 + 4):
                                        s.wait_ge(c_vg.sem, vg0 + jj * 2 + 1)
                                        c_sg.inc(s.activation(tB[:, :], tC[:, :], AF.Tanh))
                                    if jb + 2 < NHB:
                                        convs(jb + 2)

                            @blk.vector
                            def _(v):
                                v = EP(v, c_vs)
                                for jb in range(NHB):
                                    for jj in range(jb * 4, jb * 4 + 4):
                                        v.wait_ge(c_mmd.sem, mmd0 + (jj + 1))
                                        pg, pf = pgs[jj % 2], pfs[jj % 2]
                                        vgd(v, v.tensor_scalar(gbuf[:, :], pg[:, :], 1.0, None, ALU.mult))
                                        c_pev.inc(v.tensor_scalar(fbuf[:, :], pf[:, :], 1.0, None, ALU.mult))
                                        vgd(v, v.tensor_mul(tA[:, :], gbuf[:, :], gbuf[:, :]))
                                        vgd(v, v.tensor_mul(tB[:, :], tA[:, :], gbuf[:, :]))
                                        vgd(v, v.tensor_scalar(tA[:, :], tB[:, :], C3, None, ALU.mult))
                                        vgd(v, v.tensor_add(tB[:, :], gbuf[:, :], tA[:, :]))
                                        c_vg.inc(v.tensor_scalar(tC[:, :], tB[:, :], CS, None, ALU.mult))
                                        v.wait_ge(c_sg.sem, sg0 + jj + 1)
                                        vgd(v, v.tensor_scalar(tC[:, :], tB[:, :], 0.5, 0.5, ALU.mult, ALU.add))
                                        vgd(v, v.tensor_mul(tB[:, :], gbuf[:, :], tC[:, :]))
                                        vgd(v, v.tensor_mul(tA[:, :], tB[:, :], fbuf[:, :]))
                                        c_vg.inc(v.tensor_scalar(act8[:, jj * TOK:(jj + 1) * TOK],
                                                                 tA[:, :], 1.0, None, ALU.mult))

                            @blk.tensor
                            def _(t):
                                t.wait_ge(c_tev.sem, c_tev.n)
                                for jj in range(NH):
                                    jb, sub = jj // 4, jj % 4
                                    t.wait_ge(c_wcv.sem, wcv0 + (jb + 1) * 32)
                                    if jj >= 2:
                                        t.wait_ge(c_pev.sem, pev0 + (jj - 1))
                                    pg, pf = pgs[jj % 2], pfs[jj % 2]
                                    ins = None
                                    for d in range(ND):
                                        off = (jb % 2) * 8192 + d * 512 + sub * 128
                                        l0 = wgq0[:, off:off + 128]
                                        l1 = wgq1[:, off:off + 128]
                                        r = xnqT[:, d * TOK:(d + 1) * TOK]
                                        stt, spp = (d == 0), (d == ND - 1)
                                        t.matmul(pg[:, 0:512], l0, r[:, 0:512], start=stt, stop=spp)
                                        t.matmul(pg[:, 512:1024], l0, r[:, 512:1024], start=stt, stop=spp)
                                        t.matmul(pf[:, 0:512], l1, r[:, 0:512], start=stt, stop=spp)
                                        ins = t.matmul(pf[:, 512:1024], l1, r[:, 512:1024],
                                                       start=stt, stop=spp)
                                    c_mmd.inc(ins)


            with contextlib.ExitStack() as st_d:
                if PH < 4:
                    return nc
                wlstg = sb("wlstg", [128, 2 * 512], F32, st_d)
                wlq = sb("wlq", [128, 2 * NH * 512], F8, st_d)
                fout = sb("fout", [128, 2 * 512], BF16, st_d)
                with nc.Block() as blk:
                    lw0 = [c.n for c in c_lw]
                    wcv0, mmd0, pev0, stg0 = (c_wcv.n, c_mmd.n, c_pev.n, c_stg.n)
                    with nc.psum_tensor("pdA", [128, 512], F32) as pdA, \
                         nc.psum_tensor("pdB", [128, 512], F32) as pdB:
                        pds = [pdA, pdB]

                        @blk.sync
                        def _(sync):
                            for q in range(4):
                                for jj in range(NH):
                                    kq = q * NH + jj
                                    if kq >= 2:
                                        sync.wait_ge(c_wcv.sem, wcv0 + kq - 1)
                                    srcap = wl_h[jj * 128:(jj + 1) * 128,
                                                 q * 512:(q + 1) * 512]
                                    c_lw[kq % 2].dinc(sync.dma_start(
                                        wlstg[:, (kq % 2) * 512:(kq % 2 + 1) * 512], srcap))

                        @blk.scalar
                        def _(s):
                            s = EP(s, c_ss)
                            for q in range(4):
                                for jj in range(NH):
                                    kq = q * NH + jj
                                    s.wait_ge(c_lw[kq % 2].sem, lw0[kq % 2] + 16 * (kq // 2 + 1))
                                    if jj == 0 and q >= 2:
                                        s.wait_ge(c_mmd.sem, mmd0 + (q - 1) * NTQ)
                                    dst = wlq[:, (q % 2) * (NH * 512) + jj * 512:
                                              (q % 2) * (NH * 512) + (jj + 1) * 512]
                                    c_wcv.inc(s.copy(dst,
                                                     wlstg[:, (kq % 2) * 512:(kq % 2 + 1) * 512]))

                        @blk.tensor
                        def _(t):
                            t.wait_ge(c_vg.sem, c_vg.n)
                            for q in range(4):
                                for ti in range(NTQ):
                                    gi = q * NTQ + ti
                                    if gi >= 2:
                                        t.wait_ge(c_pev.sem, pev0 + gi - 1)
                                    pd = pds[gi % 2]
                                    ins = None
                                    for jj in range(NH):
                                        if ti == 0:
                                            t.wait_ge(c_wcv.sem, wcv0 + q * NH + jj + 1)
                                        l = act8[:, jj * TOK + ti * 128: jj * TOK + (ti + 1) * 128]
                                        w = wlq[:, (q % 2) * (NH * 512) + jj * 512:
                                                (q % 2) * (NH * 512) + (jj + 1) * 512]
                                        ins = t.matmul(pd[:, :], l, w,
                                                       start=(jj == 0), stop=(jj == NH - 1))
                                    c_mmd.inc(ins)

                        @blk.vector
                        def _(v):
                            v = EP(v, c_vs)
                            for q in range(4):
                                for ti in range(NTQ):
                                    gi = q * NTQ + ti
                                    v.wait_ge(c_mmd.sem, mmd0 + gi + 1)
                                    if gi >= 2:
                                        v.wait_ge(c_stg.sem, stg0 + 16 * gi)
                                    c_pev.inc(v.tensor_scalar(
                                        fout[:, (gi % 2) * 512:(gi % 2 + 1) * 512],
                                        pds[gi % 2][:, :], 1.0, None, ALU.mult))

                        @blk.gpsimd
                        def _(g):
                            for q in range(4):
                                for ti in range(NTQ):
                                    gi = q * NTQ + ti
                                    g.wait_ge(c_pev.sem, pev0 + gi + 1)
                                    dst = ffn_hbm[ti * 128:(ti + 1) * 128,
                                                  q * 512:(q + 1) * 512]
                                    c_stg.dinc(g.dma_start(
                                        dst, fout[:, (gi % 2) * 512:(gi % 2 + 1) * 512]))
                            g.wait_ge(c_stg.sem, c_stg.n)

        if DEBUG:
            with nc.Block() as blk:
                @blk.gpsimd
                def _(g):
                    c_stg.dinc(g.dma_start(dbg_ffn[:, :], ffn_hbm[:, :]))
                    g.wait_ge(c_stg.sem, c_stg.n)

        with contextlib.ExitStack() as st_p4:
            if PH < 5:
                return nc
            xsT = sb("xsT", [128, ND * T], BF16, st_p4)
            wb = sb("wb", [128, ND * 4 * 512], BF16, st_p4)

            with contextlib.ExitStack() as st_a:
                xbuf = sb("xbuf4", [128, 2 * D], F32, st_a)
                xsb = sb("xsb", [128, D], BF16, st_a)
                with nc.Block() as blk:
                    lx0 = [c.n for c in c_lx]
                    lw0 = [c.n for c in c_lw]
                    xf0, tin0, tpd0, tev0, wcv0 = (
                        c_xfree.n, c_tin.n, c_tpd.n, c_tev.n, c_wcv.n)
                    with nc.psum_tensor("ptA4", [128, 2048], BF16) as ptA, \
                         nc.psum_tensor("ptB4", [128, 2048], BF16) as ptB:
                        pps = [ptA, ptB]

                        @blk.sync
                        def _(sync):
                            for i in range(NT):
                                if i >= 2:
                                    sync.wait_ge(c_xfree.sem, xf0 + (i - 1))
                                c_lx[i % 2].dinc(sync.dma_start(xbuf[:, (i % 2) * D:(i % 2 + 1) * D],
                                                                x_h[i * 128:(i + 1) * 128, :]))
                            for d in range(ND):
                                for cb in range(4):
                                    k = d * 4 + cb
                                    if k >= 4:
                                        sync.wait_ge(c_wcv.sem, wcv0 + (k - 3))
                                    src = wq_h[d * 128:(d + 1) * 128, cb * 512:(cb + 1) * 512]
                                    c_lw[k % 4].dinc(sync.dma_start(
                                        wstg[:, (k % 4) * 512:(k % 4 + 1) * 512], src))

                        @blk.vector
                        def _(v):
                            v = EP(v, c_vs)
                            for i in range(NT):
                                v.wait_ge(c_lx[i % 2].sem, lx0[i % 2] + 16 * (i // 2 + 1))
                                sl = xbuf[:, (i % 2) * D:(i % 2 + 1) * D]
                                c_tin.inc(v.tensor_scalar(xsb[:, :], sl, A_c[:, i:i + 1],
                                                           nmA_c[:, i:i + 1], ALU.mult, ALU.add))
                                c_xfree.inc(v.tensor_scalar(scal1[:1, 2:3], scal1[:1, 3:4], 1.0, None, ALU.mult))
                                v.wait_ge(c_tpd.sem, tpd0 + (i + 1))
                                pp = pps[i % 2]
                                ins = None
                                for d in range(ND):
                                    ins = v.tensor_scalar(xsT[:, d * T + i * 128: d * T + (i + 1) * 128],
                                                          pp[:, d * 128:(d + 1) * 128], 1.0, None, ALU.mult)
                                c_tev.inc(ins)
                            for d in range(ND):
                                for cb in range(4):
                                    k = d * 4 + cb
                                    v.wait_ge(c_lw[k % 4].sem, lw0[k % 4] + 16 * (k // 4 + 1))
                                    dst = wb[:, k * 512:(k + 1) * 512]
                                    c_wcv.inc(v.tensor_scalar(
                                        dst, wstg[:, (k % 4) * 512:(k % 4 + 1) * 512],
                                        gq_c[:, d:d + 1], None, ALU.mult))

                        @blk.tensor
                        def _(t):
                            for i in range(NT):
                                t.wait_ge(c_tin.sem, tin0 + (i + 1))
                                if i >= 2:
                                    t.wait_ge(c_tev.sem, tev0 + (i - 1))
                                pp = pps[i % 2]
                                ins = None
                                for d in range(ND):
                                    ins = t.transpose(pp[:, d * 128:(d + 1) * 128],
                                                      xsb[:, d * 128:(d + 1) * 128], ident[:, :])
                                c_tpd.inc(ins)

            with contextlib.ExitStack() as st_b:
                qf = sb("qf", [128, 2 * D], F32, st_b)
                knstg = sb("knstg", [128, 2 * D], BF16, st_b)
                qbf = sb("qbf", [128, 2 * D], BF16, st_b)
                rsq_t = sb("rsq_t", [128, H], F32, st_b)
                nst1 = sb("nst1", [128, H], F32, st_b)
                nst2 = sb("nst2", [128, H], F32, st_b)

                def proj_pass(n_tiles, out_kind, wcv_base, bias_row, bias_off, out_hbm):
                    with nc.Block() as blk:
                        mmd0, sg0, vg0, tpd0, tev0, pev0, stg0 = (
                            c_mmd.n, c_sg.n, c_vg.n, c_tpd.n, c_tev.n, c_pev.n, c_stg.n)
                        with nc.psum_tensor("pq_" + out_kind, [128, 2048], F32) as pq, \
                             nc.psum_tensor("pt1_" + out_kind, [128, 2048], BF16) as pt1, \
                             nc.psum_tensor("pt2_" + out_kind, [128, 2048], BF16) as pt2:
                            pts = [pt1, pt2]

                            @blk.tensor
                            def _(t):
                                t.wait_ge(c_tev.sem, tev0)

                                def transposes(tj):
                                    t.wait_ge(c_vg.sem, vg0 + 2 * tj + 2)
                                    if tj >= 2:
                                        t.wait_ge(c_tev.sem, tev0 + (tj - 1))
                                    pp = pts[tj % 2]
                                    ins2 = None
                                    for hh in range(H):
                                        ins2 = t.transpose(pp[:, hh * 128:(hh + 1) * 128],
                                                           qbf[:, (tj % 2) * D + hh * 128:
                                                               (tj % 2) * D + (hh + 1) * 128],
                                                           ident[:, :])
                                    c_tpd.inc(ins2)

                                for ti in range(n_tiles):
                                    if ti >= 1:
                                        t.wait_ge(c_pev.sem, pev0 + ti)
                                    for d in range(ND):
                                        if ti == 0:
                                            t.wait_ge(c_wcv.sem, wcv_base + 4 * (d + 1))
                                        l = xsT[:, d * T + ti * 128: d * T + (ti + 1) * 128]
                                        stt = (d == 0)
                                        for cb in range(4):
                                            t.matmul(pq[:, cb * 512:(cb + 1) * 512], l,
                                                     wb[:, (d * 4 + cb) * 512:(d * 4 + cb + 1) * 512],
                                                     start=stt, stop=False)
                                    ins = None
                                    for cb in range(4):
                                        ins = t.matmul(pq[:, cb * 512:(cb + 1) * 512], ones1f[:1, :],
                                                       bias_row[:1, bias_off + cb * 512:
                                                                bias_off + (cb + 1) * 512],
                                                       start=False, stop=True)
                                    c_mmd.inc(ins)
                                    if out_kind != 'v' and ti >= 1:
                                        transposes(ti - 1)
                                if out_kind != 'v':
                                    transposes(n_tiles - 1)

                            if out_kind != 'v':
                                @blk.scalar
                                def _(s):
                                    s = EP(s, c_ss)
                                    for ti in range(n_tiles):
                                        s.wait_ge(c_pev.sem, pev0 + ti + 1)
                                        if ti >= 1:
                                            s.wait_ge(c_vg.sem, vg0 + 2 * (ti - 1) + 1)
                                        c_sg.inc(s.square(wstg[:, :],
                                                          qf[:, (ti % 2) * D:(ti % 2 + 1) * D]))
                                        s.wait_ge(c_vg.sem, vg0 + 2 * ti + 1)
                                        c_sg.inc(s.sqrt(nst2[:, :], nst1[:, :]))

                                @blk.vector
                                def _(v):
                                    v = EP(v, c_vs)
                                    for ti in range(n_tiles):
                                        v.wait_ge(c_mmd.sem, mmd0 + ti + 1)
                                        if ti >= 2:
                                            v.wait_ge(c_sg.sem, sg0 + 2 * (ti - 2) + 1)
                                        qfs = qf[:, (ti % 2) * D:(ti % 2 + 1) * D]
                                        ins = None
                                        for cb in range(4):
                                            ins = v.tensor_scalar(qfs[:, cb * 512:(cb + 1) * 512],
                                                                  pq[:, cb * 512:(cb + 1) * 512],
                                                                  1.0, None, ALU.mult)
                                        c_pev.inc(ins)
                                        v.wait_ge(c_sg.sem, sg0 + 2 * ti + 1)
                                        sq = wstg[:, :]
                                        vgd(v, v.tensor_reduce(rsq_t[:, :],
                                                               sq.rearrange("p (h e) -> p h e", e=128),
                                                               AX.X, ALU.add))
                                        c_vg.inc(v.tensor_scalar(nst1[:, :], rsq_t[:, :], 1.0 / HD,
                                                                 EPS32, ALU.mult, ALU.add))
                                        v.wait_ge(c_sg.sem, sg0 + 2 * ti + 2)
                                        vgd(v, v.reciprocal(rsq_t[:, :], nst2[:, :]))
                                        vgd(v, v.tensor_mul(nst2[:, :], rsq_t[:, :], rsq_t[:, :]))
                                        vgd(v, v.tensor_mul(nst2[:, :], nst2[:, :], nst1[:, :]))
                                        vgd(v, v.tensor_scalar(nst2[:, :], nst2[:, :], -0.5, 1.5,
                                                               ALU.mult, ALU.add))
                                        vgd(v, v.tensor_mul(rsq_t[:, :], rsq_t[:, :], nst2[:, :]))
                                        ins = None
                                        for hh in range(H):
                                            ins = v.tensor_scalar(
                                                qbf[:, (ti % 2) * D + hh * 128:(ti % 2) * D + (hh + 1) * 128],
                                                qfs[:, hh * 128:(hh + 1) * 128],
                                                rsq_t[:, hh:hh + 1], None, ALU.mult)
                                        c_vg.inc(ins)
                                        v.wait_ge(c_tpd.sem, tpd0 + (ti + 1))
                                        v.wait_ge(c_stg.sem, stg0 + 16 * H * ti if ti >= 2 else stg0)
                                        pp = pts[ti % 2]
                                        ins = None
                                        for hh in range(H):
                                            dst = knstg[:, (ti % 2) * D + hh * 128:
                                                        (ti % 2) * D + (hh + 1) * 128]
                                            ins = v.tensor_scalar(dst, pp[:, hh * 128:(hh + 1) * 128],
                                                                  ghs_c[:, 0:1], None, ALU.mult)
                                        c_tev.inc(ins)
                            else:
                                @blk.vector
                                def _(v):
                                    v = EP(v, c_vs)
                                    for ti in range(n_tiles):
                                        v.wait_ge(c_mmd.sem, mmd0 + (ti + 1))
                                        v.wait_ge(c_stg.sem, stg0 + 16 * H * ti if ti >= 2 else stg0)
                                        sl = knstg[:, (ti % 2) * D:(ti % 2 + 1) * D]
                                        _nocp = os.environ.get("KVNOCOPY") == "1"
                                        ins = None
                                        for cb in range(4):
                                            ins = v.tensor_scalar(sl[:, cb * 512:(cb + 1) * 512],
                                                                  pq[:, cb * 512:(cb + 1) * 512],
                                                                  1.0, None, ALU.mult)
                                        c_pev.inc(ins)
                                        c_tev.inc(v.tensor_scalar(scal1[:1, 2:3], scal1[:1, 3:4], 1.0, None, ALU.mult))

                            import os as _os
                            _skipdma = out_kind == 'v' and _os.environ.get("KSKIPV") == "1"

                            @blk.gpsimd
                            def _(g):
                                for ti in range(n_tiles):
                                    g.wait_ge(c_tev.sem, tev0 + (ti + 1))
                                    sl = knstg[:, (ti % 2) * D:(ti % 2 + 1) * D]
                                    if _skipdma:
                                        continue
                                    _kst = os.environ.get("KVMODE", "") == "v2" and out_kind == 'v'
                                    for hh in range(H):
                                        if out_kind == 'v' and not _kst:
                                            dst = out_hbm[hh, ti * 128:(ti + 1) * 128, :]
                                        else:
                                            dst = out_hbm[hh, :, ti * 128:(ti + 1) * 128]
                                        c_stg.dinc(g.dma_start(dst, sl[:, hh * 128:(hh + 1) * 128]))

                if PH < 6:
                    return nc
                proj_pass(NTQ, 'q', c_wcv.n - 64, bq_r, 0, qn_hbm)

                def kv_fold(col_off):
                    with nc.Block() as blk:
                        lw0 = [c.n for c in c_lw]
                        wcv0 = c_wcv.n

                        @blk.sync
                        def _(sync):
                            for d in range(ND):
                                for cb in range(4):
                                    k = d * 4 + cb
                                    if k >= 4:
                                        sync.wait_ge(c_wcv.sem, wcv0 + (k - 3))
                                    src = wkv_h[d * 128:(d + 1) * 128,
                                                col_off + cb * 512:col_off + (cb + 1) * 512]
                                    c_lw[k % 4].dinc(sync.dma_start(
                                        wstg[:, (k % 4) * 512:(k % 4 + 1) * 512], src))

                        @blk.vector
                        def _(v):
                            v = EP(v, c_vs)
                            for d in range(ND):
                                for cb in range(4):
                                    k = d * 4 + cb
                                    v.wait_ge(c_lw[k % 4].sem, lw0[k % 4] + 16 * (k // 4 + 1))
                                    c_wcv.inc(v.tensor_scalar(
                                        wb[:, k * 512:(k + 1) * 512],
                                        wstg[:, (k % 4) * 512:(k % 4 + 1) * 512],
                                        gkv_c[:, d:d + 1], None, ALU.mult))

                if PH < 7:
                    return nc
                kv_fold(0)
                proj_pass(NT, 'k', c_wcv.n - 64, bkv_r, 0, kn_hbm)
                if PH < 8:
                    return nc
                kv_fold(D)
                if PH < 9:
                    return nc
                _kvm = os.environ.get("KVMODE", "")
                _vmode = 'k2' if _kvm == "k" else 'v'
                _vout = v_hbm
                if _kvm == "k":
                    _vout = kn_hbm
                elif _kvm == "v2":
                    _vout = kn_hbm
                proj_pass(int(os.environ.get("KVTILES", str(NT))), _vmode, c_wcv.n - 64, bkv_r, D, _vout)

        if DEBUG:
            with nc.Block() as blk:
                @blk.gpsimd
                def _(g):
                    for hh in range(H):
                        c_stg.dinc(g.dma_start(dbg_qnT[:, hh * TOK:(hh + 1) * TOK], qn_hbm[hh, :, :]))
                        c_stg.dinc(g.dma_start(dbg_knT[:, hh * T:(hh + 1) * T], kn_hbm[hh, :, :]))
                    g.wait_ge(c_stg.sem, c_stg.n)

        with contextlib.ExitStack() as st_wv:
            if PH < 10:
                return nc
            wvT = sb("wvT", [128, H * TOK], BF16, st_wv)
            wob = sb("wob", [128, 4 * H * 512], BF16, st_wv)

            with contextlib.ExitStack() as st_p5:
                expT = sb("expT", [128, NT * TOK], BF16, st_p5)
                knb = sb("knb", [128, 2 * T], BF16, st_p5)
                vhb = sb("vhb", [128, 2 * T], BF16, st_p5)
                qnb = sb("qnb", [128, 2 * TOK], BF16, st_p5)
                zrow = sb("zrow", [1, TOK], F32, st_p5)
                zrow2 = sb("zrow2", [1, TOK], F32, st_p5)
                zb = sb("zb", [128, TOK], F32, st_p5)
                wvf = sb("wvf", [128, TOK], F32, st_p5)
                with nc.Block() as blk:
                    lg0 = [c.n for c in c_lg]
                    lw0 = [c.n for c in c_lw]
                    mmd0, zw0, sg0, zr0, bc0, wvn0, pev0, wcv0 = (
                        c_mmd.n, c_zw.n, c_sg.n, c_zr.n, c_bc.n, c_wvn.n, c_pev.n, c_wcv.n)
                    with nc.psum_tensor("psA5", [128, 1024], F32) as psA, \
                         nc.psum_tensor("psB5", [128, 1024], F32) as psB, \
                         nc.psum_tensor("pwv", [128, 1024], F32) as pwv, \
                         nc.psum_tensor("pzA", [1, 512], F32) as pzA, \
                         nc.psum_tensor("pzB", [1, 512], F32) as pzB:
                        pss = [psA, psB]

                        @blk.sync
                        def _(sync):
                            for cb in range(4):
                                for hh in range(H):
                                    k = cb * H + hh
                                    if k >= 4:
                                        sync.wait_ge(c_wcv.sem, wcv0 + (k - 3))
                                    srcw = wo_h[hh * 128:(hh + 1) * 128, cb * 512:(cb + 1) * 512]
                                    c_lw[k % 4].dinc(sync.dma_start(
                                        wstg[:, (k % 4) * 512:(k % 4 + 1) * 512], srcw))

                        @blk.gpsimd
                        def _(g):
                            g.wait_ge(c_stg.sem, c_stg.n)
                            for h in range(H):
                                if h >= 2:
                                    g.wait_ge(c_wvn.sem, wvn0 + (h - 1))
                                c_lg[h % 2].dinc(g.dma_start(knb[:, (h % 2) * T:(h % 2 + 1) * T],
                                                             kn_hbm[h, :, :]))
                                c_lg[h % 2].dinc(g.dma_start(qnb[:, (h % 2) * TOK:(h % 2 + 1) * TOK],
                                                             qn_hbm[h, :, :]))
                                c_lg[h % 2].dinc(g.dma_start(
                                    vhb[:, (h % 2) * T:(h % 2 + 1) * T].rearrange(
                                        "p (c v) -> p c v", v=128),
                                    v_hbm[h].rearrange("(c p) v -> p c v", p=128)))

                        def sc_wait(t, gi):
                            if gi >= 2:
                                t.wait_ge(c_sg.sem, sg0 + gi - 1)

                        @blk.tensor
                        def _(t):
                            for h in range(H):
                                t.wait_ge(c_lg[h % 2].sem, lg0[h % 2] + 48 * (h // 2 + 1))
                                q0 = qnb[:, (h % 2) * TOK:(h % 2) * TOK + 512]
                                q1 = qnb[:, (h % 2) * TOK + 512:(h % 2 + 1) * TOK]

                                def scores(i):
                                    gi = h * NT + i
                                    sc_wait(t, gi)
                                    if i == 1 and h >= 1:
                                        t.wait_ge(c_wvn.sem, wvn0 + h)
                                    l2 = knb[:, (h % 2) * T + i * 128:
                                             (h % 2) * T + (i + 1) * 128]
                                    ps2 = pss[gi % 2]
                                    t.matmul(ps2[:, 0:512], l2, q0, start=True, stop=True)
                                    c_mmd.inc(t.matmul(ps2[:, 512:1024], l2, q1,
                                                       start=True, stop=True))

                                scores(0)
                                scores(1)
                                if h >= 1:
                                    t.wait_ge(c_pev.sem, pev0 + h)
                                for i in range(NT):
                                    gi = h * NT + i
                                    if i + 2 < NT:
                                        scores(i + 2)
                                    t.wait_ge(c_sg.sem, sg0 + gi + 1)
                                    e = expT[:, i * TOK:(i + 1) * TOK]
                                    stt, spp = (i == 0), (i == NT - 1)
                                    t.matmul(pzA[:1, :], onescol_b[:, 0:1], e[:, 0:512],
                                             start=stt, stop=spp)
                                    t.matmul(pzB[:1, :], onescol_b[:, 0:1], e[:, 512:1024],
                                             start=stt, stop=spp)
                                    vl = vhb[:, (h % 2) * T + i * 128:(h % 2) * T + (i + 1) * 128]
                                    t.matmul(pwv[:, 0:512], vl, e[:, 0:512], start=stt, stop=spp)
                                    c_zw.inc(t.matmul(pwv[:, 512:1024], vl, e[:, 512:1024],
                                                      start=stt, stop=spp))
                                t.wait_ge(c_zr.sem, zr0 + (h + 1))
                                t.matmul(pss[1][:, 0:512], ones1f[:1, :], zrow[:1, 0:512],
                                         start=True, stop=True)
                                c_bc.inc(t.matmul(pss[1][:, 512:1024], ones1f[:1, :],
                                                  zrow[:1, 512:1024], start=True, stop=True))

                        @blk.scalar
                        def _(s):
                            s = EP(s, c_ss)
                            s.wait_ge(c_vg.sem, c_vg.n)
                            for h in range(H):
                                for kk in range(4 * h, 4 * h + 4):
                                    s.wait_ge(c_lw[kk % 4].sem, lw0[kk % 4] + 16 * (kk // 4 + 1))
                                    c_wcv.inc(s.copy(wob[:, kk * 512:(kk + 1) * 512],
                                                     wstg[:, (kk % 4) * 512:(kk % 4 + 1) * 512]))
                                for i in range(NT):
                                    gi = h * NT + i
                                    s.wait_ge(c_mmd.sem, mmd0 + gi + 1)
                                    ps = pss[gi % 2]
                                    e = expT[:, i * TOK:(i + 1) * TOK]
                                    c_sg.inc(s.activation(e[:, :], ps[:, :], AF.Exp,
                                                          bias=negM_c[:, 0:1], scale=1.0))

                        @blk.vector
                        def _(v):
                            v = EP(v, c_vs)
                            for h in range(H):
                                v.wait_ge(c_zw.sem, zw0 + (h + 1) * NT)
                                wvs = wvf[:, :]
                                v.tensor_scalar(wvs[:, 0:512], pwv[:, 0:512], 1.0, None, ALU.mult)
                                v.tensor_scalar(wvs[:, 512:1024], pwv[:, 512:1024], 1.0, None, ALU.mult)
                                v.tensor_scalar(zrow2[:1, 0:512], pzA[:1, :], 1.0, None, ALU.mult)
                                c_pev.inc(v.tensor_scalar(zrow2[:1, 512:1024], pzB[:1, :],
                                                          1.0, None, ALU.mult))
                                v.wait_ge(c_vs.sem, c_vs.n)
                                c_zr.inc(v.reciprocal(zrow[:1, :], zrow2[:1, :]))
                                v.wait_ge(c_bc.sem, bc0 + (h + 1))
                                vgd(v, v.tensor_scalar(zb[:, :], pss[1][:, :], 1.0, None, ALU.mult))
                                c_wvn.inc(v.tensor_mul(wvT[:, h * TOK:(h + 1) * TOK],
                                                       wvs[:, :], zb[:, :]))

            if DEBUG:
                with nc.Block() as blk:
                    @blk.gpsimd
                    def _(g):
                        c_stg.dinc(g.dma_start(dbg_wvT[:, :], wvT[:, :]))
                        g.wait_ge(c_stg.sem, c_stg.n)

            with contextlib.ExitStack() as st_p6:
                if PH < 11:
                    return nc
                xrb = sb("xrb", [128, 2 * D], F32, st_p6)
                frb = sb("frb", [128, 2 * D], BF16, st_p6)
                yout = sb("yout", [128, 2 * D], F32, st_p6)
                with nc.Block() as blk:
                    lx0 = [c.n for c in c_lx]
                    lw0 = [c.n for c in c_lw]
                    lg0 = [c.n for c in c_lg]
                    wcv0, mmd0, pev0, stg0, xf0, vg0 = (
                        c_wcv.n, c_mmd.n, c_pev.n,
                        c_stg.n, c_xfree.n, c_vg.n)
                    with nc.psum_tensor("poA", [128, 512], F32) as poA, \
                         nc.psum_tensor("poB", [128, 512], F32) as poB:
                        pos = [poA, poB]

                        @blk.sync
                        def _(sync):
                            for ti in range(NTQ):
                                if ti >= 2:
                                    sync.wait_ge(c_xfree.sem, xf0 + (ti - 1))
                                c_lx[ti % 2].dinc(sync.dma_start(xrb[:, (ti % 2) * D:(ti % 2 + 1) * D],
                                                                 x_h[ti * 128:(ti + 1) * 128, :]))

                        @blk.gpsimd
                        def _(g):
                            for ti in range(NTQ + 2):
                                if 0 <= ti - 2 < NTQ:
                                    tj = ti - 2
                                    g.wait_ge(c_pev.sem, pev0 + (tj + 1) * 4)
                                    c_stg.dinc(g.dma_start(y_h[tj * 128:(tj + 1) * 128, :],
                                                           yout[:, (tj % 2) * D:(tj % 2 + 1) * D]))
                                if ti < NTQ:
                                    if ti >= 2:
                                        g.wait_ge(c_vg.sem, vg0 + (ti - 1))
                                    c_lg[ti % 2].dinc(g.dma_start(frb[:, (ti % 2) * D:(ti % 2 + 1) * D],
                                                                  ffn_hbm[ti * 128:(ti + 1) * 128, :]))
                            g.wait_ge(c_stg.sem, c_stg.n)

                        @blk.tensor
                        def _(t):
                            t.wait_ge(c_wvn.sem, c_wvn.n)
                            t.wait_ge(c_wcv.sem, wcv0)
                            for ti in range(NTQ):
                                for cb in range(4):
                                    gi = ti * 4 + cb
                                    if gi >= 2:
                                        t.wait_ge(c_pev.sem, pev0 + (gi - 1))
                                    po = pos[gi % 2]
                                    for hh in range(H):
                                        l = wvT[:, hh * TOK + ti * 128: hh * TOK + (ti + 1) * 128]
                                        t.matmul(po[:, :], l,
                                                 wob[:, (cb * H + hh) * 512:(cb * H + hh + 1) * 512],
                                                 start=(hh == 0), stop=False)
                                    c_mmd.inc(t.matmul(po[:, :], ones1f[:1, :],
                                                       bo_r[:1, cb * 512:(cb + 1) * 512],
                                                       start=False, stop=True))

                        @blk.vector
                        def _(v):
                            v = EP(v, c_vs)
                            for ti in range(NTQ):
                                v.wait_ge(c_lx[ti % 2].sem, lx0[ti % 2] + 16 * (ti // 2 + 1))
                                v.wait_ge(c_lg[ti % 2].sem, lg0[ti % 2] + 16 * (ti // 2 + 1))
                                if ti >= 2:
                                    v.wait_ge(c_stg.sem, stg0 + 16 * ti)
                                for cb in range(4):
                                    gi = ti * 4 + cb
                                    v.wait_ge(c_mmd.sem, mmd0 + (gi + 1))
                                    po = pos[gi % 2]
                                    lo = (ti % 2) * D + cb * 512
                                    yl = yout[:, lo:lo + 512]
                                    vgd(v, v.tensor_add(yl, po[:, :], xrb[:, lo:lo + 512]))
                                    c_pev.inc(v.tensor_add(yl, yl, frb[:, lo:lo + 512]))
                                c_xfree.inc(v.tensor_scalar(scal1[:1, 2:3], scal1[:1, 3:4], 1.0, None, ALU.mult))
                                c_vg.inc(v.tensor_scalar(scal1[:1, 4:5], scal1[:1, 5:6], 1.0, None, ALU.mult))

    return nc


_NC_CACHE = {}


def _get_nc():
    key = ("nc", DEBUG)
    if key not in _NC_CACHE:
        _NC_CACHE[key] = build_nc()
    return _NC_CACHE[key]


def kernel(L0_attn_x, L0_attn_g_q, L0_attn_Wq, L0_attn_bq, L0_attn_g_kv,
           L0_attn_Wkv, L0_attn_bkv, L0_attn_g_head, L0_attn_Wo, L0_attn_bo,
           L0_ffn_w_gating, L0_ffn_w_linear):
    x = np.ascontiguousarray(np.asarray(L0_attn_x, dtype=np.float32))
    common = {
        "g_q": np.ascontiguousarray(L0_attn_g_q, dtype=np.float32),
        "Wq": np.ascontiguousarray(L0_attn_Wq, dtype=np.float32),
        "bq": np.ascontiguousarray(L0_attn_bq, dtype=np.float32),
        "g_kv": np.ascontiguousarray(L0_attn_g_kv, dtype=np.float32),
        "Wkv": np.ascontiguousarray(L0_attn_Wkv, dtype=np.float32),
        "bkv": np.ascontiguousarray(L0_attn_bkv, dtype=np.float32),
        "g_head": np.ascontiguousarray(L0_attn_g_head, dtype=np.float32),
        "Wo": np.ascontiguousarray(L0_attn_Wo, dtype=np.float32),
        "bo": np.ascontiguousarray(L0_attn_bo, dtype=np.float32),
        "w_gating": np.ascontiguousarray(L0_ffn_w_gating, dtype=np.float32),
        "w_linear": np.ascontiguousarray(L0_ffn_w_linear, dtype=np.float32),
    }
    in_maps = []
    for c in range(8):
        b, h = c // 2, c % 2
        xb = x[b]
        xc = np.concatenate([xb[h * TOK:(h + 1) * TOK], xb[(1 - h) * TOK:(2 - h) * TOK]], axis=0)
        m = dict(common)
        m["x"] = np.ascontiguousarray(xc)
        in_maps.append(m)
    nc = _get_nc()
    res = run_bass_kernel_spmd(nc, in_maps, core_ids=list(range(8)))
    out = np.empty((B, T, D), dtype=np.float32)
    for c in range(8):
        b, h = c // 2, c % 2
        out[b, h * TOK:(h + 1) * TOK] = res.results[c]["y"]
    return out
